# version 17
# baseline (speedup 1.0000x reference)
import numpy as np
import ml_dtypes
import concourse.bass as bass
import concourse.mybir as mybir
from concourse.bass_utils import run_bass_kernel_spmd

F32 = mybir.dt.float32
BF16 = mybir.dt.bfloat16
AF = mybir.ActivationFunctionType
ALU = mybir.AluOpType
AX = mybir.AxisListType

ENGINES = ("tensor", "vector", "scalar", "gpsimd", "sync")
N_DMA_SEMS = 4



class Op:
    __slots__ = ("engine", "fn", "deps", "idx", "is_dma", "dma_sem_i",
                 "dma_val", "needs_inc", "inc_val", "tag", "dma_prewait")

    def __init__(self, engine, fn, deps, idx, is_dma, tag=""):
        self.engine = engine
        self.fn = fn
        self.deps = deps
        self.idx = idx
        self.is_dma = is_dma
        self.dma_sem_i = -1
        self.dma_val = 0
        self.needs_inc = False
        self.inc_val = 0
        self.tag = tag
        self.dma_prewait = 0


class Buf:

    __slots__ = ("name", "last_w", "readers")

    def __init__(self, name=""):
        self.name = name
        self.last_w = None
        self.readers = []


class TBuf:

    def __init__(self, name, n_rows, gran):
        self.gran = gran
        self.n = (n_rows + gran - 1) // gran
        self.bufs = [Buf(f"{name}[{i}]") for i in range(self.n)]

    def r(self, r0=None, r1=None):
        if r0 is None:
            return self.bufs
        return self.bufs[r0 // self.gran:(r1 + self.gran - 1) // self.gran]

    w = r


class Sched:
    def __init__(self, nc, drains=False):
        self.nc = nc
        self.drains = drains
        self.ops = []
        self.per_engine = {e: [] for e in ENGINES}
        self.dma_counts = {e: [0] * N_DMA_SEMS for e in ENGINES}
        self.dma_rr = {e: 0 for e in ENGINES}
        self.fence_deps = []
        self.last_dma = {e: [None] * N_DMA_SEMS for e in ENGINES}

    def fence(self):
        deps = []
        for e in ENGINES:
            ops = self.per_engine[e]
            if ops and ops[-1].tag != "fence":
                deps.append(self.op(e, lambda eng: eng.drain(), tag="fence"))
            elif ops:
                deps.append(ops[-1])
            for o in self.last_dma[e]:
                if o is not None:
                    deps.append(o)
        self.fence_deps = deps

    def op(self, engine, fn, reads=(), writes=(), tag=""):
        return self._rec(engine, fn, reads, writes, False, tag)

    def dma(self, engine, fn, reads=(), writes=(), tag=""):
        return self._rec(engine, fn, reads, writes, True, tag)

    def _rec(self, engine, fn, reads, writes, is_dma, tag):
        deps = []
        seen = set()

        def add(d):
            if d is not None and id(d) not in seen:
                seen.add(id(d))
                deps.append(d)

        def flat(bs):
            out = []
            for b in bs:
                if isinstance(b, (list, tuple)):
                    out.extend(b)
                else:
                    out.append(b)
            return out

        reads = flat(reads)
        writes = flat(writes)
        for b in reads:
            add(b.last_w)
        for b in writes:
            add(b.last_w)
            for rd in b.readers:
                add(rd)
        for d in self.fence_deps:
            add(d)
        o = Op(engine, fn, deps, len(self.per_engine[engine]), is_dma, tag)
        if is_dma:
            i = self.dma_rr[engine] % N_DMA_SEMS
            self.dma_rr[engine] += 1
            self.dma_counts[engine][i] += 1
            o.dma_sem_i = i
            o.dma_val = 16 * self.dma_counts[engine][i]
            o.dma_prewait = o.dma_val - 16 if o.dma_val > 16 else 0
            self.last_dma[engine][i] = o
        self.ops.append(o)
        self.per_engine[engine].append(o)
        for b in reads:
            b.readers.append(o)
        for b in writes:
            b.last_w = o
            b.readers = []
        return o

    def finalize(self):
        nc = self.nc
        for o in self.ops:
            for d in o.deps:
                if not d.is_dma:
                    d.needs_inc = True
        eng_sems = {}
        dma_sems = {}
        for e in ENGINES:
            cnt = 0
            for o in self.per_engine[e]:
                if o.needs_inc and not o.is_dma:
                    cnt += 1
                    o.inc_val = cnt
            if cnt > 0:
                eng_sems[e] = nc.alloc_semaphore(name=f"esem_{e}")
            if self.dma_rr[e] > 0:
                dma_sems[e] = [nc.alloc_semaphore(name=f"dsem_{e}{i}")
                               for i in range(N_DMA_SEMS)]
        self.eng_sems = eng_sems
        self.dma_sems = dma_sems

        def engine_body(e):
            def body(eng):
                watermark = {}
                drained_upto = -1
                for o in self.per_engine[e]:
                    waits = {}
                    for d in o.deps:
                        if d.is_dma:
                            sem = dma_sems[d.engine][d.dma_sem_i]
                            val = d.dma_val
                        else:
                            sem = eng_sems[d.engine]
                            val = d.inc_val
                        k = (sem.name, sem.num)
                        if val > waits.get(k, (0, None))[0]:
                            waits[k] = (val, sem)
                    if o.is_dma and o.dma_prewait:
                        sem = dma_sems[e][o.dma_sem_i]
                        k = (sem.name, sem.num)
                        if o.dma_prewait > waits.get(k, (0, None))[0]:
                            waits[k] = (o.dma_prewait, sem)
                    for k, (val, sem) in waits.items():
                        if watermark.get(k, 0) >= val:
                            continue
                        eng.wait_ge(sem, val)
                        watermark[k] = val
                    ins = o.fn(eng)
                    if isinstance(ins, (list, tuple)):
                        ins = ins[-1]
                    if o.is_dma:
                        ins.then_inc(dma_sems[e][o.dma_sem_i], 16)
                    elif o.needs_inc:
                        ins.then_inc(eng_sems[e], 1)
                for i, cnt in enumerate(self.dma_counts[e]):
                    if cnt > 0:
                        eng.wait_ge(dma_sems[e][i], 16 * cnt)
            return body

        with nc.Block() as block:
            for e in ENGINES:
                if self.per_engine[e]:
                    getattr(block, e)(engine_body(e))

    def n_ops(self):
        return {e: len(v) for e, v in self.per_engine.items() if v}


class Arena:

    BASE = 32768

    def __init__(self, nc, sched=None):
        self.nc = nc
        self.sched = sched
        self.base = self.BASE
        self.top = nc.SBUF_PARTITION_SIZE_BYTES
        self.marks = []
        self.peak = self.BASE
        self.cnt = 0

    def alloc(self, name, shape, dtype, align=64):
        nbytes = int(np.prod(shape[1:])) * mybir.dt.size(dtype)
        off = (self.base + align - 1) // align * align
        self.cnt += 1
        t = self.nc.alloc_sbuf_tensor_at(f"{name}_{self.cnt}", list(shape),
                                         dtype, offset=off)
        self.base = off + nbytes
        self.peak = max(self.peak, self.base)
        assert self.base <= self.top, (
            f"SBUF arena overflow at {name}: {self.base} > {self.top}")
        return t

    def overlay(self, name, shape, dtype, at):
        nbytes = int(np.prod(shape[1:])) * mybir.dt.size(dtype)
        assert at + nbytes <= self.top
        self.cnt += 1
        return self.nc.alloc_sbuf_tensor_at(f"{name}_{self.cnt}", list(shape),
                                            dtype, offset=at)

    def mark(self):
        self.marks.append(self.base)

    def release(self):
        self.base = self.marks.pop()
        if self.sched is not None:
            self.sched.fence()



D = 1024
NH_A = 16
HD = 64
DL = 256
LAM = 0.01 * 0.01
QSC_A = 1.0 / (np.sqrt(1024.0) + 1e-8)
NH_F = 8
INNER = 512
WIN = 512
HID = 2730
QK_SCALE = 8.0
NEG = -1e9
EPS_LN = 1e-5

PB = 128
ZR = 16


def _xpos_tables(S):
    hi = np.arange(0, HD, 2, dtype=np.float64)
    inv = 1.0 / 10000.0 ** (hi / HD)
    t = np.arange(S, dtype=np.float64)
    fr = t[:, None] * inv[None, :]
    fr = np.concatenate([fr, fr], axis=-1)
    base = (hi + 0.4 * HD) / (1.4 * HD)
    power = (t - S // 2) / float(WIN // 2)
    sc = base[None, :] ** power[:, None]
    sc = np.concatenate([sc, sc], axis=-1)
    cos, sin = np.cos(fr), np.sin(fr)
    sign = np.ones(HD)
    sign[:HD // 2] = -1.0
    cs_q = QK_SCALE * cos * sc
    sn_q = QK_SCALE * sin * sc * sign[None, :]
    cs_k = cos / sc
    sn_k = sin / sc * sign[None, :]
    return [a.astype(np.float32) for a in (cs_q, sn_q, cs_k, sn_k)]


def build_consts(S):
    t = np.arange(S, dtype=np.float64)
    ebf = {}
    ebf["enq"] = np.exp(-LAM * t) * QSC_A
    ebf["epq"] = np.exp(+LAM * t) * QSC_A
    ebf["enk"] = np.exp(-LAM * t)
    ebf["epk"] = np.exp(+LAM * t)
    cbf = np.concatenate(
        [np.tile(v.astype(ml_dtypes.bfloat16)[None, :], (PB, 1))
         for v in (ebf["enq"], ebf["epq"], ebf["enk"], ebf["epk"])],
        axis=1)

    kk = np.arange(128)[:, None]
    qq5 = np.arange(512)[None, :]
    corr = []
    for r in range(4):
        d = qq5 - kk - 128 * r
        corr.append(np.where(d >= 0, 1.0, np.exp(2 * LAM * d)))
    corr = np.concatenate(corr, axis=1).astype(np.float32)

    q1 = np.arange(128)[None, :]
    mask_lead = np.where(q1.T > q1, NEG, 0.0)
    mask_trail = np.where(q1.T < q1, NEG, 0.0)
    maskT_old = np.where(q1 > q1.T, NEG, 0.0)
    maskT_fut = np.where(q1 < q1.T, NEG, 0.0)
    ident = np.eye(128)
    c32 = np.concatenate(
        [corr, mask_lead, mask_trail, maskT_old, maskT_fut, ident],
        axis=1).astype(np.float32)

    mb = np.concatenate([mask_lead, mask_trail, maskT_old, maskT_fut, ident],
                        axis=1).astype(ml_dtypes.bfloat16)
    cbf = np.concatenate([cbf, mb], axis=1)

    rot = np.stack(_xpos_tables(S), axis=0)
    return cbf, c32, rot


C_CORR = 0
C_MLEAD = 2048
C_MTRAIL = 2048 + 128
C_MTOLD = 2048 + 256
C_MTFUT = 2048 + 384
C_IDENT = 2048 + 512
C32_COLS = 2048 + 640


class Ctx:
    pass


def _ts(i, n):
    return slice(i * n, (i + 1) * n)



def load_w_fm(cx, src_ap, K, N, name, out_dtype=BF16):
    s, ar = cx.s, cx.ar
    nch = K // 128
    t = ar.alloc(name, [PB, nch * N], out_dtype)
    b = Buf(name)
    re = src_ap.rearrange("(c p) n -> p c n", p=128)
    s.dma("gpsimd", lambda e, t=t, re=re, nch=nch, N=N: e.dma_start(
        t[:, :].rearrange("p (c n) -> p c n", c=nch), re),
        writes=[b], tag=f"ldw_{name}")
    return t, b


def mm_fm_tm(cx, psum_ap, w_t, w_cols, act_t, act_cols, nch, N,
             start=True, stop=True):
    def fn(e):
        ins = None
        for c in range(nch):
            ins = e.matmul(psum_ap,
                           act_t[:, act_cols(c)],
                           w_t[:, w_cols(c)][:, 0:N] if callable(w_cols)
                           else w_t[:, w_cols][:, 0:N],
                           start=(start and c == 0),
                           stop=(stop and c == nch - 1))
        return ins
    return fn


def emit_ln(cx, xt, bx, cols, out_bf, b_obf, out_f32=None, b_of32=None,
            gamma=None, beta=None, b_gb=()):
    s = cx.s
    n = cols.stop - cols.start
    pr = 0 if cols.start == 0 else 1
    st = cx.ln_scr
    bs = cx.b_lnp[pr]
    sq = cx.fsc
    bsq = cx.b_fscp[pr]
    c = pr * 8
    sqc = slice(pr * n, (pr + 1) * n)
    s.op("vector", lambda e: e.reduce_sum(st[:, c:c + 1], xt[:, cols],
                                          axis=AX.X),
         reads=[bx], writes=[bs])
    s.op("scalar", lambda e: e.activation(sq[:, sqc], xt[:, cols], AF.Square),
         reads=[bx], writes=[bsq])
    s.op("vector", lambda e: e.reduce_sum(st[:, c + 1:c + 2], sq[:, sqc],
                                          axis=AX.X),
         reads=[bsq], writes=[bs])
    s.op("vector", lambda e: e.tensor_scalar_mul(st[:, c + 2:c + 3],
                                                 st[:, c:c + 1], -1.0 / n),
         reads=[bs], writes=[bs])
    s.op("vector", lambda e: e.tensor_mul(st[:, c + 3:c + 4],
                                          st[:, c + 2:c + 3],
                                          st[:, c + 2:c + 3]),
         reads=[bs], writes=[bs])
    s.op("vector", lambda e: e.scalar_tensor_tensor(
        st[:, c + 4:c + 5], st[:, c + 1:c + 2], 1.0 / n,
        st[:, c + 3:c + 4], op0=ALU.mult, op1=ALU.subtract),
        reads=[bs], writes=[bs])
    s.op("scalar", lambda e: e.activation(st[:, c + 5:c + 6],
                                          st[:, c + 4:c + 5], AF.Sqrt,
                                          bias=cx.kc[:, 0:1]),
         reads=[bs, cx.b_kc], writes=[bs])
    s.op("vector", lambda e: e.reciprocal(st[:, c + 6:c + 7],
                                          st[:, c + 5:c + 6]),
         reads=[bs], writes=[bs])
    if gamma is None:
        if out_f32 is not None:
            s.op("vector", lambda e: e.tensor_scalar(
                out_f32, xt[:, cols], st[:, c + 2:c + 3], st[:, c + 6:c + 7],
                op0=ALU.add, op1=ALU.mult),
                reads=[bx, bs], writes=[b_of32])
            s.op("scalar", lambda e: e.activation(out_bf, out_f32, AF.Copy),
                 reads=[b_of32], writes=[b_obf])
        else:
            s.op("vector", lambda e: e.tensor_scalar(
                out_bf, xt[:, cols], st[:, c + 2:c + 3], st[:, c + 6:c + 7],
                op0=ALU.add, op1=ALU.mult),
                reads=[bx, bs], writes=[b_obf])
    else:
        s.op("vector", lambda e: e.tensor_scalar(
            sq[:, sqc], xt[:, cols], st[:, c + 2:c + 3], st[:, c + 6:c + 7],
            op0=ALU.add, op1=ALU.mult),
            reads=[bx, bs], writes=[bsq])
        s.op("vector", lambda e: e.tensor_mul(sq[:, sqc], sq[:, sqc], gamma),
             reads=[bsq] + list(b_gb), writes=[bsq])
        s.op("vector", lambda e: e.tensor_add(out_bf, sq[:, sqc], beta),
             reads=[bsq] + list(b_gb), writes=[b_obf])


def emit_ln_pass(cx, src_ap, src_tb, dst_bf_ap, dst_bf_tb, dst_f32_ap=None,
                 dst_f32_tb=None, tag="ln"):
    s, ar, S = cx.s, cx.ar, cx.S
    NT = S // 128
    ar.mark()
    xt = ar.alloc("lnx", [PB, 2 * D], F32)
    ob = ar.alloc("lnob", [PB, 2 * D], BF16)
    of = ar.alloc("lnof", [PB, 2 * D], F32) if dst_f32_ap is not None else None
    bx = [Buf(), Buf()]
    bo = [Buf(), Buf()]
    bf32 = [Buf(), Buf()]
    for t in range(NT):
        p = t % 2
        cols = slice(p * D, (p + 1) * D)
        s.dma("sync", lambda e, t=t, cols=cols: e.dma_start(
            xt[:, cols], src_ap[_ts(t, 128), :]),
            reads=[src_tb.r(t * 128, t * 128 + 128)] if src_tb else [],
            writes=[bx[p]], tag=tag + "ld")
        emit_ln(cx, xt, bx[p], cols,
                ob[:, cols], bo[p],
                (of[:, cols] if of is not None else None),
                (bf32[p] if of is not None else None))
        s.dma("sync", lambda e, t=t, cols=cols: e.dma_start(
            dst_bf_ap[ZR + t * 128:ZR + (t + 1) * 128, :], ob[:, cols]),
            reads=[bo[p]], writes=sp_w(dst_bf_tb, t),
            tag=tag + "st")
        if of is not None:
            s.dma("sync", lambda e, t=t, cols=cols: e.dma_start(
                dst_f32_ap[ZR + t * 128:ZR + (t + 1) * 128, :], of[:, cols]),
                reads=[bf32[p]],
                writes=sp_w(dst_f32_tb, t),
                tag=tag + "stf")
    ar.release()


def emit_zero_rows(cx):
    s, ar = cx.s, cx.ar
    ar.mark()
    zb = ar.alloc("zrow_b", [ZR, D], BF16)
    zf = ar.alloc("zrow_f", [ZR, D], F32)
    b1, b2 = Buf(), Buf()
    s.op("vector", lambda e: e.memset(zb[:, :], 0.0), writes=[b1])
    s.op("vector", lambda e: e.memset(zf[:, :], 0.0), writes=[b2])
    for name in ("xbf_a", "xbf_b"):
        d = getattr(cx, name)
        s.dma("sync", lambda e, d=d: e.dma_start(d.ap()[0:ZR, :], zb[:, :]),
              reads=[b1], writes=getattr(cx, "tb_" + name).r(0, 128))
    for name in ("xf_a", "xf_b"):
        d = getattr(cx, name)
        s.dma("sync", lambda e, d=d: e.dma_start(d.ap()[0:ZR, :], zf[:, :]),
              reads=[b2], writes=getattr(cx, "tb_" + name).r(0, 128))
    ar.release()


def load_fm_from_spill(cx, spill_ap, rows, col0, ncols, name, reads,
                       shift=0, eng="sync", ar=None):
    s = cx.s
    ar = ar or cx.ar
    nch = ncols // 128
    t = ar.alloc(name, [PB, nch * rows], BF16)
    bufs = [Buf(f"{name}{c}") for c in range(nch)]
    for c in range(nch):
        src = spill_ap[ZR - shift:ZR - shift + rows,
                       col0 + c * 128:col0 + (c + 1) * 128]
        s.dma(eng, lambda e, src=src, c=c: e.dma_start(
            t[:, c * rows:(c + 1) * rows], src, transpose=True),
            reads=reads, writes=[bufs[c]], tag=f"T_{name}")
    return t, bufs


class TBuf2:

    def __init__(self, name, nr, nc):
        self.bufs = [[Buf(f"{name}[{i},{j}]") for j in range(nc)]
                     for i in range(nr)]

    def b(self, i, j):
        return self.bufs[i][j]

    def row(self, i):
        return self.bufs[i]

    def all(self):
        return [b for r in self.bufs for b in r]


def bank(cx, i):
    return cx.PW[i // 2][:, (i % 2) * 512:(i % 2 + 1) * 512]


def bbuf(cx, i):
    return cx.b_pw[i]


def emit_cs_stat(cx, psum_ap, b_ps, acc, b_acc, first):
    s = cx.s
    sq = cx.sq_st
    bsq = cx.b_sq_st
    stb = bank(cx, 7)
    bst = bbuf(cx, 7)
    s.op("scalar", lambda e: e.activation(sq[:, :], psum_ap, AF.Square),
         reads=[b_ps], writes=[bsq])
    s.op("tensor", lambda e: e.matmul(stb[0:1, 0:512], cx.ones128[:, 0:1],
                                      sq[:, :], start=True, stop=True),
         reads=[bsq, cx.b_ones128], writes=[bst])
    s.op("vector", lambda e: e.reduce_max(cx.st8[0:1, 0:1], stb[0:1, 0:512],
                                          axis=AX.X),
         reads=[bst], writes=[cx.b_st8])
    if first:
        s.op("vector", lambda e: e.tensor_copy(acc[0:1, 0:1], cx.st8[0:1, 0:1]),
             reads=[cx.b_st8], writes=[b_acc])
    else:
        s.op("vector", lambda e: e.tensor_max(acc[0:1, 0:1], acc[0:1, 0:1],
                                              cx.st8[0:1, 0:1]),
             reads=[cx.b_st8, b_acc], writes=[b_acc])


def emit_negb(cx, acc_q, b_q, acc_k, b_k, margin, dst, b_dst):
    s = cx.s
    st, bst = cx.st8, cx.b_st8
    s.op("scalar", lambda e: e.activation(st[0:1, 1:2], acc_q[0:1, 0:1],
                                          AF.Sqrt),
         reads=[b_q], writes=[bst])
    s.op("scalar", lambda e: e.activation(st[0:1, 2:3], acc_k[0:1, 0:1],
                                          AF.Sqrt),
         reads=[b_k], writes=[bst])
    s.op("vector", lambda e: e.tensor_mul(st[0:1, 3:4], st[0:1, 1:2],
                                          st[0:1, 2:3]),
         reads=[bst], writes=[bst])
    s.op("scalar", lambda e: e.activation(st[0:1, 4:5], st[0:1, 3:4],
                                          AF.Copy, scale=-margin),
         reads=[bst], writes=[bst])
    s.op("vector", lambda e: e.tensor_copy(cx.st8bf[0:1, 0:1], st[0:1, 4:5]),
         reads=[bst], writes=[cx.b_st8bf])
    pb = bank(cx, 7)
    s.op("tensor", lambda e: e.matmul(pb[:, 0:1], cx.ones128[0:1, :],
                                      cx.st8bf[0:1, 0:1], start=True,
                                      stop=True),
         reads=[cx.b_st8bf, cx.b_ones128], writes=[bbuf(cx, 7)])
    s.op("vector", lambda e: e.tensor_copy(dst[:, 0:1], pb[:, 0:1]),
         reads=[bbuf(cx, 7)], writes=[b_dst])


def emit_attn_a_proj(cx):
    s, ar, S, nc = cx.s, cx.ar, cx.S, cx.nc
    NTW = S // 512
    ar.mark()
    stg = ar.alloc("stgA", [PB, 4 * 512], BF16)
    b_stg = [Buf() for _ in range(4)]

    def proj_pass(w_t, b_w, act_t, b_act, nch, A, tag, n_out=8):
        for n in range(n_out):
            for t in range(NTW):
                pk = bank(cx, (n + t) % 4)
                bpk = bbuf(cx, (n + t) % 4)
                def mm(e, n=n, t=t, pk=pk, w_t=w_t, act_t=act_t, nch=nch):
                    ins = None
                    for c in range(nch):
                        ins = e.matmul(
                            pk, w_t[:, c * n_out * 128 + n * 128:
                                    c * n_out * 128 + (n + 1) * 128],
                            act_t[:, c * S + t * 512:c * S + (t + 1) * 512],
                            start=(c == 0), stop=(c == nch - 1))
                    return ins
                s.op("tensor", mm, reads=[b_w] + b_act, writes=[bpk],
                     tag=tag)
                A(n, t, pk, bpk)
    ar.mark()
    xn, b_xn = load_fm_from_spill(cx, cx.xbf_a.ap(), S, 0, D, "xnA",
                                  reads=cx.tb_xbf_a.r())
    ar.mark()
    cx.ebf = ar.alloc("ebfq", [PB, 2 * S], BF16)
    cx.b_ebf = Buf("ebfq")
    s.dma("sync", lambda e, t_=cx.ebf: e.dma_start(t_[:, :],
                                                   cx.d_cbf.ap()[:, 0:2 * S]),
          writes=[cx.b_ebf], tag="ldebf")
    wq, b_wq = load_w_fm(cx, cx.w["L0_attn_a_Wq"].ap(), D, D, "wqA")

    def q_evac(n, t, pk, bpk):
        i0, i1 = (t % 2) * 2, (t % 2) * 2 + 1
        s.op("vector", lambda e, t=t, pk=pk, i0=i0, eb=cx.ebf: e.tensor_mul(
            stg[:, i0 * 512:(i0 + 1) * 512], pk,
            eb[:, 0 * S + t * 512:0 * S + (t + 1) * 512]),
            reads=[bpk, cx.b_ebf], writes=[b_stg[i0]], tag="qsc-")
        s.op("vector", lambda e, t=t, pk=pk, i1=i1, eb=cx.ebf: e.tensor_mul(
            stg[:, i1 * 512:(i1 + 1) * 512], pk,
            eb[:, 1 * S + t * 512:1 * S + (t + 1) * 512]),
            reads=[bpk, cx.b_ebf], writes=[b_stg[i1]], tag="qsc+")
        s.dma("scalar", lambda e, n=n, t=t, i0=i0: e.dma_start(
            cx.qsp.ap()[0, _ts(n, 128), _ts(t, 512)],
            stg[:, i0 * 512:(i0 + 1) * 512]),
            reads=[b_stg[i0]], writes=[cx.tb_qsp[0].b(n, t)], tag="qst-")
        s.dma("scalar", lambda e, n=n, t=t, i1=i1: e.dma_start(
            cx.qsp.ap()[1, _ts(n, 128), _ts(t, 512)],
            stg[:, i1 * 512:(i1 + 1) * 512]),
            reads=[b_stg[i1]], writes=[cx.tb_qsp[1].b(n, t)], tag="qst+")
        emit_cs_stat(cx, pk, bpk, cx.cs_q, cx.b_cs_q, n == 0 and t == 0)
    proj_pass(wq, b_wq, xn, b_xn, 8, q_evac, "qmm")
    ar.release()
    ar.mark()
    wkv = ar.alloc("wkv", [PB, 8 * D], BF16)
    b_wkv = Buf("wkv")
    re = cx.w["L0_attn_a_Wk"].ap().rearrange("(c p) n -> p c n", p=128)
    s.dma("gpsimd", lambda e, re=re: e.dma_start(
        wkv[:, :].rearrange("p (c n) -> p c n", c=8), re),
        writes=[b_wkv], tag="ldwk")
    ar.mark()
    wv, b_wv = load_w_fm(cx, cx.w["L0_attn_a_Wv"].ap(), D, D, "wvA")
    s.op("vector", lambda e: e.tensor_add(wkv[:, :], wkv[:, :], wv[:, :]),
         reads=[b_wkv, b_wv], writes=[b_wkv])
    ar.release()

    def kv_evac(n, t, pk, bpk):
        i = t % 4
        s.op("scalar", lambda e, pk=pk, i=i: e.activation(
            stg[:, i * 512:(i + 1) * 512], pk, AF.Copy),
            reads=[bpk], writes=[b_stg[i]], tag="kvev")
        s.dma("scalar", lambda e, n=n, t=t, i=i: e.dma_start(
            cx.kvsp.ap()[_ts(n, 128), _ts(t, 512)],
            stg[:, i * 512:(i + 1) * 512]),
            reads=[b_stg[i]], writes=[cx.tb_kvsp.b(n, t)], tag="kvst")
    proj_pass(wkv, b_wkv, xn, b_xn, 8, kv_evac, "kvmm")
    ar.release()
    ar.release()
    lat = ar.alloc("latA", [PB, 2 * S], BF16)
    b_lat = [Buf(), Buf()]
    ar.mark()
    kv = ar.alloc("kvA", [PB, 8 * S], BF16)
    b_kv = [Buf() for _ in range(8)]
    for n in range(8):
        s.dma("sync", lambda e, n=n: e.dma_start(
            kv[:, n * S:(n + 1) * S], cx.kvsp.ap()[_ts(n, 128), :]),
            reads=[cx.tb_kvsp.b(n, t) for t in range(NTW)],
            writes=[b_kv[n]], tag="kvld")
    wdn, b_wdn = load_w_fm(cx, cx.w["L0_attn_a_W_down_kv"].ap(), D, DL,
                           "wdnA")

    def lat_evac(n, t, pk, bpk):
        s.op("scalar", lambda e, n=n, t=t, pk=pk: e.activation(
            lat[:, n * S + t * 512:n * S + (t + 1) * 512], pk, AF.Copy),
            reads=[bpk], writes=[b_lat[n]], tag="latev")

    for n in range(2):
        for t in range(NTW):
            pk = bank(cx, (n + t) % 4)
            bpk = bbuf(cx, (n + t) % 4)
            def mm(e, n=n, t=t, pk=pk):
                ins = None
                for c in range(8):
                    ins = e.matmul(
                        pk, wdn[:, c * DL + n * 128:c * DL + (n + 1) * 128],
                        kv[:, c * S + t * 512:c * S + (t + 1) * 512],
                        start=(c == 0), stop=(c == 7))
                return ins
            s.op("tensor", mm, reads=[b_wdn] + b_kv, writes=[bpk],
                 tag="latmm")
            lat_evac(n, t, pk, bpk)
    ar.release()
    ar.mark()
    cx.ebf = ar.alloc("ebfk", [PB, 2 * S], BF16)
    cx.b_ebf = Buf("ebfk")
    s.dma("sync", lambda e, t_=cx.ebf: e.dma_start(t_[:, :],
                                                   cx.d_cbf.ap()[:, 2 * S:4 * S]),
          writes=[cx.b_ebf], tag="ldebfk")
    wuk, b_wuk = load_w_fm(cx, cx.w["L0_attn_a_W_up_k"].ap(), DL, D, "wukA")

    def k_evac(n, t, pk, bpk):
        i0, i1 = (t % 2) * 2, (t % 2) * 2 + 1
        s.op("vector", lambda e, t=t, pk=pk, i0=i0, eb=cx.ebf: e.tensor_mul(
            stg[:, i0 * 512:(i0 + 1) * 512], pk,
            eb[:, 0 * S + t * 512:0 * S + (t + 1) * 512]),
            reads=[bpk, cx.b_ebf], writes=[b_stg[i0]], tag="ksc-")
        s.op("vector", lambda e, t=t, pk=pk, i1=i1, eb=cx.ebf: e.tensor_mul(
            stg[:, i1 * 512:(i1 + 1) * 512], pk,
            eb[:, 1 * S + t * 512:1 * S + (t + 1) * 512]),
            reads=[bpk, cx.b_ebf], writes=[b_stg[i1]], tag="ksc+")
        s.dma("scalar", lambda e, n=n, t=t, i0=i0: e.dma_start(
            cx.qsp.ap()[2, _ts(n, 128), _ts(t, 512)],
            stg[:, i0 * 512:(i0 + 1) * 512]),
            reads=[b_stg[i0]], writes=[cx.tb_qsp[2].b(n, t)], tag="kst-")
        s.dma("scalar", lambda e, n=n, t=t, i1=i1: e.dma_start(
            cx.qsp.ap()[3, _ts(n, 128), _ts(t, 512)],
            stg[:, i1 * 512:(i1 + 1) * 512]),
            reads=[b_stg[i1]], writes=[cx.tb_qsp[3].b(n, t)], tag="kst+")
        emit_cs_stat(cx, pk, bpk, cx.cs_k, cx.b_cs_k, n == 0 and t == 0)

    for n in range(8):
        for t in range(NTW):
            pk = bank(cx, (n + t) % 4)
            bpk = bbuf(cx, (n + t) % 4)
            def mm(e, n=n, t=t, pk=pk):
                ins = None
                for c in range(2):
                    ins = e.matmul(
                        pk, wuk[:, c * D + n * 128:c * D + (n + 1) * 128],
                        lat[:, c * S + t * 512:c * S + (t + 1) * 512],
                        start=(c == 0), stop=(c == 1))
                return ins
            s.op("tensor", mm, reads=[b_wuk, b_lat[0], b_lat[1]],
                 writes=[bpk], tag="kRmm")
            k_evac(n, t, pk, bpk)
    ar.release()
    ar.mark()
    wuv, b_wuv = load_w_fm(cx, cx.w["L0_attn_a_W_up_v"].ap(), DL, D, "wuvA")
    vst = ar.alloc("vstA", [PB, 2 * 1040], BF16)
    b_vst = [Buf(), Buf()]
    s.op("vector", lambda e: e.memset(vst[:, :], 1.0),
         writes=[b_vst[0], b_vst[1]])
    for t in range(S // 128):
        pw = cx.PW[2 + (t % 2)]
        bws = [bbuf(cx, 4 + 2 * (t % 2)), bbuf(cx, 5 + 2 * (t % 2))]
        def mm(e, t=t, pw=pw):
            ins = None
            for half in range(2):
                for c in range(2):
                    ins = e.matmul(
                        pw[:, half * 512:(half + 1) * 512],
                        lat[:, c * S + t * 128:c * S + (t + 1) * 128],
                        wuv[:, c * D + half * 512:c * D + (half + 1) * 512],
                        start=(c == 0), stop=(c == 1))
            return ins
        s.op("tensor", mm, reads=[b_wuv, b_lat[0], b_lat[1]], writes=bws,
             tag="vRmm")
        p = t % 2
        s.op("scalar", lambda e, pw=pw, p=p: e.activation(
            vst[:, p * 1040:(p + 1) * 1040].rearrange(
                "q (h c) -> q h c", h=16)[:, :, 0:64],
            pw[:, 0:1024].rearrange("q (h d) -> q h d", h=16),
            AF.Copy),
            reads=bws, writes=[b_vst[p]], tag="vev")
        s.dma("scalar", lambda e, t=t, p=p: e.dma_start(
            cx.vsp.ap().rearrange("g r c -> r g c")[_ts(t, 128)],
            vst[:, p * 1040:(p + 1) * 1040].rearrange("q (g c) -> q g c", g=8)),
            reads=[b_vst[p]], writes=cx.tb_vsp.row(t), tag="vst")
    emit_negb(cx, cx.cs_q, cx.b_cs_q, cx.cs_k, cx.b_cs_k,
              np.exp(2 * LAM * cx.S) * 1.0000001, cx.negb, cx.b_negb)
    ar.release()
    ar.release()


def emit_div_evac(cx, p, qc, h, ab, bab, ofst, b_ofst, rzrow, b_rzrow,
                  rzbc, b_rzbc, tag):
    s = cx.s
    sl = (p * 16 + qc * 2 + h) % 4
    s.op("scalar", lambda e, ab=ab: e.activation(rzrow[64:65, 0:512],
                                                 ab[64:65, 0:512], AF.Copy),
         reads=[bab], writes=[b_rzrow], tag=tag + "cp")
    s.dma("sync", lambda e, sl=sl: e.dma_start(
        cx.zb.ap()[sl:sl + 1, :], rzrow[64:65, 0:512]),
        reads=[b_rzrow], writes=[cx.tb_zb[sl]], tag=tag + "zo")
    src_ap = bass.AP(cx.zb.ap().tensor, sl * 512, [[0, 64], [1, 512]])
    s.dma("sync", lambda e, src_ap=src_ap: e.dma_start(rzbc[:, :], src_ap),
          reads=[cx.tb_zb[sl]], writes=[b_rzbc], tag=tag + "zi")
    s.op("vector", lambda e: e.reciprocal(rzbc[:, :], rzbc[:, :]),
         reads=[b_rzbc], writes=[b_rzbc], tag=tag + "rc")
    s.op("vector", lambda e, ab=ab, h=h: e.tensor_mul(
        ofst[h * 64:(h + 1) * 64, :], ab[0:64, 0:512], rzbc[:, :]),
        reads=[bab, b_rzbc], writes=[b_ofst], tag=tag + "dv")


def emit_attn_a_attn(cx):
    s, ar, S = cx.s, cx.ar, cx.S
    NKT = S // 128
    NQC = S // 512
    ar.mark()
    sl = {}
    b_sl = {}
    for nm in ("qn", "qp"):
        sl[nm] = [ar.alloc(f"A3{nm}", [PB, S], BF16)]
        b_sl[nm] = [Buf()]
    kz = {}
    b_kz = {}
    for nm in ("kn0", "kn1", "kp0", "kp1"):
        kz[nm] = ar.alloc(f"A3{nm}", [PB, S], BF16)
        b_kz[nm] = Buf()
        zr = slice(64, 128) if nm[-1] == "0" else slice(0, 64)
        s.op("vector", lambda e, t_=kz[nm], zr=zr: e.memset(t_[zr, :], 0.0),
             writes=[b_kz[nm]], tag="A3kz0")
    vv = [ar.alloc("A3v", [PB, NKT * 130], BF16)]
    b_vv = [Buf()]
    eT = ar.alloc("A3e", [PB, NKT * 1024], BF16)
    b_eT = [Buf(f"A3e{k}") for k in range(NKT)]
    ofst = [ar.alloc(f"A3o{i}", [PB, 512], BF16) for i in range(2)]
    b_ofst = [Buf(), Buf()]
    rzrow = [ar.alloc(f"A3rz{i}", [PB, 512], F32) for i in range(2)]
    b_rzrow = [Buf() for _ in range(2)]
    rzbc = [ar.alloc(f"A3rb{i}", [64, 512], F32) for i in range(2)]
    b_rzbc = [Buf(), Buf()]
    SLOT = {"qn": 0, "qp": 1, "kn": 2, "kp": 3}

    def sel(kt, qc, d):
        r = kt - 4 * qc
        if kt * 128 + 127 < qc * 512:
            return "kp", sl["qn"][0], b_sl["qn"][0], -1
        if kt * 128 > qc * 512 + 511:
            return "kn", sl["qp"][0], b_sl["qp"][0], -1
        return "kp", sl["qn"][0], b_sl["qn"][0], r

    def emit_kt(p, qc, kt, d, pend):
        kside, Qt, bQ, dg = sel(kt, qc, d)
        par = kt % 3
        pw = cx.PW[(0, 1, 3)[par]]
        bk0 = (0, 2, 6)[par]
        pwb = [bbuf(cx, bk0), bbuf(cx, bk0 + 1)]
        K0, K1 = kz[kside + "0"], kz[kside + "1"]
        bK = [b_kz[kside + "0"], b_kz[kside + "1"]]

        def mmp(e, K0=K0, K1=K1, Qt=Qt, kt=kt, qc=qc, pw=pw):
            e.matmul(pw[:, 0:512], K0[0:64, _ts(kt, 128)],
                     Qt[0:64, _ts(qc, 512)], start=True, stop=True)
            return e.matmul(pw[:, 512:1024], K1[64:128, _ts(kt, 128)],
                            Qt[64:128, _ts(qc, 512)], start=True, stop=True)
        s.op("tensor", mmp, reads=bK + [bQ], writes=pwb, tag="A3mm")
        if dg >= 0:
            s.op("vector", lambda e, dg=dg, pw=pw: e.tensor_mul(
                pw[:, 0:1024].rearrange("p (h c) -> p h c", h=2),
                pw[:, 0:1024].rearrange("p (h c) -> p h c", h=2),
                cx.c32[:, C_CORR + dg * 512:C_CORR + (dg + 1) * 512]
                .unsqueeze(1).broadcast_to([PB, 2, 512])),
                reads=pwb + [cx.b_c32], writes=pwb, tag="A3corr")
        if pend is not None:
            emit_av(kt, d)
        s.op("scalar", lambda e, kt=kt, pw=pw: e.activation(
            eT[:, _ts(kt, 1024)], pw[:, 0:1024], AF.Exp,
            bias=cx.negb[:, 0:1]),
            reads=pwb + [cx.b_negb], writes=[b_eT[kt]], tag="A3exp")

    def emit_av(kt, d):
        for h in range(2):
            ab = cx.PW[2][:, h * 512:(h + 1) * 512]
            s.op("tensor", lambda e, h=h, kt=kt, ab=ab: e.matmul(
                ab[0:65, :],
                vv[0][:, kt * 130 + h * 65:kt * 130 + (h + 1) * 65],
                eT[:, kt * 1024 + h * 512:kt * 1024 + (h + 1) * 512],
                start=(kt == 0), stop=(kt == NKT - 1)),
                reads=[b_eT[kt], b_vv[0]], writes=[bbuf(cx, 4 + h)],
                tag="A3av")

    def emit_div(p, qc):
        for h in range(2):
            ab = cx.PW[2][:, h * 512:(h + 1) * 512]
            dpar = (qc * 2 + h) % 4
            emit_div_evac(cx, p, qc, h, ab, bbuf(cx, 4 + h), ofst[qc % 2],
                          b_ofst[qc % 2], rzrow[dpar % 2], b_rzrow[dpar % 2],
                          rzbc[dpar % 2], b_rzbc[dpar % 2], "A3")
        s.dma("scalar", lambda e, p=p, qc=qc: e.dma_start(
            cx.ospf.ap()[_ts(p, 128), _ts(qc, 512)], ofst[qc % 2][:, :]),
            reads=[b_ofst[qc % 2]], writes=[cx.tb_ospf.b(p, qc)], tag="A3ost")

    for p in range(8):
        d = 0
        for nm in ("qn", "qp"):
            s.dma("sync", lambda e, nm=nm, p=p: e.dma_start(
                sl[nm][0][:, :], cx.qsp.ap()[SLOT[nm], _ts(p, 128), :]),
                reads=[cx.tb_qsp[SLOT[nm]].row(p)], writes=[b_sl[nm][0]],
                tag="A3ld")
        for nm, h in (("kn", 0), ("kn", 1), ("kp", 0), ("kp", 1)):
            s.dma("sync", lambda e, nm=nm, h=h, p=p: e.dma_start(
                kz[nm + str(h)][h * 64:h * 64 + 64, :],
                cx.qsp.ap()[SLOT[nm], p * 128 + h * 64:p * 128 + h * 64 + 64,
                            :]),
                reads=[cx.tb_qsp[SLOT[nm]].row(p)], writes=[b_kz[nm + str(h)]],
                tag="A3ldk")
        s.dma("gpsimd", lambda e, p=p: e.dma_start(
            vv[0][:, :].rearrange("q (t c) -> q t c", c=130),
            cx.vsp.ap()[p].rearrange("(t q) c -> q t c", q=128)),
            reads=[b.b(t, p) for t in range(NKT) for b in [cx.tb_vsp]],
            writes=[b_vv[0]], tag="A3ldv")
        pend = None
        for qc in range(NQC):
            for kt in range(NKT):
                emit_kt(p, qc, kt, d, pend)
            if pend is not None:
                emit_div(p, pend)
            pend = qc
        for kt in range(NKT):
            emit_av(kt, d)
        emit_div(p, pend)
    ar.release()


def emit_attn_out(cx, wo_name, ncols_o, x_res_ap, x_res_tb, x_out_ap,
                  x_out_tb, xbf_out_ap, xbf_out_tb, xf_out_ap=None,
                  xf_out_tb=None, tag="AO"):
    s, ar, S = cx.s, cx.ar, cx.S
    NT = S // 128
    nch = ncols_o // 128
    ar.mark()
    ofm = cx.ar.alloc(tag + "_ofm", [PB, nch * S], BF16)
    b_ofm = [Buf() for _ in range(nch)]
    for c in range(nch):
        s.dma("sync", lambda e, c=c: e.dma_start(
            ofm[:, c * S:(c + 1) * S], cx.ospf.ap()[_ts(c, 128), :]),
            reads=cx.tb_ospf.row(c), writes=[b_ofm[c]], tag=tag + "ofml")
    wo, b_wo = load_w_fm(cx, cx.w[wo_name].ap(), ncols_o, D, tag + "_wo")
    xt = ar.alloc(tag + "xt", [PB, 2 * D], F32)
    x1 = ar.alloc(tag + "x1", [PB, 2 * D], F32)
    ob = ar.alloc(tag + "ob", [PB, 2 * D], BF16)
    of = ar.alloc(tag + "of", [PB, 2 * D], F32)
    b_xt = [Buf(), Buf()]
    b_x1 = [Buf(), Buf()]
    b_ob = [Buf(), Buf()]
    b_of = [Buf(), Buf()]
    for t in range(NT):
        pr = t % 2
        cols = slice(pr * D, (pr + 1) * D)
        pw = cx.PW[pr]
        bws = [bbuf(cx, 2 * pr), bbuf(cx, 2 * pr + 1)]
        s.dma("sync", lambda e, t=t, cols=cols: e.dma_start(
            xt[:, cols], x_res_ap[_ts(t, 128), :]),
            reads=[x_res_tb.r(t * 128, t * 128 + 128)] if x_res_tb else [],
            writes=[b_xt[pr]], tag=tag + "ldx")
        def mm(e, t=t, pw=pw):
            ins = None
            for half in range(2):
                for c in range(nch):
                    ins = e.matmul(
                        pw[:, half * 512:(half + 1) * 512],
                        ofm[:, c * S + t * 128:c * S + (t + 1) * 128],
                        wo[:, c * D + half * 512:c * D + (half + 1) * 512],
                        start=(c == 0), stop=(c == nch - 1))
            return ins
        s.op("tensor", mm, reads=b_ofm + [b_wo], writes=bws, tag=tag + "mm")
        s.op("vector", lambda e, cols=cols, pw=pw: e.tensor_add(
            x1[:, cols], pw[:, 0:1024], xt[:, cols]),
            reads=bws + [b_xt[pr]], writes=[b_x1[pr]], tag=tag + "res")
        s.dma("scalar", lambda e, t=t, cols=cols: e.dma_start(
            x_out_ap[_ts(t, 128), :], x1[:, cols]),
            reads=[b_x1[pr]], writes=[x_out_tb.w(t * 128, t * 128 + 128)],
            tag=tag + "stx")
        emit_ln(cx, x1, b_x1[pr], cols, ob[:, cols], b_ob[pr],
                (of[:, cols] if xf_out_ap is not None else None),
                (b_of[pr] if xf_out_ap is not None else None))
        s.dma("scalar", lambda e, t=t, cols=cols: e.dma_start(
            xbf_out_ap[ZR + t * 128:ZR + (t + 1) * 128, :], ob[:, cols]),
            reads=[b_ob[pr]], writes=sp_w(xbf_out_tb, t),
            tag=tag + "stb")
        if xf_out_ap is not None:
            s.dma("scalar", lambda e, t=t, cols=cols: e.dma_start(
                xf_out_ap[ZR + t * 128:ZR + (t + 1) * 128, :], of[:, cols]),
                reads=[b_of[pr]],
                writes=sp_w(xf_out_tb, t), tag=tag + "stf")
    ar.release()


def sp_w(tb, t):
    return tb.r(128 + t * 128, 128 + (t + 1) * 128)


def sp_rs(tb, t, shift=0):
    return tb.r(128 + t * 128 - shift, 128 + (t + 1) * 128 - shift)


W_NAMES = [
    "L0_attn_a_Wq", "L0_attn_a_Wk", "L0_attn_a_Wv", "L0_attn_a_Wo",
    "L0_attn_a_W_down_kv", "L0_attn_a_W_up_k", "L0_attn_a_W_up_v",
    "L0_attn_b_Wq", "L0_attn_b_Wk", "L0_attn_b_Wv", "L0_attn_b_Wo",
    "L0_ffn_wqkv", "L0_ffn_q_scale", "L0_ffn_k_scale", "L0_ffn_wo",
    "L0_ffn_ln_g", "L0_ffn_ln_b", "L0_ffn_w1", "L0_ffn_b1", "L0_ffn_w2",
    "L0_ffn_b2",
]
W_SHAPES = {
    "L0_attn_a_Wq": [D, D], "L0_attn_a_Wk": [D, D], "L0_attn_a_Wv": [D, D],
    "L0_attn_a_Wo": [D, D], "L0_attn_a_W_down_kv": [D, DL],
    "L0_attn_a_W_up_k": [DL, D], "L0_attn_a_W_up_v": [DL, D],
    "L0_attn_b_Wq": [D, D], "L0_attn_b_Wk": [D, D], "L0_attn_b_Wv": [D, D],
    "L0_attn_b_Wo": [D, D],
    "L0_ffn_wqkv": [2, D, 3 * INNER], "L0_ffn_q_scale": [2, HD],
    "L0_ffn_k_scale": [2, HD], "L0_ffn_wo": [2, INNER, D],
    "L0_ffn_ln_g": [2, D], "L0_ffn_ln_b": [2, D],
    "L0_ffn_w1": [2, D, 2 * HID], "L0_ffn_b1": [2, 2 * HID],
    "L0_ffn_w2": [2, HID, D], "L0_ffn_b2": [2, D],
}


def make_ctx(S, drains=False):
    nc = bass.Bass("TRN2", target_bir_lowering=False)
    cx = Ctx()
    cx.nc = nc
    cx.S = S
    cx.s = Sched(nc, drains=drains)
    cx.ar = Arena(nc, cx.s)
    NT = S // 128
    cx.d_xin = nc.dram_tensor("L0_attn_a_x", [S, D], F32, kind="ExternalInput")
    cx.d_out = nc.dram_tensor("out", [S, D], F32, kind="ExternalOutput")
    cx.w = {}
    for n in W_NAMES:
        if n == "L0_attn_a_x":
            continue
        cx.w[n] = nc.dram_tensor(n, W_SHAPES[n], F32, kind="ExternalInput")
    cx.d_cbf = nc.dram_tensor("CBF", [PB, 4 * S + 640], BF16, kind="ExternalInput")
    cx.d_c32 = nc.dram_tensor("C32", [PB, C32_COLS], F32, kind="ExternalInput")
    cx.d_rot = nc.dram_tensor("ROT", [4, S, HD], F32, kind="ExternalInput")
    cx.x_a = nc.dram_tensor("x_a", [S, D], F32)
    cx.x_b = nc.dram_tensor("x_b", [S, D], F32)
    cx.xf_a = nc.dram_tensor("xf_a", [ZR + S, D], F32)
    cx.xf_b = nc.dram_tensor("xf_b", [ZR + S, D], F32)
    cx.xbf_a = nc.dram_tensor("xbf_a", [ZR + S, D], BF16)
    cx.xbf_b = nc.dram_tensor("xbf_b", [ZR + S, D], BF16)
    cx.qsp = nc.dram_tensor("qsp", [4, D, S], BF16)
    cx.kvsp = nc.dram_tensor("kvsp", [D, S], BF16)
    cx.vsp = nc.dram_tensor("vsp", [8, S, 130], BF16)
    cx.osp = nc.dram_tensor("osp", [S, D], BF16)
    cx.ospf = nc.dram_tensor("ospf", [D, S], BF16)
    cx.zb = nc.dram_tensor("zb", [4, 512], F32)
    cx.qtm = nc.dram_tensor("qtm", [S, INNER], BF16)
    cx.ktm = nc.dram_tensor("ktm", [S, INNER], BF16)
    cx.tb_out = TBuf("out", S, 128)
    cx.tb_x_a = TBuf("x_a", S, 128)
    cx.tb_x_b = TBuf("x_b", S, 128)
    cx.tb_xf_a = TBuf("xf_a", 128 + S, 128)
    cx.tb_xf_b = TBuf("xf_b", 128 + S, 128)
    cx.tb_xbf_a = TBuf("xbf_a", 128 + S, 128)
    cx.tb_xbf_b = TBuf("xbf_b", 128 + S, 128)
    cx.tb_qsp = [TBuf2(f"qsp{i}", 8, S // 512) for i in range(4)]
    cx.tb_kvsp = TBuf2("kvsp", 8, S // 512)
    cx.tb_vsp = TBuf2("vsp", NT, 8)
    cx.tb_osp = TBuf2("osp", NT, 8)
    cx.tb_ospf = TBuf2("ospf", 8, S // 512)
    cx.tb_zb = [Buf(f"zb{i}") for i in range(4)]
    cx.tb_qtm = TBuf2("qtm", NT, 4)
    cx.tb_ktm = TBuf2("ktm", NT, 4)
    cx.PW = [nc.alloc_psum_tensor(f"PW{i}", [PB, 1024], F32)
             for i in range(4)]
    cx.b_pw = [Buf(f"pw{i}") for i in range(8)]
    ar = cx.ar
    cx.ones128 = ar.alloc("ones128", [PB, 128], BF16)
    cx.b_ones128 = Buf("ones128")
    cx.c32 = ar.alloc("c32sb", [PB, C32_COLS], F32)
    cx.b_c32 = Buf("c32")
    cx.mkb = ar.alloc("mkb", [PB, 640], BF16)
    cx.b_mkb = Buf("mkb")
    cx.st8 = ar.alloc("st8", [PB, 8], F32)
    cx.b_st8 = Buf("st8")
    cx.st8bf = ar.alloc("st8bf", [PB, 2], BF16)
    cx.b_st8bf = Buf("st8bf")
    cx.sq_st = ar.alloc("sq_st", [PB, 512], BF16)
    cx.b_sq_st = Buf("sq_st")
    cx.cs_q = ar.alloc("cs_q", [1, 2], F32)
    cx.b_cs_q = Buf("cs_q")
    cx.cs_k = ar.alloc("cs_k", [1, 2], F32)
    cx.b_cs_k = Buf("cs_k")
    cx.negb = ar.alloc("negb", [PB, 4], F32)
    cx.b_negb = Buf("negb")
    cx.ln_scr = ar.alloc("ln_scr", [PB, 16], F32)
    cx.b_ln_scr = Buf("ln_scr")
    cx.fsc = ar.alloc("fsc", [PB, 2048], F32)
    cx.b_fsc = Buf("fsc")
    cx.b_fscp = [Buf("fscp0"), Buf("fscp1")]
    cx.b_lnp = [Buf("lnp0"), Buf("lnp1")]
    cx.f8 = ar.alloc("f8", [PB, 48], F32)
    cx.b_f8 = [Buf("f8a"), Buf("f8b")]
    cx.t_ob = ar.alloc("t_ob", [PB, 2 * D], BF16)
    cx.bt_ob = [Buf(), Buf()]
    cx.t_of = ar.alloc("t_of", [PB, 2 * D], F32)
    cx.bt_of = [Buf(), Buf()]
    cx.t_xr = ar.alloc("t_xr", [PB, 2 * D], F32)
    cx.bt_xr = [Buf(), Buf()]
    cx.t_xo = ar.alloc("t_xo", [PB, 2 * D], F32)
    cx.bt_xo = [Buf(), Buf()]
    cx.kc = ar.alloc("kconst", [PB, 2], F32)
    cx.b_kc = Buf("kconst")
    s = cx.s
    s.op("vector", lambda e: e.memset(cx.kc[:, 0:1], EPS_LN),
         writes=[cx.b_kc])
    s.op("vector", lambda e: e.memset(cx.ones128[:, :], 1.0),
         writes=[cx.b_ones128])
    s.dma("sync", lambda e: e.dma_start(cx.c32[:, :], cx.d_c32.ap()),
          writes=[cx.b_c32])
    s.dma("sync", lambda e: e.dma_start(cx.mkb[:, :],
                                        cx.d_cbf.ap()[:, 4 * S:4 * S + 640]),
          writes=[cx.b_mkb])
    return cx


def build_program(S, upto="full", drains=False):
    cx = make_ctx(S, drains=drains)
    emit_zero_rows(cx)
    emit_ln_pass(cx, cx.d_xin.ap(), None, cx.xbf_a.ap(), cx.tb_xbf_a,
                 tag="ln0")
    emit_attn_a_proj(cx)
    emit_attn_a_attn(cx)
    if upto == "attn_a":
        emit_attn_out(cx, "L0_attn_a_Wo", D, cx.d_xin.ap(), None,
                      cx.d_out.ap(), cx.tb_x_a, cx.xbf_b.ap(), cx.tb_xbf_b,
                      cx.xf_a.ap(), cx.tb_xf_a, tag="AO")
        cx.s.finalize()
        return cx
    emit_attn_out(cx, "L0_attn_a_Wo", D, cx.d_xin.ap(), None,
                  cx.x_a.ap(), cx.tb_x_a, cx.xbf_b.ap(), cx.tb_xbf_b,
                  cx.xf_a.ap(), cx.tb_xf_a, tag="AO")
    if upto == "ffn_l0":
        tail = make_tail_dbg(cx)
        emit_ffn_layer(cx, 0, cx.xbf_b.ap(), cx.tb_xbf_b, cx.xf_a.ap(),
                       cx.tb_xf_a, cx.xf_b.ap(), cx.tb_xf_b, tail)
        cx.s.finalize()
        return cx
    tail = make_tail_stream(cx, cx.xf_a.ap(), cx.tb_xf_a, cx.xbf_a.ap(),
                            cx.tb_xbf_a)
    emit_ffn_layer(cx, 0, cx.xbf_b.ap(), cx.tb_xbf_b, cx.xf_a.ap(),
                   cx.tb_xf_a, cx.xf_b.ap(), cx.tb_xf_b, tail)
    if upto == "ffn1":
        tail = make_tail_sublayer(cx, cx.x_a.ap(), cx.tb_x_a,
                                  cx.d_out.ap(), cx.tb_out, None, None)
        emit_ffn_layer(cx, 1, cx.xbf_a.ap(), cx.tb_xbf_a, cx.xf_a.ap(),
                       cx.tb_xf_a, cx.xf_b.ap(), cx.tb_xf_b, tail)
        cx.s.finalize()
        return cx
    tail = make_tail_sublayer(cx, cx.x_a.ap(), cx.tb_x_a, cx.x_b.ap(),
                              cx.tb_x_b, cx.xbf_b.ap(), cx.tb_xbf_b)
    emit_ffn_layer(cx, 1, cx.xbf_a.ap(), cx.tb_xbf_a, cx.xf_a.ap(),
                   cx.tb_xf_a, cx.xf_b.ap(), cx.tb_xf_b, tail)
    emit_attn_b_proj(cx)
    emit_attn_b_attn(cx)
    if upto == "attn_b":
        emit_attn_out(cx, "L0_attn_b_Wo", D, cx.x_b.ap(), cx.tb_x_b,
                      cx.d_out.ap(), cx.tb_out, cx.xbf_a.ap(), cx.tb_xbf_a,
                      cx.xf_a.ap(), cx.tb_xf_a, tag="BO")
        cx.s.finalize()
        return cx
    emit_attn_out(cx, "L0_attn_b_Wo", D, cx.x_b.ap(), cx.tb_x_b,
                  cx.x_a.ap(), cx.tb_x_a, cx.xbf_a.ap(), cx.tb_xbf_a,
                  cx.xf_a.ap(), cx.tb_xf_a, tag="BO")
    tail = make_tail_stream(cx, cx.xf_a.ap(), cx.tb_xf_a, cx.xbf_b.ap(),
                            cx.tb_xbf_b)
    emit_ffn_layer(cx, 0, cx.xbf_a.ap(), cx.tb_xbf_a, cx.xf_a.ap(),
                   cx.tb_xf_a, cx.xf_b.ap(), cx.tb_xf_b, tail)
    tail = make_tail_sublayer(cx, cx.x_a.ap(), cx.tb_x_a, cx.d_out.ap(),
                              cx.tb_out, None, None)
    emit_ffn_layer(cx, 1, cx.xbf_b.ap(), cx.tb_xbf_b, cx.xf_a.ap(),
                   cx.tb_xf_a, cx.xf_b.ap(), cx.tb_xf_b, tail)
    cx.s.finalize()
    return cx


def make_tail_dbg(cx):
    def tail(t, x1, cols, b):
        cx.s.dma("scalar", lambda e, t=t, cols=cols: e.dma_start(
            cx.d_out.ap()[_ts(t, 128), :], x1[:, cols]),
            reads=[b], writes=cx.tb_out.w(t * 128, (t + 1) * 128),
            tag="dbgout")
    return tail



def emit_l2n_rot(cx, psq, bpsq, rt, b_rt, tab_a, tab_b, tile_t, out_bf, b_out):
    s = cx.s
    pr = tile_t % 2
    F0 = pr * 1024
    G0 = pr * 24
    f = cx.fsc
    bf_ = cx.b_fscp[pr]
    f8 = cx.f8
    b8 = cx.b_f8[pr]
    NT = cx.S // 128
    s.op("scalar", lambda e: e.activation(f[:, F0:F0 + 512], psq, AF.Square),
         reads=[bpsq], writes=[bf_], tag="l2sq")
    s.op("vector", lambda e: e.reduce_sum(
        f8[:, G0:G0 + 8], f[:, F0:F0 + 512].rearrange("p (h d) -> p h d", h=8), axis=AX.X),
        reads=[bf_], writes=[b8], tag="l2ss")
    s.op("scalar", lambda e: e.activation(f8[:, G0 + 8:G0 + 16], f8[:, G0:G0 + 8], AF.Sqrt),
         reads=[b8], writes=[b8], tag="l2rt")
    s.op("vector", lambda e: e.tensor_scalar_max(f8[:, G0 + 8:G0 + 16], f8[:, G0 + 8:G0 + 16],
                                                 1e-12),
         reads=[b8], writes=[b8], tag="l2cl")
    s.op("vector", lambda e: e.reciprocal(f8[:, G0 + 16:G0 + 24], f8[:, G0 + 8:G0 + 16]),
         reads=[b8], writes=[b8], tag="l2rc")
    s.op("vector", lambda e: e.tensor_tensor(
        f[:, F0:F0 + 512].rearrange("p (h d) -> p h d", h=8),
        psq.rearrange("p (h d) -> p h d", h=8),
        f8[:, G0 + 16:G0 + 24].unsqueeze(2).broadcast_to([PB, 8, 64]), op=ALU.mult),
        reads=[bpsq, b8], writes=[bf_], tag="l2mul")
    s.op("vector", lambda e: e.tensor_copy(
        f[:, F0 + 512:F0 + 1024].rearrange("p (h d) -> p h d", h=8)[:, :, 0:32],
        f[:, F0:F0 + 512].rearrange("p (h d) -> p h d", h=8)[:, :, 32:64]),
        reads=[bf_], writes=[bf_], tag="l2sh")
    s.op("vector", lambda e: e.tensor_copy(
        f[:, F0 + 512:F0 + 1024].rearrange("p (h d) -> p h d", h=8)[:, :, 32:64],
        f[:, F0:F0 + 512].rearrange("p (h d) -> p h d", h=8)[:, :, 0:32]),
        reads=[bf_], writes=[bf_], tag="l2sh2")
    s.op("vector", lambda e: e.tensor_tensor(
        f[:, F0:F0 + 512].rearrange("p (h d) -> p h d", h=8),
        f[:, F0:F0 + 512].rearrange("p (h d) -> p h d", h=8),
        rt[:, tab_a + tile_t * 64:tab_a + (tile_t + 1) * 64]
        .unsqueeze(1).broadcast_to([PB, 8, 64]), op=ALU.mult),
        reads=[bf_, b_rt], writes=[bf_], tag="l2ta")
    s.op("vector", lambda e: e.tensor_tensor(
        f[:, F0 + 512:F0 + 1024].rearrange("p (h d) -> p h d", h=8),
        f[:, F0 + 512:F0 + 1024].rearrange("p (h d) -> p h d", h=8),
        rt[:, tab_b + tile_t * 64:tab_b + (tile_t + 1) * 64]
        .unsqueeze(1).broadcast_to([PB, 8, 64]), op=ALU.mult),
        reads=[bf_, b_rt], writes=[bf_], tag="l2tb")
    s.op("vector", lambda e: e.tensor_add(out_bf, f[:, F0:F0 + 512],
                                          f[:, F0 + 512:F0 + 1024]),
         reads=[bf_], writes=[b_out], tag="l2o")


def emit_ffn_qkv(cx, l, xbf_ap, tb_xbf):
    s, ar, S = cx.s, cx.ar, cx.S
    NT = S // 128
    NB = S // 512
    rt = ar.alloc("rtab", [PB, 4 * NT * 64], F32)
    b_rt = Buf("rtab")
    ar.mark()
    rb = ar.alloc("rbase", [PB, 4 * NT * 64], F32)
    b_rb = Buf()
    gn = ar.alloc("gain", [PB, 4 * 64], F32)
    b_gn = Buf()
    gr = ar.alloc("grow", [1, 2 * 64], F32)
    b_gr = Buf()
    for i in range(4):
        s.dma("sync", lambda e, i=i: e.dma_start(
            rb[:, i * NT * 64:(i + 1) * NT * 64].rearrange(
                "p (t d) -> p t d", d=64),
            cx.d_rot.ap()[i].rearrange("(t p) d -> p t d", p=128)),
            writes=[b_rb], tag="rbld")
    s.dma("sync", lambda e, l=l: e.dma_start(
        gr[0:1, 0:64], cx.w["L0_ffn_q_scale"].ap()[l:l + 1, :]),
        writes=[b_gr], tag="gq")
    s.dma("sync", lambda e, l=l: e.dma_start(
        gr[0:1, 64:128], cx.w["L0_ffn_k_scale"].ap()[l:l + 1, :]),
        writes=[b_gr], tag="gk")
    grb = ar.alloc("growb", [1, 2 * 64], BF16)
    b_grb = Buf()
    s.op("vector", lambda e: e.tensor_copy(grb[0:1, :], gr[0:1, :]),
         reads=[b_gr], writes=[b_grb])
    pb = bank(cx, 6)
    s.op("tensor", lambda e: e.matmul(pb[:, 0:128], cx.ones128[0:1, :],
                                      grb[0:1, :], start=True, stop=True),
         reads=[b_grb, cx.b_ones128], writes=[bbuf(cx, 6)], tag="gbc")
    s.op("vector", lambda e: e.tensor_copy(gn[:, 0:128], pb[:, 0:128]),
         reads=[bbuf(cx, 6)], writes=[b_gn])
    s.op("vector", lambda e: e.tensor_copy(
        gn[:, 128:256].rearrange("p (g d) -> p g d", g=2)[:, :, 0:32],
        gn[:, 0:128].rearrange("p (g d) -> p g d", g=2)[:, :, 32:64]),
        reads=[b_gn], writes=[b_gn], tag="gsh")
    s.op("vector", lambda e: e.tensor_copy(
        gn[:, 128:256].rearrange("p (g d) -> p g d", g=2)[:, :, 32:64],
        gn[:, 0:128].rearrange("p (g d) -> p g d", g=2)[:, :, 0:32]),
        reads=[b_gn], writes=[b_gn], tag="gsh2")
    for i, gcol in enumerate((0, 128, 64, 192)):
        s.op("vector", lambda e, i=i, gcol=gcol: e.tensor_tensor(
            rt[:, i * NT * 64:(i + 1) * NT * 64].rearrange(
                "p (t d) -> p t d", d=64),
            rb[:, i * NT * 64:(i + 1) * NT * 64].rearrange(
                "p (t d) -> p t d", d=64),
            gn[:, gcol:gcol + 64].unsqueeze(1).broadcast_to([PB, NT, 64]),
            op=ALU.mult),
            reads=[b_rb, b_gn], writes=[b_rt], tag="rtb")
    ar.release()
    wqkv, b_wqkv = load_w_fm(cx, cx.w["L0_ffn_wqkv"].ap()[l], D, 3 * INNER,
                             "wqkv")
    s.op("vector", lambda e, vp=cx.vp: e.memset(vp[:, :], 1.0),
         writes=cx.b_vp, tag="vpones")
    hfm = [ar.alloc(f"hfm{i}", [PB, 8 * 512], BF16) for i in range(2)]
    b_hfm = [Buf(), Buf()]
    qst = ar.alloc("qst", [PB, 2 * 512], BF16)
    b_qst = [Buf(), Buf()]
    kst = ar.alloc("kst", [PB, 2 * 512], BF16)
    b_kst = [Buf(), Buf()]
    for b in range(NB):
        d = b % 2
        for c in range(8):
            sh = 1 if c >= 4 else 0
            s.dma("scalar", lambda e, b=b, c=c, sh=sh, d=d: e.dma_start(
                hfm[d][:, c * 512:(c + 1) * 512],
                xbf_ap[ZR - sh + b * 512:ZR - sh + (b + 1) * 512,
                       c * 128:(c + 1) * 128], transpose=True),
                reads=tb_xbf.r(128 + b * 512 - sh, 128 + (b + 1) * 512),
                writes=[b_hfm[d]], tag="hfmT")
        for tt in range(4):
            t = b * 4 + tt
            bk = (tt % 2) * 3
            pq, pk, pv = bank(cx, bk), bank(cx, bk + 1), bank(cx, bk + 2)
            bq, bkk, bv = bbuf(cx, bk), bbuf(cx, bk + 1), bbuf(cx, bk + 2)
            def mm(e, tt=tt, d=d, pq=pq, pk=pk, pv=pv):
                ins = None
                for c in range(8):
                    for j, p in enumerate((pq, pk, pv)):
                        ins = e.matmul(
                            p, hfm[d][:, c * 512 + tt * 128:
                                      c * 512 + (tt + 1) * 128],
                            wqkv[:, c * 3 * INNER + j * 512:
                                 c * 3 * INNER + (j + 1) * 512],
                            start=(c == 0), stop=(c == 7))
                return ins
            s.op("tensor", mm, reads=[b_hfm[d], b_wqkv],
                 writes=[bq, bkk, bv], tag="qkvmm")
            s.op("scalar", lambda e, t=t, pv=pv, vp=cx.vp: e.activation(
                vp[:, t * 520:(t + 1) * 520].rearrange(
                    "p (h c) -> p h c", h=8)[:, :, 0:64],
                pv.rearrange("p (h d) -> p h d", h=8), AF.Copy),
                reads=[bv], writes=[cx.b_vp[t]], tag="vev")
            dd = t % 2
            emit_l2n_rot(cx, pq, bq, rt, b_rt, 0 * NT * 64, 1 * NT * 64, t,
                         qst[:, dd * 512:(dd + 1) * 512], b_qst[dd])
            s.dma("scalar", lambda e, t=t, dd=dd: e.dma_start(
                cx.qtm.ap()[_ts(t, 128), :], qst[:, dd * 512:(dd + 1) * 512]),
                reads=[b_qst[dd]], writes=cx.tb_qtm.row(t), tag="qspl")
            emit_l2n_rot(cx, pk, bkk, rt, b_rt, 2 * NT * 64, 3 * NT * 64, t,
                         kst[:, dd * 512:(dd + 1) * 512], b_kst[dd])
            s.dma("scalar", lambda e, t=t, dd=dd: e.dma_start(
                cx.ktm.ap()[_ts(t, 128), :], kst[:, dd * 512:(dd + 1) * 512]),
                reads=[b_kst[dd]], writes=cx.tb_ktm.row(t), tag="kspl")


def emit_ffn_attn(cx, l):
    s, ar, S = cx.s, cx.ar, cx.S
    NT = S // 128
    NW = S // 512
    qf = [ar.alloc(f"F2q{i}", [PB, S], BF16) for i in range(2)]
    kf = [ar.alloc(f"F2k{i}", [PB, S], BF16) for i in range(2)]
    b_qf = [Buf(), Buf()]
    b_kf = [Buf(), Buf()]
    qa = ar.alloc("F2qa", [PB, S], BF16)
    ka = ar.alloc("F2ka", [PB, S], BF16)
    b_qa = [Buf(f"qa{w}") for w in range(NW * 4)]
    b_qa0 = Buf("qa0")
    b_ka = Buf("ka")
    eT = ar.alloc("F2e", [PB, 8 * 512], BF16)
    b_eT = [Buf(f"F2e{sl}") for sl in range(8)]
    nm = ar.alloc("F2nm", [PB, 4], F32)
    b_nm = [Buf(f"F2nm{i}") for i in range(4)]
    rz = ar.alloc("F2rz", [PB, 2], F32)
    b_rz = Buf()
    s.op("vector", lambda e: e.memset(ka[64:128, :], 0.0), writes=[b_ka],
         tag="kaz")
    s.op("vector", lambda e: e.memset(ka[64:65, :], 1.0), writes=[b_ka],
         tag="kaones")
    s.op("vector", lambda e: e.memset(qa[64:128, :], 0.0), writes=[b_qa0],
         tag="qaz")
    for hp in range(4):
        d = hp % 2
        s.dma("sync", lambda e, hp=hp, d=d: e.dma_start(
            qf[d][:, :], cx.qtm.ap()[:, _ts(hp, 128)], transpose=True),
            reads=[cx.tb_qtm.b(t, hp) for t in range(NT)],
            writes=[b_qf[d]], tag="qfT")
        s.dma("sync", lambda e, hp=hp, d=d: e.dma_start(
            kf[d][:, :], cx.ktm.ap()[:, _ts(hp, 128)], transpose=True),
            reads=[cx.tb_ktm.b(t, hp) for t in range(NT)],
            writes=[b_kf[d]], tag="kfT")
        for h01 in range(2):
            h = 2 * hp + h01
            hs = slice(h01 * 64, h01 * 64 + 64)

            def emit_mrow(qg, sl4):
                pt = cx.PW[3]
                s.op("tensor", lambda e, sl4=sl4: e.transpose(
                    pt[0:1, 512:640], nm[:, sl4:sl4 + 1],
                    cx.c32[:, C_IDENT:C_IDENT + 128]),
                    reads=[b_nm[sl4], cx.b_c32], writes=[bbuf(cx, 7)],
                    tag="F2tr")
                s.op("vector", lambda e, qg=qg: e.tensor_copy(
                    qa[64:65, qg:qg + 128], pt[0:1, 512:640]),
                    reads=[bbuf(cx, 7)], writes=[b_qa[qg // 128]],
                    tag="F2mrow")
            lagq = []
            for w in range(NW):
                for qt in range(4):
                    qg = w * 512 + qt * 128
                    lo = qg - 512
                    loc = max(lo, 0)
                    width = qg + 128 - loc
                    pw = cx.PW[qt % 2]
                    bws = [bbuf(cx, 2 * (qt % 2)), bbuf(cx, 2 * (qt % 2) + 1)]
                    def mmx(e, hs=hs, qg=qg, loc=loc, width=width, pw=pw,
                            d=d):
                        n1 = min(width, 512)
                        ins = e.matmul(pw[:, 0:n1], qf[d][hs, qg:qg + 128],
                                       kf[d][hs, loc:loc + n1], start=True,
                                       stop=True)
                        if width > 512:
                            ins = e.matmul(pw[:, 512:width],
                                           qf[d][hs, qg:qg + 128],
                                           kf[d][hs, loc + 512:loc + width],
                                           start=True, stop=True)
                        return ins
                    s.op("tensor", mmx, reads=[b_qf[d], b_kf[d]], writes=bws,
                         tag="F2mx")
                    if lo >= 0:
                        s.op("vector", lambda e, pw=pw: e.tensor_add(
                            pw[:, 0:128], pw[:, 0:128],
                            cx.c32[:, C_MLEAD:C_MLEAD + 128]),
                            reads=bws + [cx.b_c32], writes=[bws[0]],
                            tag="F2ml")
                    s.op("vector", lambda e, pw=pw, width=width: e.tensor_add(
                        pw[:, width - 128:width], pw[:, width - 128:width],
                        cx.c32[:, C_MTRAIL:C_MTRAIL + 128]),
                        reads=bws + [cx.b_c32],
                        writes=[bws[(width - 1) // 512]], tag="F2mt")
                    sl4 = (w * 4 + qt) % 4
                    s.op("vector", lambda e, pw=pw, width=width, sl4=sl4:
                         e.reduce_max(
                             nm[:, sl4:sl4 + 1], pw[:, 0:width], axis=AX.X,
                             negate=True),
                         reads=bws, writes=[b_nm[sl4]], tag="F2rm")
                    lagq.append((qg, sl4))
                    if len(lagq) > 2:
                        emit_mrow(*lagq.pop(0))
            while lagq:
                emit_mrow(*lagq.pop(0))
            s.op("vector", lambda e, hs=hs, d=d: e.tensor_copy(
                qa[0:64, :], qf[d][hs, :]),
                reads=[b_qf[d]], writes=[b_qa0], tag="F2qa0")
            s.op("vector", lambda e, hs=hs, d=d: e.tensor_copy(
                ka[0:64, :], kf[d][hs, :]),
                reads=[b_kf[d]], writes=[b_ka], tag="F2ka0")
            for w in range(NW):
                kt0 = w * 4 - 4
                for ktg in range(max(0, kt0), w * 4 + 4):
                    slot = ktg - kt0
                    qs_lo = max(0, ktg - w * 4)
                    qs_hi = min(3, ktg - w * 4 + 4)
                    nqs = qs_hi - qs_lo + 1
                    qg0 = w * 512 + qs_lo * 128
                    pe = bank(cx, 4 + (ktg % 2))
                    bpe = bbuf(cx, 4 + (ktg % 2))
                    moffs = []
                    if 0 <= ktg - w * 4 <= 3:
                        moffs.append(((ktg - w * 4 - qs_lo) * 128, 128))
                    if 0 <= ktg - w * 4 + 4 <= 3:
                        moffs.append(((ktg - w * 4 + 4 - qs_lo) * 128, 0))

                    def mme(e, ktg=ktg, qg0=qg0, nqs=nqs, pe=pe,
                            moffs=tuple(moffs)):
                        ins = e.matmul(pe[:, 0:nqs * 128],
                                       ka[:, _ts(ktg, 128)],
                                       qa[:, qg0:qg0 + nqs * 128],
                                       start=True, stop=(not moffs))
                        for i, (off, m0) in enumerate(moffs):
                            ins = e.matmul(pe[:, off:off + 128],
                                           cx.mkb[:, m0:m0 + 128],
                                           cx.mkb[:, 512:640], start=False,
                                           stop=(i == len(moffs) - 1))
                        return ins
                    s.op("tensor", mme,
                         reads=[b_ka, b_qa0, cx.b_mkb] +
                         [b_qa[(qg0 // 128) + i] for i in range(nqs)],
                         writes=[bpe], tag="F2me")
                    s.op("scalar", lambda e, pe=pe, slot=slot, qs_lo=qs_lo,
                         nqs=nqs: e.activation(
                             eT[:, slot * 512 + qs_lo * 128:
                                slot * 512 + qs_lo * 128 + nqs * 128],
                             pe[:, 0:nqs * 128], AF.Exp),
                         reads=[bpe], writes=[b_eT[slot]], tag="F2ex")
                for qs in range(4):
                    qg = w * 512 + qs * 128
                    ktgs = [k for k in range(max(0, w * 4 + qs - 4),
                                             w * 4 + qs + 1)]
                    ab = bank(cx, 6)
                    bab = bbuf(cx, 6)
                    def av(e, qs=qs, w=w, h=h, ktgs=tuple(ktgs), ab=ab,
                           kt0=kt0, vp=cx.vp):
                        ins = None
                        for i, ktg in enumerate(ktgs):
                            slot = ktg - kt0
                            ins = e.matmul(
                                ab[:, 0:65],
                                eT[:, slot * 512 + qs * 128:
                                   slot * 512 + (qs + 1) * 128],
                                vp[:, ktg * 520 + h * 65:
                                   ktg * 520 + (h + 1) * 65],
                                start=(i == 0), stop=(i == len(ktgs) - 1))
                        return ins
                    s.op("tensor", av,
                         reads=[b_eT[k - kt0] for k in ktgs] +
                         [cx.b_vp[k] for k in ktgs],
                         writes=[bab], tag="F2av")
                    s.op("vector", lambda e, ab=ab: e.reciprocal(
                        rz[:, 0:1], ab[:, 64:65]),
                        reads=[bab], writes=[b_rz], tag="F2rz")
                    s.op("vector", lambda e, ab=ab, qg=qg, h=h, of=cx.of:
                         e.tensor_scalar_mul(
                             of[:, (qg // 128) * 512 + h * 64:
                                (qg // 128) * 512 + (h + 1) * 64],
                             ab[:, 0:64], rz[:, 0:1]),
                         reads=[bab, b_rz], writes=[cx.b_of[qg // 128][h]],
                         tag="F2div")


def emit_ffn_wo(cx, l, xf_in_ap, tb_xf_in, xf_mid_ap, tb_xf_mid):
    s, ar, S = cx.s, cx.ar, cx.S
    NT = S // 128
    NB = S // 512
    for t in range(NT):
        s.dma("scalar", lambda e, t=t, of=cx.of: e.dma_start(
            cx.osp.ap()[_ts(t, 128), 0:512], of[:, _ts(t, 512)]),
            reads=cx.b_of[t], writes=[cx.tb_osp.b(t, h) for h in range(8)],
            tag="F3osp")
    wo, b_wo = load_w_fm(cx, cx.w["L0_ffn_wo"].ap()[l], INNER, D, "F3wo")
    ofm = [ar.alloc(f"F3of{i}", [PB, 4 * 512], BF16) for i in range(2)]
    b_ofm = [Buf(), Buf()]
    xt = ar.alloc("F3xt", [PB, 2 * D], F32)
    b_xt = [Buf(), Buf()]
    x1 = ar.alloc("F3x1", [PB, 2 * D], F32)
    b_x1 = [Buf(), Buf()]
    for b in range(NB):
        d = b % 2
        for c in range(4):
            s.dma("sync", lambda e, b=b, c=c, d=d: e.dma_start(
                ofm[d][:, c * 512:(c + 1) * 512],
                cx.osp.ap()[_ts(b, 512), c * 128:(c + 1) * 128],
                transpose=True),
                reads=[cx.tb_osp.b(b * 4 + tt, h) for tt in range(4)
                       for h in range(8)],
                writes=[b_ofm[d]], tag="F3ofT")
        for tt in range(4):
            t = b * 4 + tt
            pr = t % 2
            cols = slice(pr * D, (pr + 1) * D)
            pw = cx.PW[pr]
            bws = [bbuf(cx, 2 * pr), bbuf(cx, 2 * pr + 1)]
            s.dma("sync", lambda e, t=t, cols=cols: e.dma_start(
                xt[:, cols], xf_in_ap[ZR + t * 128:ZR + (t + 1) * 128, :]),
                reads=sp_rs(tb_xf_in, t), writes=[b_xt[pr]], tag="F3ldx")
            def mm(e, tt=tt, d=d, pw=pw):
                ins = None
                for half in range(2):
                    for c in range(4):
                        ins = e.matmul(
                            pw[:, half * 512:(half + 1) * 512],
                            ofm[d][:, c * 512 + tt * 128:
                                   c * 512 + (tt + 1) * 128],
                            wo[:, c * D + half * 512:
                               c * D + (half + 1) * 512],
                            start=(c == 0), stop=(c == 3))
                return ins
            s.op("tensor", mm, reads=[b_ofm[d], b_wo], writes=bws,
                 tag="F3mm")
            s.op("vector", lambda e, cols=cols, pw=pw: e.tensor_add(
                x1[:, cols], pw[:, 0:1024], xt[:, cols]),
                reads=bws + [b_xt[pr]], writes=[b_x1[pr]], tag="F3res")
            s.dma("scalar", lambda e, t=t, cols=cols: e.dma_start(
                xf_mid_ap[ZR + t * 128:ZR + (t + 1) * 128, :], x1[:, cols]),
                reads=[b_x1[pr]], writes=sp_w(tb_xf_mid, t), tag="F3stx")


def emit_bcast_row(cx, src_row_ap, dst, b_dst, ncols, tag="bc"):
    s = cx.s
    src = bass.AP(src_row_ap.tensor, src_row_ap.offset,
                  [[0, PB]] + [list(x) for x in src_row_ap.ap[1:]])
    s.dma("sync", lambda e: e.dma_start(dst[:, 0:ncols], src),
          writes=[b_dst], tag=tag)


def emit_ffn_mlp(cx, l, xf_mid_ap, tb_xf_mid, tail):
    s, ar, S = cx.s, cx.ar, cx.S
    NT = S // 128
    BLK = min(1024, S)
    NBLK = S // BLK
    NCH = (HID + 127) // 128
    lngb = ar.alloc("lngb", [PB, 3 * D], F32)
    b_lngb = Buf()
    emit_bcast_row(cx, cx.w["L0_ffn_ln_g"].ap()[l:l + 1, :],
                   lngb[:, 0:D], b_lngb, D, "bcg")
    emit_bcast_row(cx, cx.w["L0_ffn_ln_b"].ap()[l:l + 1, :],
                   lngb[:, D:2 * D], b_lngb, D, "bcb")
    emit_bcast_row(cx, cx.w["L0_ffn_b2"].ap()[l:l + 1, :],
                   lngb[:, 2 * D:3 * D], b_lngb, D, "bc2")
    b1sb = ar.alloc("b1sb", [PB, 2 * NCH], F32)
    b_b1 = Buf()
    for part in range(2):
        off = part * HID
        s.dma("sync", lambda e, part=part, off=off: e.dma_start(
            b1sb[:, part * NCH:part * NCH + 21],
            cx.w["L0_ffn_b1"].ap()[l, off:off + 21 * 128]
            .rearrange("(c p) -> p c", p=128),
            allow_slow_non_contiguous=True),
            writes=[b_b1], tag="b1ld")
        s.dma("sync", lambda e, part=part, off=off: e.dma_start(
            b1sb[0:42, part * NCH + 21:part * NCH + 22],
            cx.w["L0_ffn_b1"].ap()[l, off + 21 * 128:off + HID]
            .rearrange("(c p) -> p c", p=42),
            allow_slow_non_contiguous=True),
            writes=[b_b1], tag="b1ldt")
    h2st = ar.alloc("h2st", [PB, 2 * D], F32)
    b_h2 = [Buf(), Buf()]
    h2bf = ar.alloc("h2bf", [PB, 2 * D], BF16)
    b_h2bf = [Buf(), Buf()]
    gl = ar.alloc("gl", [PB, 2 * 512], F32)
    b_gl = [Buf(), Buf()]
    x1 = ar.alloc("Mx1", [PB, 2 * D], F32)
    b_x1 = [Buf(), Buf()]
    xfm = ar.alloc("Mxf", [PB, 2 * D], F32)
    b_xfm = [Buf(), Buf()]
    h4 = ar.alloc("h4", [PB, NCH * BLK], BF16)
    b_h4 = [Buf(f"h4{c}") for c in range(NCH)]
    u0 = (ar.base + 63) // 64 * 64
    w2sb = ar.alloc("w2sb", [PB, NCH * D], BF16)
    h2nfm = ar.overlay("h2nfm", [PB, 8 * BLK], BF16, u0)
    w1a = ar.overlay("w1a", [PB, 2 * 8 * 128], BF16, u0 + 8 * BLK * 2)
    w1g = ar.overlay("w1g", [PB, 2 * 8 * 128], BF16,
                     u0 + 8 * BLK * 2 + 2 * 8 * 128 * 2)
    b_h2nfm = Buf()
    b_w1a = [Buf(), Buf()]
    b_w1g = [Buf(), Buf()]
    b_w2 = Buf()
    for B in range(NBLK):
        for tt in range(BLK // 128):
            t = B * (BLK // 128) + tt
            pr = tt % 2
            cols = slice(pr * D, (pr + 1) * D)
            s.dma("sync", lambda e, t=t, cols=cols: e.dma_start(
                h2st[:, cols][:, 0:512],
                xf_mid_ap[ZR + t * 128:ZR + (t + 1) * 128, 0:512]),
                reads=sp_rs(tb_xf_mid, t), writes=[b_h2[pr]], tag="Mh2a")
            s.dma("sync", lambda e, t=t, cols=cols: e.dma_start(
                h2st[:, cols][:, 512:1024],
                xf_mid_ap[ZR - 1 + t * 128:ZR - 1 + (t + 1) * 128, 512:1024]),
                reads=sp_rs(tb_xf_mid, t, 1), writes=[b_h2[pr]], tag="Mh2b")
            emit_ln(cx, h2st, b_h2[pr], cols, h2bf[:, cols], b_h2bf[pr],
                    gamma=lngb[:, 0:D], beta=lngb[:, D:2 * D],
                    b_gb=[b_lngb])
            s.dma("scalar", lambda e, t=t, cols=cols: e.dma_start(
                cx.osp.ap()[_ts(t, 128), :], h2bf[:, cols]),
                reads=[b_h2bf[pr]],
                writes=[cx.tb_osp.b(t, h) for h in range(8)], tag="Mh2sp")
        for c in range(8):
            s.dma("sync", lambda e, B=B, c=c: e.dma_start(
                h2nfm[:, c * BLK:(c + 1) * BLK],
                cx.osp.ap()[B * BLK:(B + 1) * BLK, c * 128:(c + 1) * 128],
                transpose=True),
                reads=[cx.tb_osp.b(B * (BLK // 128) + tt, h)
                       for tt in range(BLK // 128) for h in range(8)],
                writes=[b_h2nfm, b_w2], tag="Mh2T")
        for hc in range(NCH):
            hsz = 128 if hc < NCH - 1 else HID - 128 * (NCH - 1)
            d = hc % 2
            for part, wt, bw in (("a", w1a, b_w1a), ("g", w1g, b_w1g)):
                off = (0 if part == "a" else HID) + hc * 128
                s.dma("gpsimd", lambda e, wt=wt, d=d, off=off, hsz=hsz, l=l:
                      e.dma_start(
                          wt[:, d * 8 * 128:d * 8 * 128 + 8 * hsz]
                          .rearrange("p (c n) -> p c n", c=8),
                          cx.w["L0_ffn_w1"].ap()[l, :, off:off + hsz]
                          .rearrange("(c p) n -> p c n", p=128)),
                      writes=[bw[d], b_w2], tag="Mw1ld")
            for ts in range(BLK // 512):
                pa, pg = bank(cx, 2 * (ts % 2)), bank(cx, 2 * (ts % 2) + 1)
                ba, bg = bbuf(cx, 2 * (ts % 2)), bbuf(cx, 2 * (ts % 2) + 1)
                def mm(e, hsz=hsz, d=d, ts=ts, pa=pa, pg=pg):
                    ins = None
                    for c in range(8):
                        ins = e.matmul(
                            pa[0:hsz, 0:512],
                            w1a[:, d * 8 * 128 + c * hsz:
                                d * 8 * 128 + (c + 1) * hsz],
                            h2nfm[:, c * BLK + ts * 512:
                                  c * BLK + (ts + 1) * 512],
                            start=(c == 0), stop=(c == 7))
                        ins = e.matmul(
                            pg[0:hsz, 0:512],
                            w1g[:, d * 8 * 128 + c * hsz:
                                d * 8 * 128 + (c + 1) * hsz],
                            h2nfm[:, c * BLK + ts * 512:
                                  c * BLK + (ts + 1) * 512],
                            start=(c == 0), stop=(c == 7))
                    return ins
                s.op("tensor", mm, reads=[b_w1a[d], b_w1g[d], b_h2nfm],
                     writes=[ba, bg], tag="Mw1mm")
                gd = ts % 2
                s.op("scalar", lambda e, hsz=hsz, hc=hc, pg=pg, gd=gd:
                     e.activation(gl[0:hsz, gd * 512:(gd + 1) * 512],
                                  pg[0:hsz, 0:512], AF.Gelu,
                                  bias=b1sb[0:hsz, NCH + hc:NCH + hc + 1]),
                     reads=[bg, b_b1], writes=[b_gl[gd]], tag="Mgelu")
                s.op("vector", lambda e, hsz=hsz, hc=hc, ts=ts, pa=pa, gd=gd:
                     e.scalar_tensor_tensor(
                         h4[0:hsz, hc * BLK + ts * 512:
                            hc * BLK + (ts + 1) * 512],
                         pa[0:hsz, 0:512], b1sb[0:hsz, hc:hc + 1],
                         gl[0:hsz, gd * 512:(gd + 1) * 512],
                         op0=ALU.add, op1=ALU.mult),
                     reads=[ba, b_gl[gd], b_b1], writes=[b_h4[hc]],
                     tag="Mh4")
        s.dma("gpsimd", lambda e, l=l: e.dma_start(
            w2sb[:, 0:21 * D].rearrange("p (c n) -> p c n", c=21),
            cx.w["L0_ffn_w2"].ap()[l, 0:21 * 128, :]
            .rearrange("(c p) n -> p c n", p=128)),
            writes=[b_w2, b_h2nfm, b_w1a[0], b_w1a[1], b_w1g[0], b_w1g[1]],
            tag="Mw2ld")
        s.dma("gpsimd", lambda e, l=l: e.dma_start(
            w2sb[0:42, 21 * D:22 * D],
            cx.w["L0_ffn_w2"].ap()[l, 21 * 128:HID, :]),
            writes=[b_w2], tag="Mw2ldt")
        for tt in range(BLK // 128):
            t = B * (BLK // 128) + tt
            pr = tt % 2
            cols = slice(pr * D, (pr + 1) * D)
            pw = cx.PW[2 + pr]
            bws = [bbuf(cx, 4 + 2 * pr), bbuf(cx, 5 + 2 * pr)]
            def mm(e, tt=tt, pw=pw):
                ins = None
                for half in range(2):
                    for hc in range(NCH):
                        hsz = 128 if hc < NCH - 1 else HID - 128 * (NCH - 1)
                        ins = e.matmul(
                            pw[:, half * 512:(half + 1) * 512],
                            h4[0:hsz, hc * BLK + tt * 128:
                               hc * BLK + (tt + 1) * 128],
                            w2sb[0:hsz, hc * D + half * 512:
                                 hc * D + (half + 1) * 512],
                            start=(hc == 0), stop=(hc == NCH - 1))
                return ins
            s.op("tensor", mm, reads=b_h4 + [b_w2], writes=bws, tag="Mw2mm")
            s.dma("sync", lambda e, t=t, cols=cols: e.dma_start(
                xfm[:, cols], xf_mid_ap[ZR + t * 128:ZR + (t + 1) * 128, :]),
                reads=sp_rs(tb_xf_mid, t), writes=[b_xfm[pr]], tag="Mldxf")
            s.op("vector", lambda e, cols=cols, pw=pw: e.tensor_add(
                x1[:, cols], pw[:, 0:1024], lngb[:, 2 * D:3 * D]),
                reads=bws + [b_lngb], writes=[b_x1[pr]], tag="Mb2")
            s.op("vector", lambda e, cols=cols: e.tensor_add(
                x1[:, cols], x1[:, cols], xfm[:, cols]),
                reads=[b_x1[pr], b_xfm[pr]], writes=[b_x1[pr]], tag="Mres")
            tail(t, x1, cols, b_x1[pr])


def emit_ffn_layer(cx, l, xbf_in_ap, tb_xbf_in, xf_in_ap, tb_xf_in,
                   xf_mid_ap, tb_xf_mid, tail):
    ar = cx.ar
    S = cx.S
    NT = S // 128
    ar.mark()
    cx.vp = ar.alloc("vp", [PB, NT * 520], BF16)
    cx.b_vp = [Buf(f"vp{t}") for t in range(NT)]
    cx.of = ar.alloc("of", [PB, NT * 512], BF16)
    cx.b_of = [[Buf(f"of{t},{h}") for h in range(8)] for t in range(NT)]
    ar.mark()
    emit_ffn_qkv(cx, l, xbf_in_ap, tb_xbf_in)
    ar.release()
    ar.mark()
    emit_ffn_attn(cx, l)
    ar.release()
    ar.mark()
    emit_ffn_wo(cx, l, xf_in_ap, tb_xf_in, xf_mid_ap, tb_xf_mid)
    ar.release()
    ar.release()
    ar.mark()
    emit_ffn_mlp(cx, l, xf_mid_ap, tb_xf_mid, tail)
    ar.release()


def make_tail_stream(cx, xf_out_ap, tb_xf_out, xbf_out_ap, tb_xbf_out):
    s = cx.s
    ob = cx.t_ob
    b_ob = cx.bt_ob

    def tail(t, x1, cols, b_x1):
        pr = 0 if cols.start == 0 else 1
        s.dma("scalar", lambda e, t=t, cols=cols: e.dma_start(
            xf_out_ap[ZR + t * 128:ZR + (t + 1) * 128, :], x1[:, cols]),
            reads=[b_x1], writes=sp_w(tb_xf_out, t), tag="Tstf")
        s.op("scalar", lambda e, cols=cols, pr=pr: e.activation(
            ob[:, pr * D:(pr + 1) * D], x1[:, cols], AF.Copy),
            reads=[b_x1], writes=[b_ob[pr]], tag="Tcast")
        s.dma("scalar", lambda e, t=t, pr=pr: e.dma_start(
            xbf_out_ap[ZR + t * 128:ZR + (t + 1) * 128, :],
            ob[:, pr * D:(pr + 1) * D]),
            reads=[b_ob[pr]], writes=sp_w(tb_xbf_out, t), tag="Tstb")
    return tail


def make_tail_sublayer(cx, x_res_ap, tb_x_res, x_out_ap, tb_x_out,
                       xbf_out_ap, tb_xbf_out, xf_out_ap=None,
                       tb_xf_out=None):
    s = cx.s
    xr, b_xr = cx.t_xr, cx.bt_xr
    xo, b_xo = cx.t_xo, cx.bt_xo
    ob, b_ob = cx.t_ob, cx.bt_ob
    of, b_of = cx.t_of, cx.bt_of

    def tail(t, x1, cols, b_x1):
        pr = 0 if cols.start == 0 else 1
        s.dma("sync", lambda e, t=t, pr=pr: e.dma_start(
            xr[:, pr * D:(pr + 1) * D], x_res_ap[_ts(t, 128), :]),
            reads=tb_x_res.r(t * 128, (t + 1) * 128) if tb_x_res else [],
            writes=[b_xr[pr]], tag="Tldr")
        s.op("vector", lambda e, cols=cols, pr=pr: e.tensor_add(
            xo[:, pr * D:(pr + 1) * D], x1[:, cols],
            xr[:, pr * D:(pr + 1) * D]),
            reads=[b_x1, b_xr[pr]], writes=[b_xo[pr]], tag="Tadd")
        s.dma("scalar", lambda e, t=t, pr=pr: e.dma_start(
            x_out_ap[_ts(t, 128), :], xo[:, pr * D:(pr + 1) * D]),
            reads=[b_xo[pr]], writes=tb_x_out.w(t * 128, (t + 1) * 128),
            tag="Tstx")
        if xbf_out_ap is None:
            return
        emit_ln(cx, xo, b_xo[pr], slice(pr * D, (pr + 1) * D),
                ob[:, pr * D:(pr + 1) * D], b_ob[pr],
                (of[:, pr * D:(pr + 1) * D] if xf_out_ap is not None
                 else None),
                (b_of[pr] if xf_out_ap is not None else None))
        s.dma("scalar", lambda e, t=t, pr=pr: e.dma_start(
            xbf_out_ap[ZR + t * 128:ZR + (t + 1) * 128, :],
            ob[:, pr * D:(pr + 1) * D]),
            reads=[b_ob[pr]], writes=sp_w(tb_xbf_out, t), tag="Tstb2")
        if xf_out_ap is not None:
            s.dma("scalar", lambda e, t=t, pr=pr: e.dma_start(
                xf_out_ap[ZR + t * 128:ZR + (t + 1) * 128, :],
                of[:, pr * D:(pr + 1) * D]),
                reads=[b_of[pr]], writes=sp_w(tb_xf_out, t), tag="Tstf2")
    return tail



def emit_attn_b_proj(cx):
    s, ar, S = cx.s, cx.ar, cx.S
    NTW = S // 512
    ar.mark()
    xn, b_xn = load_fm_from_spill(cx, cx.xbf_b.ap(), S, 0, D, "xnB",
                                  reads=cx.tb_xbf_b.r())
    stg = ar.alloc("stgB", [PB, 4 * 512], BF16)
    b_stg = [Buf() for _ in range(4)]
    for slot, wname, scale in ((0, "L0_attn_b_Wq", 0.125),
                               (2, "L0_attn_b_Wk", 1.0)):
        ar.mark()
        wt, b_wt = load_w_fm(cx, cx.w[wname].ap(), D, D, "wB" + wname[-2:])
        acc, b_acc = (cx.cs_q, cx.b_cs_q) if slot == 0 else \
            (cx.cs_k, cx.b_cs_k)
        for n in range(8):
            for t in range(NTW):
                pk = bank(cx, (n + t) % 4)
                bpk = bbuf(cx, (n + t) % 4)
                def mm(e, n=n, t=t, pk=pk, wt=wt):
                    ins = None
                    for c in range(8):
                        ins = e.matmul(
                            pk, wt[:, c * D + n * 128:c * D + (n + 1) * 128],
                            xn[:, c * S + t * 512:c * S + (t + 1) * 512],
                            start=(c == 0), stop=(c == 7))
                    return ins
                s.op("tensor", mm, reads=[b_wt] + b_xn, writes=[bpk],
                     tag="Bmm")
                i = t % 4
                s.op("scalar", lambda e, pk=pk, i=i, scale=scale:
                     e.activation(stg[:, i * 512:(i + 1) * 512], pk, AF.Copy,
                                  scale=scale),
                     reads=[bpk], writes=[b_stg[i]], tag="Bev")
                s.dma("scalar", lambda e, n=n, t=t, i=i, slot=slot:
                      e.dma_start(cx.qsp.ap()[slot, _ts(n, 128), _ts(t, 512)],
                                  stg[:, i * 512:(i + 1) * 512]),
                      reads=[b_stg[i]], writes=[cx.tb_qsp[slot].b(n, t)],
                      tag="Bst")
                emit_cs_stat(cx, pk, bpk, acc, b_acc, n == 0 and t == 0)
        ar.release()
    ar.mark()
    wv, b_wv = load_w_fm(cx, cx.w["L0_attn_b_Wv"].ap(), D, D, "wBv")
    vst = ar.alloc("vstB", [PB, 2 * 1040], BF16)
    b_vst = [Buf(), Buf()]
    s.op("vector", lambda e: e.memset(vst[:, :], 1.0),
         writes=[b_vst[0], b_vst[1]])
    for t in range(S // 128):
        pw = cx.PW[2 + (t % 2)]
        bws = [bbuf(cx, 4 + 2 * (t % 2)), bbuf(cx, 5 + 2 * (t % 2))]
        def mm(e, t=t, pw=pw):
            ins = None
            for half in range(2):
                for c in range(8):
                    ins = e.matmul(
                        pw[:, half * 512:(half + 1) * 512],
                        xn[:, c * S + t * 128:c * S + (t + 1) * 128],
                        wv[:, c * D + half * 512:c * D + (half + 1) * 512],
                        start=(c == 0), stop=(c == 7))
            return ins
        s.op("tensor", mm, reads=[b_wv] + b_xn, writes=bws, tag="BvRmm")
        p = t % 2
        s.op("scalar", lambda e, pw=pw, p=p: e.activation(
            vst[:, p * 1040:(p + 1) * 1040].rearrange(
                "q (h c) -> q h c", h=16)[:, :, 0:64],
            pw[:, 0:1024].rearrange("q (h d) -> q h d", h=16),
            AF.Copy),
            reads=bws, writes=[b_vst[p]], tag="Bvev")
        s.dma("scalar", lambda e, t=t, p=p: e.dma_start(
            cx.vsp.ap().rearrange("g r c -> r g c")[_ts(t, 128)],
            vst[:, p * 1040:(p + 1) * 1040].rearrange("q (g c) -> q g c",
                                                      g=8)),
            reads=[b_vst[p]], writes=cx.tb_vsp.row(t), tag="Bvst")
    emit_negb(cx, cx.cs_q, cx.b_cs_q, cx.cs_k, cx.b_cs_k, 0.1250001,
              cx.negb, cx.b_negb)
    ar.release()
    ar.release()


def emit_attn_b_attn(cx):
    s, ar, S = cx.s, cx.ar, cx.S
    NKT = S // 128
    NQC = S // 512
    ar.mark()
    qq = [ar.alloc(f"B3q{i}", [PB, S], BF16) for i in range(2)]
    kk = ar.alloc("B3k", [PB, S], BF16)
    b_qq = [Buf(), Buf()]
    b_kk = Buf()
    vv = [ar.alloc(f"B3v{i}", [PB, NKT * 130], BF16) for i in range(2)]
    b_vv = [Buf(), Buf()]
    eT = ar.alloc("B3e", [PB, NKT * 1024], BF16)
    b_eT = [Buf(f"B3e{k}") for k in range(NKT)]
    ofst = [ar.alloc(f"B3o{i}", [PB, 512], BF16) for i in range(2)]
    b_ofst = [Buf(), Buf()]
    rzrow = [ar.alloc(f"B3rz{i}", [PB, 512], F32) for i in range(2)]
    b_rzrow = [Buf() for _ in range(2)]
    rzbc = [ar.alloc(f"B3rb{i}", [64, 512], F32) for i in range(2)]
    b_rzbc = [Buf(), Buf()]

    def emit_kt(qc, kt, d, pend, npend):
        par = kt % 3
        pw = cx.PW[(0, 1, 3)[par]]
        bk0 = (0, 2, 6)[par]
        pwb = [bbuf(cx, bk0), bbuf(cx, bk0 + 1)]
        r = kt - 4 * qc
        if r < 0:
            def mmp(e, kt=kt, qc=qc, d=d, pw=pw):
                ins = None
                for h in range(2):
                    hs = slice(h * 64, h * 64 + 64)
                    ins = e.matmul(pw[:, h * 512:(h + 1) * 512],
                                   kk[hs, _ts(kt, 128)],
                                   qq[d][hs, _ts(qc, 512)],
                                   start=True, stop=True)
                return ins
            s.op("tensor", mmp, reads=[b_kk, b_qq[d]], writes=pwb, tag="B3mm")
        else:
            def mmd(e, kt=kt, qc=qc, d=d, pw=pw, r=r):
                ins = None
                for qs in range(r, 4):
                    for h in range(2):
                        hs = slice(h * 64, h * 64 + 64)
                        ins = e.matmul(
                            pw[:, h * 512 + qs * 128:h * 512 + (qs + 1) * 128],
                            kk[hs, _ts(kt, 128)],
                            qq[d][hs, qc * 512 + qs * 128:
                                  qc * 512 + (qs + 1) * 128],
                            start=True, stop=True)
                return ins
            s.op("tensor", mmd, reads=[b_kk, b_qq[d]], writes=pwb, tag="B3md")
            for h in range(2):
                s.op("vector", lambda e, h=h, r=r, pw=pw: e.tensor_add(
                    pw[:, h * 512 + r * 128:h * 512 + (r + 1) * 128],
                    pw[:, h * 512 + r * 128:h * 512 + (r + 1) * 128],
                    cx.c32[:, C_MTFUT:C_MTFUT + 128]),
                    reads=pwb + [cx.b_c32], writes=[pwb[h]], tag="B3msk")
        if pend is not None and kt < npend:
            emit_av(pend, kt, d)
        if r < 0:
            s.op("scalar", lambda e, kt=kt, pw=pw: e.activation(
                eT[:, _ts(kt, 1024)], pw[:, 0:1024], AF.Exp,
                bias=cx.negb[:, 0:1]),
                reads=pwb + [cx.b_negb], writes=[b_eT[kt]], tag="B3exp")
        else:
            for h in range(2):
                s.op("scalar", lambda e, h=h, kt=kt, r=r, pw=pw: e.activation(
                    eT[:, kt * 1024 + h * 512 + r * 128:
                       kt * 1024 + (h + 1) * 512],
                    pw[:, h * 512 + r * 128:h * 512 + 512],
                    AF.Exp, bias=cx.negb[:, 0:1]),
                    reads=pwb + [cx.b_negb], writes=[b_eT[kt]], tag="B3exd")

    def emit_av(qcp, kt, d):
        n = 4 * qcp + 4
        r = kt - 4 * qcp
        cc = max(r, 0) * 128
        for h in range(2):
            ab = cx.PW[2][:, h * 512:(h + 1) * 512]
            s.op("tensor", lambda e, h=h, d=d, kt=kt, ab=ab, cc=cc, n=n:
                 e.matmul(
                     ab[0:65, cc:512],
                     vv[d][:, kt * 130 + h * 65:kt * 130 + (h + 1) * 65],
                     eT[:, kt * 1024 + h * 512 + cc:
                        kt * 1024 + (h + 1) * 512],
                     start=(kt == 0), stop=(kt == n - 1)),
                 reads=[b_eT[kt], b_vv[d]], writes=[bbuf(cx, 4 + h)],
                 tag="B3av")

    def emit_div(p, qc):
        for h in range(2):
            ab = cx.PW[2][:, h * 512:(h + 1) * 512]
            dpar = (qc * 2 + h) % 4
            emit_div_evac(cx, p, qc, h, ab, bbuf(cx, 4 + h), ofst[qc % 2],
                          b_ofst[qc % 2], rzrow[dpar % 2], b_rzrow[dpar % 2],
                          rzbc[dpar % 2], b_rzbc[dpar % 2], "B3")
        s.dma("scalar", lambda e, p=p, qc=qc: e.dma_start(
            cx.ospf.ap()[_ts(p, 128), _ts(qc, 512)], ofst[qc % 2][:, :]),
            reads=[b_ofst[qc % 2]], writes=[cx.tb_ospf.b(p, qc)], tag="B3ost")

    for p in range(8):
        d = p % 2
        s.dma("sync", lambda e, p=p, d=d: e.dma_start(
            qq[d][:, :], cx.qsp.ap()[0, _ts(p, 128), :]),
            reads=[cx.tb_qsp[0].row(p)], writes=[b_qq[d]], tag="B3ldq")
        s.dma("sync", lambda e, p=p: e.dma_start(
            kk[:, :], cx.qsp.ap()[2, _ts(p, 128), :]),
            reads=[cx.tb_qsp[2].row(p)], writes=[b_kk], tag="B3ldk")
        s.dma("gpsimd", lambda e, p=p, d=d: e.dma_start(
            vv[d][:, :].rearrange("q (t c) -> q t c", c=130),
            cx.vsp.ap()[p].rearrange("(t q) c -> q t c", q=128)),
            reads=[b.b(t, p) for t in range(NKT) for b in [cx.tb_vsp]],
            writes=[b_vv[d]], tag="B3ldv")
        pend = None
        for qc in range(NQC):
            npend = 4 * pend + 4 if pend is not None else 0
            for kt in range(4 * qc + 4):
                emit_kt(qc, kt, d, pend, npend)
            if pend is not None:
                emit_div(p, pend)
            pend = qc
        for kt in range(4 * pend + 4):
            emit_av(pend, kt, d)
        emit_div(p, pend)
    ar.release()


_PROGRAM_CACHE = {}


def kernel(**inputs):
    S = 4096
    B = int(inputs["L0_attn_a_x"].shape[0])
    if S not in _PROGRAM_CACHE:
        _PROGRAM_CACHE[S] = (build_program(S), build_consts(S))
    cx, (cbf, c32, rot) = _PROGRAM_CACHE[S]
    x = np.ascontiguousarray(np.asarray(inputs["L0_attn_a_x"], np.float32))
    base = {n: np.ascontiguousarray(np.asarray(inputs[n], np.float32))
            for n in W_NAMES if n != "L0_attn_a_x"}
    base["CBF"] = cbf
    base["C32"] = c32
    base["ROT"] = rot
    in_maps = []
    for b in range(B):
        m = dict(base)
        m["L0_attn_a_x"] = x[b]
        in_maps.append(m)
    res = run_bass_kernel_spmd(cx.nc, in_maps, core_ids=list(range(B)))
    return np.stack([np.asarray(r["out"], np.float32)
                     for r in res.results], axis=0)


# version 18
# speedup vs baseline: 1.1751x; 1.1751x over previous
import numpy as np
import ml_dtypes
import concourse.bass as bass
import concourse.mybir as mybir
from concourse.bass_utils import run_bass_kernel_spmd

F32 = mybir.dt.float32
BF16 = mybir.dt.bfloat16
AF = mybir.ActivationFunctionType
ALU = mybir.AluOpType
AX = mybir.AxisListType

ENGINES = ("tensor", "vector", "scalar", "gpsimd", "sync")
N_DMA_SEMS = 4



class Op:
    __slots__ = ("engine", "fn", "deps", "idx", "is_dma", "dma_sem_i",
                 "dma_val", "needs_inc", "inc_val", "tag", "dma_prewait")

    def __init__(self, engine, fn, deps, idx, is_dma, tag=""):
        self.engine = engine
        self.fn = fn
        self.deps = deps
        self.idx = idx
        self.is_dma = is_dma
        self.dma_sem_i = -1
        self.dma_val = 0
        self.needs_inc = False
        self.inc_val = 0
        self.tag = tag
        self.dma_prewait = 0


class Buf:

    __slots__ = ("name", "last_w", "readers")

    def __init__(self, name=""):
        self.name = name
        self.last_w = None
        self.readers = []


class TBuf:

    def __init__(self, name, n_rows, gran):
        self.gran = gran
        self.n = (n_rows + gran - 1) // gran
        self.bufs = [Buf(f"{name}[{i}]") for i in range(self.n)]

    def r(self, r0=None, r1=None):
        if r0 is None:
            return self.bufs
        return self.bufs[r0 // self.gran:(r1 + self.gran - 1) // self.gran]

    w = r


class Sched:
    def __init__(self, nc, drains=False):
        self.nc = nc
        self.drains = drains
        self.ops = []
        self.per_engine = {e: [] for e in ENGINES}
        self.dma_counts = {e: [0] * N_DMA_SEMS for e in ENGINES}
        self.dma_rr = {e: 0 for e in ENGINES}
        self.fence_deps = []
        self.last_dma = {e: [None] * N_DMA_SEMS for e in ENGINES}

    def fence(self):
        deps = []
        for e in ENGINES:
            ops = self.per_engine[e]
            if ops and ops[-1].tag != "fence":
                deps.append(self.op(e, lambda eng: eng.drain(), tag="fence"))
            elif ops:
                deps.append(ops[-1])
            for o in self.last_dma[e]:
                if o is not None:
                    deps.append(o)
        self.fence_deps = deps

    def op(self, engine, fn, reads=(), writes=(), tag=""):
        return self._rec(engine, fn, reads, writes, False, tag)

    def dma(self, engine, fn, reads=(), writes=(), tag=""):
        return self._rec(engine, fn, reads, writes, True, tag)

    def _rec(self, engine, fn, reads, writes, is_dma, tag):
        deps = []
        seen = set()

        def add(d):
            if d is not None and id(d) not in seen:
                seen.add(id(d))
                deps.append(d)

        def flat(bs):
            out = []
            for b in bs:
                if isinstance(b, (list, tuple)):
                    out.extend(b)
                else:
                    out.append(b)
            return out

        reads = flat(reads)
        writes = flat(writes)
        for b in reads:
            add(b.last_w)
        for b in writes:
            add(b.last_w)
            for rd in b.readers:
                add(rd)
        for d in self.fence_deps:
            add(d)
        o = Op(engine, fn, deps, len(self.per_engine[engine]), is_dma, tag)
        if is_dma:
            i = self.dma_rr[engine] % N_DMA_SEMS
            self.dma_rr[engine] += 1
            self.dma_counts[engine][i] += 1
            o.dma_sem_i = i
            o.dma_val = 16 * self.dma_counts[engine][i]
            o.dma_prewait = o.dma_val - 16 if o.dma_val > 16 else 0
            self.last_dma[engine][i] = o
        self.ops.append(o)
        self.per_engine[engine].append(o)
        for b in reads:
            b.readers.append(o)
        for b in writes:
            b.last_w = o
            b.readers = []
        return o

    def finalize(self):
        nc = self.nc
        for o in self.ops:
            for d in o.deps:
                if not d.is_dma:
                    d.needs_inc = True
        eng_sems = {}
        dma_sems = {}
        for e in ENGINES:
            cnt = 0
            for o in self.per_engine[e]:
                if o.needs_inc and not o.is_dma:
                    cnt += 1
                    o.inc_val = cnt
            if cnt > 0:
                eng_sems[e] = nc.alloc_semaphore(name=f"esem_{e}")
            if self.dma_rr[e] > 0:
                dma_sems[e] = [nc.alloc_semaphore(name=f"dsem_{e}{i}")
                               for i in range(N_DMA_SEMS)]
        self.eng_sems = eng_sems
        self.dma_sems = dma_sems

        def engine_body(e):
            def body(eng):
                watermark = {}
                drained_upto = -1
                for o in self.per_engine[e]:
                    waits = {}
                    for d in o.deps:
                        if d.is_dma:
                            sem = dma_sems[d.engine][d.dma_sem_i]
                            val = d.dma_val
                        else:
                            sem = eng_sems[d.engine]
                            val = d.inc_val
                        k = (sem.name, sem.num)
                        if val > waits.get(k, (0, None))[0]:
                            waits[k] = (val, sem)
                    if o.is_dma and o.dma_prewait:
                        sem = dma_sems[e][o.dma_sem_i]
                        k = (sem.name, sem.num)
                        if o.dma_prewait > waits.get(k, (0, None))[0]:
                            waits[k] = (o.dma_prewait, sem)
                    for k, (val, sem) in waits.items():
                        if watermark.get(k, 0) >= val:
                            continue
                        eng.wait_ge(sem, val)
                        watermark[k] = val
                    ins = o.fn(eng)
                    if isinstance(ins, (list, tuple)):
                        ins = ins[-1]
                    if o.is_dma:
                        ins.then_inc(dma_sems[e][o.dma_sem_i], 16)
                    elif o.needs_inc:
                        ins.then_inc(eng_sems[e], 1)
                for i, cnt in enumerate(self.dma_counts[e]):
                    if cnt > 0:
                        eng.wait_ge(dma_sems[e][i], 16 * cnt)
            return body

        with nc.Block() as block:
            for e in ENGINES:
                if self.per_engine[e]:
                    getattr(block, e)(engine_body(e))

    def n_ops(self):
        return {e: len(v) for e, v in self.per_engine.items() if v}


class Arena:

    BASE = 32768

    def __init__(self, nc, sched=None):
        self.nc = nc
        self.sched = sched
        self.base = self.BASE
        self.top = nc.SBUF_PARTITION_SIZE_BYTES
        self.marks = []
        self.peak = self.BASE
        self.cnt = 0

    def alloc(self, name, shape, dtype, align=64):
        nbytes = int(np.prod(shape[1:])) * mybir.dt.size(dtype)
        off = (self.base + align - 1) // align * align
        self.cnt += 1
        t = self.nc.alloc_sbuf_tensor_at(f"{name}_{self.cnt}", list(shape),
                                         dtype, offset=off)
        self.base = off + nbytes
        self.peak = max(self.peak, self.base)
        assert self.base <= self.top, (
            f"SBUF arena overflow at {name}: {self.base} > {self.top}")
        return t

    def overlay(self, name, shape, dtype, at):
        nbytes = int(np.prod(shape[1:])) * mybir.dt.size(dtype)
        assert at + nbytes <= self.top
        self.cnt += 1
        return self.nc.alloc_sbuf_tensor_at(f"{name}_{self.cnt}", list(shape),
                                            dtype, offset=at)

    def mark(self):
        self.marks.append(self.base)

    def release(self):
        self.base = self.marks.pop()
        if self.sched is not None:
            self.sched.fence()



D = 1024
NH_A = 16
HD = 64
DL = 256
LAM = 0.01 * 0.01
QSC_A = 1.0 / (np.sqrt(1024.0) + 1e-8)
NH_F = 8
INNER = 512
WIN = 512
HID = 2730
QK_SCALE = 8.0
NEG = -1e9
EPS_LN = 1e-5

PB = 128
ZR = 16


def _xpos_tables(S):
    hi = np.arange(0, HD, 2, dtype=np.float64)
    inv = 1.0 / 10000.0 ** (hi / HD)
    t = np.arange(S, dtype=np.float64)
    fr = t[:, None] * inv[None, :]
    fr = np.concatenate([fr, fr], axis=-1)
    base = (hi + 0.4 * HD) / (1.4 * HD)
    power = (t - S // 2) / float(WIN // 2)
    sc = base[None, :] ** power[:, None]
    sc = np.concatenate([sc, sc], axis=-1)
    cos, sin = np.cos(fr), np.sin(fr)
    sign = np.ones(HD)
    sign[:HD // 2] = -1.0
    cs_q = QK_SCALE * cos * sc
    sn_q = QK_SCALE * sin * sc * sign[None, :]
    cs_k = cos / sc
    sn_k = sin / sc * sign[None, :]
    return [a.astype(np.float32) for a in (cs_q, sn_q, cs_k, sn_k)]


def build_consts(S):
    t = np.arange(S, dtype=np.float64)
    ebf = {}
    ebf["enq"] = np.exp(-LAM * t) * QSC_A
    ebf["epq"] = np.exp(+LAM * t) * QSC_A
    ebf["enk"] = np.exp(-LAM * t)
    ebf["epk"] = np.exp(+LAM * t)
    cbf = np.concatenate(
        [np.tile(v.astype(ml_dtypes.bfloat16)[None, :], (PB, 1))
         for v in (ebf["enq"], ebf["epq"], ebf["enk"], ebf["epk"])],
        axis=1)

    kk = np.arange(128)[:, None]
    qq5 = np.arange(512)[None, :]
    corr = []
    for r in range(4):
        d = qq5 - kk - 128 * r
        corr.append(np.where(d >= 0, 1.0, np.exp(2 * LAM * d)))
    corr = np.concatenate(corr, axis=1).astype(np.float32)

    q1 = np.arange(128)[None, :]
    mask_lead = np.where(q1.T > q1, NEG, 0.0)
    mask_trail = np.where(q1.T < q1, NEG, 0.0)
    maskT_old = np.where(q1 > q1.T, NEG, 0.0)
    maskT_fut = np.where(q1 < q1.T, NEG, 0.0)
    ident = np.eye(128)
    c32 = np.concatenate(
        [corr, mask_lead, mask_trail, maskT_old, maskT_fut, ident],
        axis=1).astype(np.float32)

    sh_old = np.concatenate([maskT_old[64:128], maskT_old[0:64]], axis=0)
    sh_fut = np.concatenate([maskT_fut[64:128], maskT_fut[0:64]], axis=0)
    mb = np.concatenate([mask_lead, mask_trail, maskT_old, maskT_fut, ident,
                         sh_old, sh_fut],
                        axis=1).astype(ml_dtypes.bfloat16)
    cbf = np.concatenate([cbf, mb], axis=1)

    rot = np.stack(_xpos_tables(S), axis=0)
    return cbf, c32, rot


C_CORR = 0
C_MLEAD = 2048
C_MTRAIL = 2048 + 128
C_MTOLD = 2048 + 256
C_MTFUT = 2048 + 384
C_IDENT = 2048 + 512
C32_COLS = 2048 + 640


class Ctx:
    pass


def _ts(i, n):
    return slice(i * n, (i + 1) * n)



def load_w_fm(cx, src_ap, K, N, name, out_dtype=BF16):
    s, ar = cx.s, cx.ar
    nch = K // 128
    t = ar.alloc(name, [PB, nch * N], out_dtype)
    b = Buf(name)
    re = src_ap.rearrange("(c p) n -> p c n", p=128)
    s.dma("gpsimd", lambda e, t=t, re=re, nch=nch, N=N: e.dma_start(
        t[:, :].rearrange("p (c n) -> p c n", c=nch), re),
        writes=[b], tag=f"ldw_{name}")
    return t, b


def mm_fm_tm(cx, psum_ap, w_t, w_cols, act_t, act_cols, nch, N,
             start=True, stop=True):
    def fn(e):
        ins = None
        for c in range(nch):
            ins = e.matmul(psum_ap,
                           act_t[:, act_cols(c)],
                           w_t[:, w_cols(c)][:, 0:N] if callable(w_cols)
                           else w_t[:, w_cols][:, 0:N],
                           start=(start and c == 0),
                           stop=(stop and c == nch - 1))
        return ins
    return fn


def emit_ln(cx, xt, bx, cols, out_bf, b_obf, out_f32=None, b_of32=None,
            gamma=None, beta=None, b_gb=()):
    s = cx.s
    n = cols.stop - cols.start
    pr = 0 if cols.start == 0 else 1
    st = cx.ln_scr
    bs = cx.b_lnp[pr]
    sq = cx.fsc
    bsq = cx.b_fscp[pr]
    c = pr * 8
    sqc = slice(pr * n, (pr + 1) * n)
    s.op("vector", lambda e: e.reduce_sum(st[:, c:c + 1], xt[:, cols],
                                          axis=AX.X),
         reads=[bx], writes=[bs])
    s.op("scalar", lambda e: e.activation(sq[:, sqc], xt[:, cols], AF.Square),
         reads=[bx], writes=[bsq])
    s.op("vector", lambda e: e.reduce_sum(st[:, c + 1:c + 2], sq[:, sqc],
                                          axis=AX.X),
         reads=[bsq], writes=[bs])
    s.op("vector", lambda e: e.tensor_scalar_mul(st[:, c + 2:c + 3],
                                                 st[:, c:c + 1], -1.0 / n),
         reads=[bs], writes=[bs])
    s.op("vector", lambda e: e.tensor_mul(st[:, c + 3:c + 4],
                                          st[:, c + 2:c + 3],
                                          st[:, c + 2:c + 3]),
         reads=[bs], writes=[bs])
    s.op("vector", lambda e: e.scalar_tensor_tensor(
        st[:, c + 4:c + 5], st[:, c + 1:c + 2], 1.0 / n,
        st[:, c + 3:c + 4], op0=ALU.mult, op1=ALU.subtract),
        reads=[bs], writes=[bs])
    s.op("scalar", lambda e: e.activation(st[:, c + 5:c + 6],
                                          st[:, c + 4:c + 5], AF.Sqrt,
                                          bias=cx.kc[:, 0:1]),
         reads=[bs, cx.b_kc], writes=[bs])
    s.op("vector", lambda e: e.reciprocal(st[:, c + 6:c + 7],
                                          st[:, c + 5:c + 6]),
         reads=[bs], writes=[bs])
    if gamma is None:
        if out_f32 is not None:
            s.op("vector", lambda e: e.tensor_scalar(
                out_f32, xt[:, cols], st[:, c + 2:c + 3], st[:, c + 6:c + 7],
                op0=ALU.add, op1=ALU.mult),
                reads=[bx, bs], writes=[b_of32])
            s.op("scalar", lambda e: e.activation(out_bf, out_f32, AF.Copy),
                 reads=[b_of32], writes=[b_obf])
        else:
            s.op("vector", lambda e: e.tensor_scalar(
                out_bf, xt[:, cols], st[:, c + 2:c + 3], st[:, c + 6:c + 7],
                op0=ALU.add, op1=ALU.mult),
                reads=[bx, bs], writes=[b_obf])
    else:
        s.op("vector", lambda e: e.tensor_scalar(
            sq[:, sqc], xt[:, cols], st[:, c + 2:c + 3], st[:, c + 6:c + 7],
            op0=ALU.add, op1=ALU.mult),
            reads=[bx, bs], writes=[bsq])
        s.op("vector", lambda e: e.tensor_mul(sq[:, sqc], sq[:, sqc], gamma),
             reads=[bsq] + list(b_gb), writes=[bsq])
        s.op("vector", lambda e: e.tensor_add(out_bf, sq[:, sqc], beta),
             reads=[bsq] + list(b_gb), writes=[b_obf])


def emit_ln_pass(cx, src_ap, src_tb, dst_bf_ap, dst_bf_tb, dst_f32_ap=None,
                 dst_f32_tb=None, tag="ln"):
    s, ar, S = cx.s, cx.ar, cx.S
    NT = S // 128
    ar.mark()
    xt = ar.alloc("lnx", [PB, 2 * D], F32)
    ob = ar.alloc("lnob", [PB, 2 * D], BF16)
    of = ar.alloc("lnof", [PB, 2 * D], F32) if dst_f32_ap is not None else None
    bx = [Buf(), Buf()]
    bo = [Buf(), Buf()]
    bf32 = [Buf(), Buf()]
    for t in range(NT):
        p = t % 2
        cols = slice(p * D, (p + 1) * D)
        s.dma("sync", lambda e, t=t, cols=cols: e.dma_start(
            xt[:, cols], src_ap[_ts(t, 128), :]),
            reads=[src_tb.r(t * 128, t * 128 + 128)] if src_tb else [],
            writes=[bx[p]], tag=tag + "ld")
        emit_ln(cx, xt, bx[p], cols,
                ob[:, cols], bo[p],
                (of[:, cols] if of is not None else None),
                (bf32[p] if of is not None else None))
        s.dma("sync", lambda e, t=t, cols=cols: e.dma_start(
            dst_bf_ap[ZR + t * 128:ZR + (t + 1) * 128, :], ob[:, cols]),
            reads=[bo[p]], writes=sp_w(dst_bf_tb, t),
            tag=tag + "st")
        if of is not None:
            s.dma("sync", lambda e, t=t, cols=cols: e.dma_start(
                dst_f32_ap[ZR + t * 128:ZR + (t + 1) * 128, :], of[:, cols]),
                reads=[bf32[p]],
                writes=sp_w(dst_f32_tb, t),
                tag=tag + "stf")
    ar.release()


def emit_zero_rows(cx):
    s, ar = cx.s, cx.ar
    ar.mark()
    zb = ar.alloc("zrow_b", [ZR, D], BF16)
    zf = ar.alloc("zrow_f", [ZR, D], F32)
    b1, b2 = Buf(), Buf()
    s.op("vector", lambda e: e.memset(zb[:, :], 0.0), writes=[b1])
    s.op("vector", lambda e: e.memset(zf[:, :], 0.0), writes=[b2])
    for name in ("xbf_a", "xbf_b"):
        d = getattr(cx, name)
        s.dma("sync", lambda e, d=d: e.dma_start(d.ap()[0:ZR, :], zb[:, :]),
              reads=[b1], writes=getattr(cx, "tb_" + name).r(0, 128))
    for name in ("xf_a", "xf_b"):
        d = getattr(cx, name)
        s.dma("sync", lambda e, d=d: e.dma_start(d.ap()[0:ZR, :], zf[:, :]),
              reads=[b2], writes=getattr(cx, "tb_" + name).r(0, 128))
    ar.release()


def load_fm_from_spill(cx, spill_ap, rows, col0, ncols, name, reads,
                       shift=0, eng="sync", ar=None):
    s = cx.s
    ar = ar or cx.ar
    nch = ncols // 128
    t = ar.alloc(name, [PB, nch * rows], BF16)
    bufs = [Buf(f"{name}{c}") for c in range(nch)]
    for c in range(nch):
        src = spill_ap[ZR - shift:ZR - shift + rows,
                       col0 + c * 128:col0 + (c + 1) * 128]
        s.dma(eng, lambda e, src=src, c=c: e.dma_start(
            t[:, c * rows:(c + 1) * rows], src, transpose=True),
            reads=reads, writes=[bufs[c]], tag=f"T_{name}")
    return t, bufs


class TBuf2:

    def __init__(self, name, nr, nc):
        self.bufs = [[Buf(f"{name}[{i},{j}]") for j in range(nc)]
                     for i in range(nr)]

    def b(self, i, j):
        return self.bufs[i][j]

    def row(self, i):
        return self.bufs[i]

    def all(self):
        return [b for r in self.bufs for b in r]


def bank(cx, i):
    return cx.PW[i // 2][:, (i % 2) * 512:(i % 2 + 1) * 512]


def bbuf(cx, i):
    return cx.b_pw[i]


def emit_cs_stat(cx, psum_ap, b_ps, acc, b_acc, first):
    s = cx.s
    sq = cx.sq_st
    bsq = cx.b_sq_st
    stb = bank(cx, 7)
    bst = bbuf(cx, 7)
    s.op("scalar", lambda e: e.activation(sq[:, :], psum_ap, AF.Square),
         reads=[b_ps], writes=[bsq])
    s.op("tensor", lambda e: e.matmul(stb[0:1, 0:512], cx.ones128[:, 0:1],
                                      sq[:, :], start=True, stop=True),
         reads=[bsq, cx.b_ones128], writes=[bst])
    s.op("vector", lambda e: e.reduce_max(cx.st8[0:1, 0:1], stb[0:1, 0:512],
                                          axis=AX.X),
         reads=[bst], writes=[cx.b_st8])
    if first:
        s.op("vector", lambda e: e.tensor_copy(acc[0:1, 0:1], cx.st8[0:1, 0:1]),
             reads=[cx.b_st8], writes=[b_acc])
    else:
        s.op("vector", lambda e: e.tensor_max(acc[0:1, 0:1], acc[0:1, 0:1],
                                              cx.st8[0:1, 0:1]),
             reads=[cx.b_st8, b_acc], writes=[b_acc])


def emit_negb(cx, acc_q, b_q, acc_k, b_k, margin, dst, b_dst):
    s = cx.s
    st, bst = cx.st8, cx.b_st8
    s.op("scalar", lambda e: e.activation(st[0:1, 1:2], acc_q[0:1, 0:1],
                                          AF.Sqrt),
         reads=[b_q], writes=[bst])
    s.op("scalar", lambda e: e.activation(st[0:1, 2:3], acc_k[0:1, 0:1],
                                          AF.Sqrt),
         reads=[b_k], writes=[bst])
    s.op("vector", lambda e: e.tensor_mul(st[0:1, 3:4], st[0:1, 1:2],
                                          st[0:1, 2:3]),
         reads=[bst], writes=[bst])
    s.op("scalar", lambda e: e.activation(st[0:1, 4:5], st[0:1, 3:4],
                                          AF.Copy, scale=-margin),
         reads=[bst], writes=[bst])
    s.op("vector", lambda e: e.tensor_copy(cx.st8bf[0:1, 0:1], st[0:1, 4:5]),
         reads=[bst], writes=[cx.b_st8bf])
    pb = bank(cx, 7)
    s.op("tensor", lambda e: e.matmul(pb[:, 0:1], cx.ones128[0:1, :],
                                      cx.st8bf[0:1, 0:1], start=True,
                                      stop=True),
         reads=[cx.b_st8bf, cx.b_ones128], writes=[bbuf(cx, 7)])
    s.op("vector", lambda e: e.tensor_copy(dst[:, 0:1], pb[:, 0:1]),
         reads=[bbuf(cx, 7)], writes=[b_dst])


def emit_attn_a_proj(cx):
    s, ar, S, nc = cx.s, cx.ar, cx.S, cx.nc
    NTW = S // 512
    ar.mark()
    stg = ar.alloc("stgA", [PB, 4 * 512], BF16)
    b_stg = [Buf() for _ in range(4)]

    def proj_pass(w_t, b_w, act_t, b_act, nch, A, tag, n_out=8):
        for n in range(n_out):
            for t in range(NTW):
                pk = bank(cx, (n + t) % 4)
                bpk = bbuf(cx, (n + t) % 4)
                def mm(e, n=n, t=t, pk=pk, w_t=w_t, act_t=act_t, nch=nch):
                    ins = None
                    for c in range(nch):
                        ins = e.matmul(
                            pk, w_t[:, c * n_out * 128 + n * 128:
                                    c * n_out * 128 + (n + 1) * 128],
                            act_t[:, c * S + t * 512:c * S + (t + 1) * 512],
                            start=(c == 0), stop=(c == nch - 1))
                    return ins
                s.op("tensor", mm, reads=[b_w] + b_act, writes=[bpk],
                     tag=tag)
                A(n, t, pk, bpk)
    ar.mark()
    xn, b_xn = load_fm_from_spill(cx, cx.xbf_a.ap(), S, 0, D, "xnA",
                                  reads=cx.tb_xbf_a.r())
    ar.mark()
    cx.ebf = ar.alloc("ebfq", [PB, 2 * S], BF16)
    cx.b_ebf = Buf("ebfq")
    s.dma("sync", lambda e, t_=cx.ebf: e.dma_start(t_[:, :],
                                                   cx.d_cbf.ap()[:, 0:2 * S]),
          writes=[cx.b_ebf], tag="ldebf")
    wq, b_wq = load_w_fm(cx, cx.w["L0_attn_a_Wq"].ap(), D, D, "wqA")

    def q_evac(n, t, pk, bpk):
        i0, i1 = (t % 2) * 2, (t % 2) * 2 + 1
        s.op("vector", lambda e, t=t, pk=pk, i0=i0, eb=cx.ebf: e.tensor_mul(
            stg[:, i0 * 512:(i0 + 1) * 512], pk,
            eb[:, 0 * S + t * 512:0 * S + (t + 1) * 512]),
            reads=[bpk, cx.b_ebf], writes=[b_stg[i0]], tag="qsc-")
        s.op("vector", lambda e, t=t, pk=pk, i1=i1, eb=cx.ebf: e.tensor_mul(
            stg[:, i1 * 512:(i1 + 1) * 512], pk,
            eb[:, 1 * S + t * 512:1 * S + (t + 1) * 512]),
            reads=[bpk, cx.b_ebf], writes=[b_stg[i1]], tag="qsc+")
        s.dma("scalar", lambda e, n=n, t=t, i0=i0: e.dma_start(
            cx.qsp.ap()[0, _ts(n, 128), _ts(t, 512)],
            stg[:, i0 * 512:(i0 + 1) * 512]),
            reads=[b_stg[i0]], writes=[cx.tb_qsp[0].b(n, t)], tag="qst-")
        s.dma("scalar", lambda e, n=n, t=t, i1=i1: e.dma_start(
            cx.qsp.ap()[1, _ts(n, 128), _ts(t, 512)],
            stg[:, i1 * 512:(i1 + 1) * 512]),
            reads=[b_stg[i1]], writes=[cx.tb_qsp[1].b(n, t)], tag="qst+")
        emit_cs_stat(cx, pk, bpk, cx.cs_q, cx.b_cs_q, n == 0 and t == 0)
    proj_pass(wq, b_wq, xn, b_xn, 8, q_evac, "qmm")
    ar.release()
    ar.mark()
    wkv = ar.alloc("wkv", [PB, 8 * D], BF16)
    b_wkv = Buf("wkv")
    re = cx.w["L0_attn_a_Wk"].ap().rearrange("(c p) n -> p c n", p=128)
    s.dma("gpsimd", lambda e, re=re: e.dma_start(
        wkv[:, :].rearrange("p (c n) -> p c n", c=8), re),
        writes=[b_wkv], tag="ldwk")
    ar.mark()
    wv, b_wv = load_w_fm(cx, cx.w["L0_attn_a_Wv"].ap(), D, D, "wvA")
    s.op("vector", lambda e: e.tensor_add(wkv[:, :], wkv[:, :], wv[:, :]),
         reads=[b_wkv, b_wv], writes=[b_wkv])
    ar.release()

    def kv_evac(n, t, pk, bpk):
        i = t % 4
        s.op("scalar", lambda e, pk=pk, i=i: e.activation(
            stg[:, i * 512:(i + 1) * 512], pk, AF.Copy),
            reads=[bpk], writes=[b_stg[i]], tag="kvev")
        s.dma("scalar", lambda e, n=n, t=t, i=i: e.dma_start(
            cx.kvsp.ap()[_ts(n, 128), _ts(t, 512)],
            stg[:, i * 512:(i + 1) * 512]),
            reads=[b_stg[i]], writes=[cx.tb_kvsp.b(n, t)], tag="kvst")
    proj_pass(wkv, b_wkv, xn, b_xn, 8, kv_evac, "kvmm")
    ar.release()
    ar.release()
    lat = ar.alloc("latA", [PB, 2 * S], BF16)
    b_lat = [Buf(), Buf()]
    ar.mark()
    kv = ar.alloc("kvA", [PB, 8 * S], BF16)
    b_kv = [Buf() for _ in range(8)]
    for n in range(8):
        s.dma("sync", lambda e, n=n: e.dma_start(
            kv[:, n * S:(n + 1) * S], cx.kvsp.ap()[_ts(n, 128), :]),
            reads=[cx.tb_kvsp.b(n, t) for t in range(NTW)],
            writes=[b_kv[n]], tag="kvld")
    wdn, b_wdn = load_w_fm(cx, cx.w["L0_attn_a_W_down_kv"].ap(), D, DL,
                           "wdnA")

    def lat_evac(n, t, pk, bpk):
        s.op("scalar", lambda e, n=n, t=t, pk=pk: e.activation(
            lat[:, n * S + t * 512:n * S + (t + 1) * 512], pk, AF.Copy),
            reads=[bpk], writes=[b_lat[n]], tag="latev")

    for n in range(2):
        for t in range(NTW):
            pk = bank(cx, (n + t) % 4)
            bpk = bbuf(cx, (n + t) % 4)
            def mm(e, n=n, t=t, pk=pk):
                ins = None
                for c in range(8):
                    ins = e.matmul(
                        pk, wdn[:, c * DL + n * 128:c * DL + (n + 1) * 128],
                        kv[:, c * S + t * 512:c * S + (t + 1) * 512],
                        start=(c == 0), stop=(c == 7))
                return ins
            s.op("tensor", mm, reads=[b_wdn] + b_kv, writes=[bpk],
                 tag="latmm")
            lat_evac(n, t, pk, bpk)
    ar.release()
    ar.mark()
    cx.ebf = ar.alloc("ebfk", [PB, 2 * S], BF16)
    cx.b_ebf = Buf("ebfk")
    s.dma("sync", lambda e, t_=cx.ebf: e.dma_start(t_[:, :],
                                                   cx.d_cbf.ap()[:, 2 * S:4 * S]),
          writes=[cx.b_ebf], tag="ldebfk")
    wuk, b_wuk = load_w_fm(cx, cx.w["L0_attn_a_W_up_k"].ap(), DL, D, "wukA")

    def k_evac(n, t, pk, bpk):
        i0, i1 = (t % 2) * 2, (t % 2) * 2 + 1
        s.op("vector", lambda e, t=t, pk=pk, i0=i0, eb=cx.ebf: e.tensor_mul(
            stg[:, i0 * 512:(i0 + 1) * 512], pk,
            eb[:, 0 * S + t * 512:0 * S + (t + 1) * 512]),
            reads=[bpk, cx.b_ebf], writes=[b_stg[i0]], tag="ksc-")
        s.op("vector", lambda e, t=t, pk=pk, i1=i1, eb=cx.ebf: e.tensor_mul(
            stg[:, i1 * 512:(i1 + 1) * 512], pk,
            eb[:, 1 * S + t * 512:1 * S + (t + 1) * 512]),
            reads=[bpk, cx.b_ebf], writes=[b_stg[i1]], tag="ksc+")
        s.dma("scalar", lambda e, n=n, t=t, i0=i0: e.dma_start(
            cx.qsp.ap()[2, _ts(n, 128), _ts(t, 512)],
            stg[:, i0 * 512:(i0 + 1) * 512]),
            reads=[b_stg[i0]], writes=[cx.tb_qsp[2].b(n, t)], tag="kst-")
        s.dma("scalar", lambda e, n=n, t=t, i1=i1: e.dma_start(
            cx.qsp.ap()[3, _ts(n, 128), _ts(t, 512)],
            stg[:, i1 * 512:(i1 + 1) * 512]),
            reads=[b_stg[i1]], writes=[cx.tb_qsp[3].b(n, t)], tag="kst+")
        emit_cs_stat(cx, pk, bpk, cx.cs_k, cx.b_cs_k, n == 0 and t == 0)

    for n in range(8):
        for t in range(NTW):
            pk = bank(cx, (n + t) % 4)
            bpk = bbuf(cx, (n + t) % 4)
            def mm(e, n=n, t=t, pk=pk):
                ins = None
                for c in range(2):
                    ins = e.matmul(
                        pk, wuk[:, c * D + n * 128:c * D + (n + 1) * 128],
                        lat[:, c * S + t * 512:c * S + (t + 1) * 512],
                        start=(c == 0), stop=(c == 1))
                return ins
            s.op("tensor", mm, reads=[b_wuk, b_lat[0], b_lat[1]],
                 writes=[bpk], tag="kRmm")
            k_evac(n, t, pk, bpk)
    ar.release()
    ar.mark()
    wuv, b_wuv = load_w_fm(cx, cx.w["L0_attn_a_W_up_v"].ap(), DL, D, "wuvA")
    vst = ar.alloc("vstA", [PB, 2 * 1040], BF16)
    b_vst = [Buf(), Buf()]
    s.op("vector", lambda e: e.memset(vst[:, :], 1.0),
         writes=[b_vst[0], b_vst[1]])
    for t in range(S // 128):
        pw = cx.PW[2 + (t % 2)]
        bws = [bbuf(cx, 4 + 2 * (t % 2)), bbuf(cx, 5 + 2 * (t % 2))]
        def mm(e, t=t, pw=pw):
            ins = None
            for half in range(2):
                for c in range(2):
                    ins = e.matmul(
                        pw[:, half * 512:(half + 1) * 512],
                        lat[:, c * S + t * 128:c * S + (t + 1) * 128],
                        wuv[:, c * D + half * 512:c * D + (half + 1) * 512],
                        start=(c == 0), stop=(c == 1))
            return ins
        s.op("tensor", mm, reads=[b_wuv, b_lat[0], b_lat[1]], writes=bws,
             tag="vRmm")
        p = t % 2
        s.op("scalar", lambda e, pw=pw, p=p: e.activation(
            vst[:, p * 1040:(p + 1) * 1040].rearrange(
                "q (h c) -> q h c", h=16)[:, :, 0:64],
            pw[:, 0:1024].rearrange("q (h d) -> q h d", h=16),
            AF.Copy),
            reads=bws, writes=[b_vst[p]], tag="vev")
        s.dma("scalar", lambda e, t=t, p=p: e.dma_start(
            cx.vsp.ap().rearrange("g r c -> r g c")[_ts(t, 128)],
            vst[:, p * 1040:(p + 1) * 1040].rearrange("q (g c) -> q g c", g=8)),
            reads=[b_vst[p]], writes=cx.tb_vsp.row(t), tag="vst")
    emit_negb(cx, cx.cs_q, cx.b_cs_q, cx.cs_k, cx.b_cs_k,
              np.exp(2 * LAM * cx.S) * 1.0000001, cx.negb, cx.b_negb)
    ar.release()
    ar.release()


def emit_div_evac(cx, p, qc, h, ab, bab, ofst, b_ofst, rzrow, b_rzrow,
                  rzbc, b_rzbc, tag):
    s = cx.s
    sl = (p * 16 + qc * 2 + h) % 4
    s.op("scalar", lambda e, ab=ab: e.activation(rzrow[64:65, 0:512],
                                                 ab[64:65, 0:512], AF.Copy),
         reads=[bab], writes=[b_rzrow], tag=tag + "cp")
    s.dma("sync", lambda e, sl=sl: e.dma_start(
        cx.zb.ap()[sl:sl + 1, :], rzrow[64:65, 0:512]),
        reads=[b_rzrow], writes=[cx.tb_zb[sl]], tag=tag + "zo")
    src_ap = bass.AP(cx.zb.ap().tensor, sl * 512, [[0, 64], [1, 512]])
    s.dma("sync", lambda e, src_ap=src_ap: e.dma_start(rzbc[:, :], src_ap),
          reads=[cx.tb_zb[sl]], writes=[b_rzbc], tag=tag + "zi")
    s.op("vector", lambda e: e.reciprocal(rzbc[:, :], rzbc[:, :]),
         reads=[b_rzbc], writes=[b_rzbc], tag=tag + "rc")
    s.op("vector", lambda e, ab=ab, h=h: e.tensor_mul(
        ofst[h * 64:(h + 1) * 64, :], ab[0:64, 0:512], rzbc[:, :]),
        reads=[bab, b_rzbc], writes=[b_ofst], tag=tag + "dv")


def emit_attn_a_attn(cx):
    s, ar, S = cx.s, cx.ar, cx.S
    NKT = S // 128
    NQC = S // 512
    ar.mark()
    sl = {}
    b_sl = {}
    for nm in ("qn", "qp"):
        sl[nm] = [ar.alloc(f"A3{nm}", [PB, S], BF16)]
        b_sl[nm] = [Buf()]
    kz = {}
    b_kz = {}
    for nm in ("kn0", "kn1", "kp0", "kp1"):
        kz[nm] = ar.alloc(f"A3{nm}", [PB, S], BF16)
        b_kz[nm] = Buf()
        zr = slice(64, 128) if nm[-1] == "0" else slice(0, 64)
        s.op("vector", lambda e, t_=kz[nm], zr=zr: e.memset(t_[zr, :], 0.0),
             writes=[b_kz[nm]], tag="A3kz0")
    vv = [ar.alloc("A3v", [PB, NKT * 130], BF16)]
    b_vv = [Buf()]
    eT = ar.alloc("A3e", [PB, NKT * 1024], BF16)
    b_eT = [Buf(f"A3e{k}") for k in range(NKT)]
    ofst = [ar.alloc(f"A3o{i}", [PB, 512], BF16) for i in range(2)]
    b_ofst = [Buf(), Buf()]
    rzrow = [ar.alloc(f"A3rz{i}", [PB, 512], F32) for i in range(2)]
    b_rzrow = [Buf() for _ in range(2)]
    rzbc = [ar.alloc(f"A3rb{i}", [64, 512], F32) for i in range(2)]
    b_rzbc = [Buf(), Buf()]
    SLOT = {"qn": 0, "qp": 1, "kn": 2, "kp": 3}

    def sel(kt, qc, d):
        r = kt - 4 * qc
        if kt * 128 + 127 < qc * 512:
            return "kp", sl["qn"][0], b_sl["qn"][0], -1
        if kt * 128 > qc * 512 + 511:
            return "kn", sl["qp"][0], b_sl["qp"][0], -1
        return "kp", sl["qn"][0], b_sl["qn"][0], r

    def emit_kt(p, qc, kt, d, pend):
        kside, Qt, bQ, dg = sel(kt, qc, d)
        par = kt % 3
        pw = cx.PW[(0, 1, 3)[par]]
        bk0 = (0, 2, 6)[par]
        pwb = [bbuf(cx, bk0), bbuf(cx, bk0 + 1)]
        K0, K1 = kz[kside + "0"], kz[kside + "1"]
        bK = [b_kz[kside + "0"], b_kz[kside + "1"]]

        def mmp(e, K0=K0, K1=K1, Qt=Qt, kt=kt, qc=qc, pw=pw):
            e.matmul(pw[:, 0:512], K0[0:64, _ts(kt, 128)],
                     Qt[0:64, _ts(qc, 512)], start=True, stop=True)
            return e.matmul(pw[:, 512:1024], K1[64:128, _ts(kt, 128)],
                            Qt[64:128, _ts(qc, 512)], start=True, stop=True)
        s.op("tensor", mmp, reads=bK + [bQ], writes=pwb, tag="A3mm")
        if dg >= 0:
            s.op("vector", lambda e, dg=dg, pw=pw: e.tensor_mul(
                pw[:, 0:1024].rearrange("p (h c) -> p h c", h=2),
                pw[:, 0:1024].rearrange("p (h c) -> p h c", h=2),
                cx.c32[:, C_CORR + dg * 512:C_CORR + (dg + 1) * 512]
                .unsqueeze(1).broadcast_to([PB, 2, 512])),
                reads=pwb + [cx.b_c32], writes=pwb, tag="A3corr")
        if pend is not None:
            emit_av(kt, d)
        s.op("scalar", lambda e, kt=kt, pw=pw: e.activation(
            eT[:, _ts(kt, 1024)], pw[:, 0:1024], AF.Exp,
            bias=cx.negb[:, 0:1]),
            reads=pwb + [cx.b_negb], writes=[b_eT[kt]], tag="A3exp")

    def emit_av(kt, d):
        for h in range(2):
            ab = cx.PW[2][:, h * 512:(h + 1) * 512]
            s.op("tensor", lambda e, h=h, kt=kt, ab=ab: e.matmul(
                ab[0:65, :],
                vv[0][:, kt * 130 + h * 65:kt * 130 + (h + 1) * 65],
                eT[:, kt * 1024 + h * 512:kt * 1024 + (h + 1) * 512],
                start=(kt == 0), stop=(kt == NKT - 1)),
                reads=[b_eT[kt], b_vv[0]], writes=[bbuf(cx, 4 + h)],
                tag="A3av")

    def emit_div(p, qc):
        for h in range(2):
            ab = cx.PW[2][:, h * 512:(h + 1) * 512]
            dpar = (qc * 2 + h) % 4
            emit_div_evac(cx, p, qc, h, ab, bbuf(cx, 4 + h), ofst[qc % 2],
                          b_ofst[qc % 2], rzrow[dpar % 2], b_rzrow[dpar % 2],
                          rzbc[dpar % 2], b_rzbc[dpar % 2], "A3")
        s.dma("scalar", lambda e, p=p, qc=qc: e.dma_start(
            cx.ospf.ap()[_ts(p, 128), _ts(qc, 512)], ofst[qc % 2][:, :]),
            reads=[b_ofst[qc % 2]], writes=[cx.tb_ospf.b(p, qc)], tag="A3ost")

    for p in range(8):
        d = 0
        for nm in ("qn", "qp"):
            s.dma("sync", lambda e, nm=nm, p=p: e.dma_start(
                sl[nm][0][:, :], cx.qsp.ap()[SLOT[nm], _ts(p, 128), :]),
                reads=[cx.tb_qsp[SLOT[nm]].row(p)], writes=[b_sl[nm][0]],
                tag="A3ld")
        for nm, h in (("kn", 0), ("kn", 1), ("kp", 0), ("kp", 1)):
            s.dma("sync", lambda e, nm=nm, h=h, p=p: e.dma_start(
                kz[nm + str(h)][h * 64:h * 64 + 64, :],
                cx.qsp.ap()[SLOT[nm], p * 128 + h * 64:p * 128 + h * 64 + 64,
                            :]),
                reads=[cx.tb_qsp[SLOT[nm]].row(p)], writes=[b_kz[nm + str(h)]],
                tag="A3ldk")
        s.dma("gpsimd", lambda e, p=p: e.dma_start(
            vv[0][:, :].rearrange("q (t c) -> q t c", c=130),
            cx.vsp.ap()[p].rearrange("(t q) c -> q t c", q=128)),
            reads=[b.b(t, p) for t in range(NKT) for b in [cx.tb_vsp]],
            writes=[b_vv[0]], tag="A3ldv")
        pend = None
        for qc in range(NQC):
            for kt in range(NKT):
                emit_kt(p, qc, kt, d, pend)
            if pend is not None:
                emit_div(p, pend)
            pend = qc
        for kt in range(NKT):
            emit_av(kt, d)
        emit_div(p, pend)
    ar.release()


def emit_attn_out(cx, wo_name, ncols_o, x_res_ap, x_res_tb, x_out_ap,
                  x_out_tb, xbf_out_ap, xbf_out_tb, xf_out_ap=None,
                  xf_out_tb=None, tag="AO"):
    s, ar, S = cx.s, cx.ar, cx.S
    NT = S // 128
    nch = ncols_o // 128
    ar.mark()
    ofm = cx.ar.alloc(tag + "_ofm", [PB, nch * S], BF16)
    b_ofm = [Buf() for _ in range(nch)]
    for c in range(nch):
        s.dma("sync", lambda e, c=c: e.dma_start(
            ofm[:, c * S:(c + 1) * S], cx.ospf.ap()[_ts(c, 128), :]),
            reads=cx.tb_ospf.row(c), writes=[b_ofm[c]], tag=tag + "ofml")
    wo, b_wo = load_w_fm(cx, cx.w[wo_name].ap(), ncols_o, D, tag + "_wo")
    xt = ar.alloc(tag + "xt", [PB, 2 * D], F32)
    x1 = ar.alloc(tag + "x1", [PB, 2 * D], F32)
    ob = ar.alloc(tag + "ob", [PB, 2 * D], BF16)
    of = ar.alloc(tag + "of", [PB, 2 * D], F32)
    b_xt = [Buf(), Buf()]
    b_x1 = [Buf(), Buf()]
    b_ob = [Buf(), Buf()]
    b_of = [Buf(), Buf()]
    for t in range(NT):
        pr = t % 2
        cols = slice(pr * D, (pr + 1) * D)
        pw = cx.PW[pr]
        bws = [bbuf(cx, 2 * pr), bbuf(cx, 2 * pr + 1)]
        s.dma("sync", lambda e, t=t, cols=cols: e.dma_start(
            xt[:, cols], x_res_ap[_ts(t, 128), :]),
            reads=[x_res_tb.r(t * 128, t * 128 + 128)] if x_res_tb else [],
            writes=[b_xt[pr]], tag=tag + "ldx")
        def mm(e, t=t, pw=pw):
            ins = None
            for half in range(2):
                for c in range(nch):
                    ins = e.matmul(
                        pw[:, half * 512:(half + 1) * 512],
                        ofm[:, c * S + t * 128:c * S + (t + 1) * 128],
                        wo[:, c * D + half * 512:c * D + (half + 1) * 512],
                        start=(c == 0), stop=(c == nch - 1))
            return ins
        s.op("tensor", mm, reads=b_ofm + [b_wo], writes=bws, tag=tag + "mm")
        s.op("vector", lambda e, cols=cols, pw=pw: e.tensor_add(
            x1[:, cols], pw[:, 0:1024], xt[:, cols]),
            reads=bws + [b_xt[pr]], writes=[b_x1[pr]], tag=tag + "res")
        s.dma("scalar", lambda e, t=t, cols=cols: e.dma_start(
            x_out_ap[_ts(t, 128), :], x1[:, cols]),
            reads=[b_x1[pr]], writes=[x_out_tb.w(t * 128, t * 128 + 128)],
            tag=tag + "stx")
        emit_ln(cx, x1, b_x1[pr], cols, ob[:, cols], b_ob[pr],
                (of[:, cols] if xf_out_ap is not None else None),
                (b_of[pr] if xf_out_ap is not None else None))
        s.dma("scalar", lambda e, t=t, cols=cols: e.dma_start(
            xbf_out_ap[ZR + t * 128:ZR + (t + 1) * 128, :], ob[:, cols]),
            reads=[b_ob[pr]], writes=sp_w(xbf_out_tb, t),
            tag=tag + "stb")
        if xf_out_ap is not None:
            s.dma("scalar", lambda e, t=t, cols=cols: e.dma_start(
                xf_out_ap[ZR + t * 128:ZR + (t + 1) * 128, :], of[:, cols]),
                reads=[b_of[pr]],
                writes=sp_w(xf_out_tb, t), tag=tag + "stf")
    ar.release()


def sp_w(tb, t):
    return tb.r(128 + t * 128, 128 + (t + 1) * 128)


def sp_rs(tb, t, shift=0):
    return tb.r(128 + t * 128 - shift, 128 + (t + 1) * 128 - shift)


W_NAMES = [
    "L0_attn_a_Wq", "L0_attn_a_Wk", "L0_attn_a_Wv", "L0_attn_a_Wo",
    "L0_attn_a_W_down_kv", "L0_attn_a_W_up_k", "L0_attn_a_W_up_v",
    "L0_attn_b_Wq", "L0_attn_b_Wk", "L0_attn_b_Wv", "L0_attn_b_Wo",
    "L0_ffn_wqkv", "L0_ffn_q_scale", "L0_ffn_k_scale", "L0_ffn_wo",
    "L0_ffn_ln_g", "L0_ffn_ln_b", "L0_ffn_w1", "L0_ffn_b1", "L0_ffn_w2",
    "L0_ffn_b2",
]
W_SHAPES = {
    "L0_attn_a_Wq": [D, D], "L0_attn_a_Wk": [D, D], "L0_attn_a_Wv": [D, D],
    "L0_attn_a_Wo": [D, D], "L0_attn_a_W_down_kv": [D, DL],
    "L0_attn_a_W_up_k": [DL, D], "L0_attn_a_W_up_v": [DL, D],
    "L0_attn_b_Wq": [D, D], "L0_attn_b_Wk": [D, D], "L0_attn_b_Wv": [D, D],
    "L0_attn_b_Wo": [D, D],
    "L0_ffn_wqkv": [2, D, 3 * INNER], "L0_ffn_q_scale": [2, HD],
    "L0_ffn_k_scale": [2, HD], "L0_ffn_wo": [2, INNER, D],
    "L0_ffn_ln_g": [2, D], "L0_ffn_ln_b": [2, D],
    "L0_ffn_w1": [2, D, 2 * HID], "L0_ffn_b1": [2, 2 * HID],
    "L0_ffn_w2": [2, HID, D], "L0_ffn_b2": [2, D],
}


def make_ctx(S, drains=False):
    nc = bass.Bass("TRN2", target_bir_lowering=False)
    cx = Ctx()
    cx.nc = nc
    cx.S = S
    cx.s = Sched(nc, drains=drains)
    cx.ar = Arena(nc, cx.s)
    NT = S // 128
    cx.d_xin = nc.dram_tensor("L0_attn_a_x", [S, D], F32, kind="ExternalInput")
    cx.d_out = nc.dram_tensor("out", [S, D], F32, kind="ExternalOutput")
    cx.w = {}
    for n in W_NAMES:
        if n == "L0_attn_a_x":
            continue
        cx.w[n] = nc.dram_tensor(n, W_SHAPES[n], F32, kind="ExternalInput")
    cx.d_cbf = nc.dram_tensor("CBF", [PB, 4 * S + 896], BF16, kind="ExternalInput")
    cx.d_c32 = nc.dram_tensor("C32", [PB, C32_COLS], F32, kind="ExternalInput")
    cx.d_rot = nc.dram_tensor("ROT", [4, S, HD], F32, kind="ExternalInput")
    cx.x_a = nc.dram_tensor("x_a", [S, D], F32)
    cx.x_b = nc.dram_tensor("x_b", [S, D], F32)
    cx.xf_a = nc.dram_tensor("xf_a", [ZR + S, D], F32)
    cx.xf_b = nc.dram_tensor("xf_b", [ZR + S, D], F32)
    cx.xbf_a = nc.dram_tensor("xbf_a", [ZR + S, D], BF16)
    cx.xbf_b = nc.dram_tensor("xbf_b", [ZR + S, D], BF16)
    cx.qsp = nc.dram_tensor("qsp", [4, D, S], BF16)
    cx.kvsp = nc.dram_tensor("kvsp", [D, S], BF16)
    cx.vsp = nc.dram_tensor("vsp", [8, S, 130], BF16)
    cx.osp = nc.dram_tensor("osp", [S, D], BF16)
    cx.ospf = nc.dram_tensor("ospf", [D, S], BF16)
    cx.zb = nc.dram_tensor("zb", [4, 512], F32)
    cx.qtm = nc.dram_tensor("qtm", [S, INNER], BF16)
    cx.ktm = nc.dram_tensor("ktm", [S, INNER], BF16)
    cx.tb_out = TBuf("out", S, 128)
    cx.tb_x_a = TBuf("x_a", S, 128)
    cx.tb_x_b = TBuf("x_b", S, 128)
    cx.tb_xf_a = TBuf("xf_a", 128 + S, 128)
    cx.tb_xf_b = TBuf("xf_b", 128 + S, 128)
    cx.tb_xbf_a = TBuf("xbf_a", 128 + S, 128)
    cx.tb_xbf_b = TBuf("xbf_b", 128 + S, 128)
    cx.tb_qsp = [TBuf2(f"qsp{i}", 8, S // 512) for i in range(4)]
    cx.tb_kvsp = TBuf2("kvsp", 8, S // 512)
    cx.tb_vsp = TBuf2("vsp", NT, 8)
    cx.tb_osp = TBuf2("osp", NT, 8)
    cx.tb_ospf = TBuf2("ospf", 8, S // 512)
    cx.tb_zb = [Buf(f"zb{i}") for i in range(4)]
    cx.tb_qtm = TBuf2("qtm", NT, 4)
    cx.tb_ktm = TBuf2("ktm", NT, 4)
    cx.PW = [nc.alloc_psum_tensor(f"PW{i}", [PB, 1024], F32)
             for i in range(4)]
    cx.b_pw = [Buf(f"pw{i}") for i in range(8)]
    ar = cx.ar
    cx.ones128 = ar.alloc("ones128", [PB, 128], BF16)
    cx.b_ones128 = Buf("ones128")
    cx.c32 = ar.alloc("c32sb", [PB, C32_COLS], F32)
    cx.b_c32 = Buf("c32")
    cx.mkb = ar.alloc("mkb", [PB, 896], BF16)
    cx.b_mkb = Buf("mkb")
    cx.st8 = ar.alloc("st8", [PB, 8], F32)
    cx.b_st8 = Buf("st8")
    cx.st8bf = ar.alloc("st8bf", [PB, 2], BF16)
    cx.b_st8bf = Buf("st8bf")
    cx.sq_st = ar.alloc("sq_st", [PB, 512], BF16)
    cx.b_sq_st = Buf("sq_st")
    cx.cs_q = ar.alloc("cs_q", [1, 2], F32)
    cx.b_cs_q = Buf("cs_q")
    cx.cs_k = ar.alloc("cs_k", [1, 2], F32)
    cx.b_cs_k = Buf("cs_k")
    cx.negb = ar.alloc("negb", [PB, 4], F32)
    cx.b_negb = Buf("negb")
    cx.ln_scr = ar.alloc("ln_scr", [PB, 16], F32)
    cx.b_ln_scr = Buf("ln_scr")
    cx.fsc = ar.alloc("fsc", [PB, 2048], F32)
    cx.b_fsc = Buf("fsc")
    cx.b_fscp = [Buf("fscp0"), Buf("fscp1")]
    cx.b_lnp = [Buf("lnp0"), Buf("lnp1")]
    cx.f8 = ar.alloc("f8", [PB, 48], F32)
    cx.b_f8 = [Buf("f8a"), Buf("f8b")]
    cx.t_ob = ar.alloc("t_ob", [PB, 2 * D], BF16)
    cx.bt_ob = [Buf(), Buf()]
    cx.t_of = ar.alloc("t_of", [PB, 2 * D], F32)
    cx.bt_of = [Buf(), Buf()]
    cx.t_xr = ar.alloc("t_xr", [PB, 2 * D], F32)
    cx.bt_xr = [Buf(), Buf()]
    cx.t_xo = ar.alloc("t_xo", [PB, 2 * D], F32)
    cx.bt_xo = [Buf(), Buf()]
    cx.kc = ar.alloc("kconst", [PB, 2], F32)
    cx.b_kc = Buf("kconst")
    s = cx.s
    s.op("vector", lambda e: e.memset(cx.kc[:, 0:1], EPS_LN),
         writes=[cx.b_kc])
    s.op("vector", lambda e: e.memset(cx.ones128[:, :], 1.0),
         writes=[cx.b_ones128])
    s.dma("sync", lambda e: e.dma_start(cx.c32[:, :], cx.d_c32.ap()),
          writes=[cx.b_c32])
    s.dma("sync", lambda e: e.dma_start(cx.mkb[:, :],
                                        cx.d_cbf.ap()[:, 4 * S:4 * S + 896]),
          writes=[cx.b_mkb])
    return cx


def build_program(S, upto="full", drains=False):
    cx = make_ctx(S, drains=drains)
    emit_zero_rows(cx)
    emit_ln_pass(cx, cx.d_xin.ap(), None, cx.xbf_a.ap(), cx.tb_xbf_a,
                 tag="ln0")
    emit_attn_a_proj(cx)
    emit_attn_a_attn(cx)
    if upto == "attn_a":
        emit_attn_out(cx, "L0_attn_a_Wo", D, cx.d_xin.ap(), None,
                      cx.d_out.ap(), cx.tb_x_a, cx.xbf_b.ap(), cx.tb_xbf_b,
                      cx.xf_a.ap(), cx.tb_xf_a, tag="AO")
        cx.s.finalize()
        return cx
    emit_attn_out(cx, "L0_attn_a_Wo", D, cx.d_xin.ap(), None,
                  cx.x_a.ap(), cx.tb_x_a, cx.xbf_b.ap(), cx.tb_xbf_b,
                  cx.xf_a.ap(), cx.tb_xf_a, tag="AO")
    if upto == "ffn_l0":
        tail = make_tail_dbg(cx)
        emit_ffn_layer(cx, 0, cx.xbf_b.ap(), cx.tb_xbf_b, cx.xf_a.ap(),
                       cx.tb_xf_a, cx.xf_b.ap(), cx.tb_xf_b, tail)
        cx.s.finalize()
        return cx
    tail = make_tail_stream(cx, cx.xf_a.ap(), cx.tb_xf_a, cx.xbf_a.ap(),
                            cx.tb_xbf_a)
    emit_ffn_layer(cx, 0, cx.xbf_b.ap(), cx.tb_xbf_b, cx.xf_a.ap(),
                   cx.tb_xf_a, cx.xf_b.ap(), cx.tb_xf_b, tail)
    if upto == "ffn1":
        tail = make_tail_sublayer(cx, cx.x_a.ap(), cx.tb_x_a,
                                  cx.d_out.ap(), cx.tb_out, None, None)
        emit_ffn_layer(cx, 1, cx.xbf_a.ap(), cx.tb_xbf_a, cx.xf_a.ap(),
                       cx.tb_xf_a, cx.xf_b.ap(), cx.tb_xf_b, tail)
        cx.s.finalize()
        return cx
    tail = make_tail_sublayer(cx, cx.x_a.ap(), cx.tb_x_a, cx.x_b.ap(),
                              cx.tb_x_b, cx.xbf_b.ap(), cx.tb_xbf_b)
    emit_ffn_layer(cx, 1, cx.xbf_a.ap(), cx.tb_xbf_a, cx.xf_a.ap(),
                   cx.tb_xf_a, cx.xf_b.ap(), cx.tb_xf_b, tail)
    emit_attn_b_proj(cx)
    emit_attn_b_attn(cx)
    if upto == "attn_b":
        emit_attn_out(cx, "L0_attn_b_Wo", D, cx.x_b.ap(), cx.tb_x_b,
                      cx.d_out.ap(), cx.tb_out, cx.xbf_a.ap(), cx.tb_xbf_a,
                      cx.xf_a.ap(), cx.tb_xf_a, tag="BO")
        cx.s.finalize()
        return cx
    emit_attn_out(cx, "L0_attn_b_Wo", D, cx.x_b.ap(), cx.tb_x_b,
                  cx.x_a.ap(), cx.tb_x_a, cx.xbf_a.ap(), cx.tb_xbf_a,
                  cx.xf_a.ap(), cx.tb_xf_a, tag="BO")
    tail = make_tail_stream(cx, cx.xf_a.ap(), cx.tb_xf_a, cx.xbf_b.ap(),
                            cx.tb_xbf_b)
    emit_ffn_layer(cx, 0, cx.xbf_a.ap(), cx.tb_xbf_a, cx.xf_a.ap(),
                   cx.tb_xf_a, cx.xf_b.ap(), cx.tb_xf_b, tail)
    tail = make_tail_sublayer(cx, cx.x_a.ap(), cx.tb_x_a, cx.d_out.ap(),
                              cx.tb_out, None, None)
    emit_ffn_layer(cx, 1, cx.xbf_b.ap(), cx.tb_xbf_b, cx.xf_a.ap(),
                   cx.tb_xf_a, cx.xf_b.ap(), cx.tb_xf_b, tail)
    cx.s.finalize()
    return cx


def make_tail_dbg(cx):
    def tail(t, x1, cols, b):
        cx.s.dma("scalar", lambda e, t=t, cols=cols: e.dma_start(
            cx.d_out.ap()[_ts(t, 128), :], x1[:, cols]),
            reads=[b], writes=cx.tb_out.w(t * 128, (t + 1) * 128),
            tag="dbgout")
    return tail



def emit_l2n_rot(cx, psq, bpsq, rt, b_rt, tab_a, tab_b, tile_t, out_bf, b_out):
    s = cx.s
    pr = tile_t % 2
    F0 = pr * 1024
    G0 = pr * 24
    f = cx.fsc
    bf_ = cx.b_fscp[pr]
    f8 = cx.f8
    b8 = cx.b_f8[pr]
    NT = cx.S // 128
    s.op("scalar", lambda e: e.activation(f[:, F0:F0 + 512], psq, AF.Square),
         reads=[bpsq], writes=[bf_], tag="l2sq")
    s.op("vector", lambda e: e.reduce_sum(
        f8[:, G0:G0 + 8], f[:, F0:F0 + 512].rearrange("p (h d) -> p h d", h=8), axis=AX.X),
        reads=[bf_], writes=[b8], tag="l2ss")
    s.op("scalar", lambda e: e.activation(f8[:, G0 + 8:G0 + 16], f8[:, G0:G0 + 8], AF.Sqrt),
         reads=[b8], writes=[b8], tag="l2rt")
    s.op("vector", lambda e: e.tensor_scalar_max(f8[:, G0 + 8:G0 + 16], f8[:, G0 + 8:G0 + 16],
                                                 1e-12),
         reads=[b8], writes=[b8], tag="l2cl")
    s.op("vector", lambda e: e.reciprocal(f8[:, G0 + 16:G0 + 24], f8[:, G0 + 8:G0 + 16]),
         reads=[b8], writes=[b8], tag="l2rc")
    s.op("vector", lambda e: e.tensor_tensor(
        f[:, F0:F0 + 512].rearrange("p (h d) -> p h d", h=8),
        psq.rearrange("p (h d) -> p h d", h=8),
        f8[:, G0 + 16:G0 + 24].unsqueeze(2).broadcast_to([PB, 8, 64]), op=ALU.mult),
        reads=[bpsq, b8], writes=[bf_], tag="l2mul")
    s.op("vector", lambda e: e.tensor_copy(
        f[:, F0 + 512:F0 + 1024].rearrange("p (h d) -> p h d", h=8)[:, :, 0:32],
        f[:, F0:F0 + 512].rearrange("p (h d) -> p h d", h=8)[:, :, 32:64]),
        reads=[bf_], writes=[bf_], tag="l2sh")
    s.op("vector", lambda e: e.tensor_copy(
        f[:, F0 + 512:F0 + 1024].rearrange("p (h d) -> p h d", h=8)[:, :, 32:64],
        f[:, F0:F0 + 512].rearrange("p (h d) -> p h d", h=8)[:, :, 0:32]),
        reads=[bf_], writes=[bf_], tag="l2sh2")
    s.op("vector", lambda e: e.tensor_tensor(
        f[:, F0:F0 + 512].rearrange("p (h d) -> p h d", h=8),
        f[:, F0:F0 + 512].rearrange("p (h d) -> p h d", h=8),
        rt[:, tab_a + tile_t * 64:tab_a + (tile_t + 1) * 64]
        .unsqueeze(1).broadcast_to([PB, 8, 64]), op=ALU.mult),
        reads=[bf_, b_rt], writes=[bf_], tag="l2ta")
    s.op("vector", lambda e: e.tensor_tensor(
        f[:, F0 + 512:F0 + 1024].rearrange("p (h d) -> p h d", h=8),
        f[:, F0 + 512:F0 + 1024].rearrange("p (h d) -> p h d", h=8),
        rt[:, tab_b + tile_t * 64:tab_b + (tile_t + 1) * 64]
        .unsqueeze(1).broadcast_to([PB, 8, 64]), op=ALU.mult),
        reads=[bf_, b_rt], writes=[bf_], tag="l2tb")
    s.op("vector", lambda e: e.tensor_add(out_bf, f[:, F0:F0 + 512],
                                          f[:, F0 + 512:F0 + 1024]),
         reads=[bf_], writes=[b_out], tag="l2o")


def emit_ffn_qkv(cx, l, xbf_ap, tb_xbf):
    s, ar, S = cx.s, cx.ar, cx.S
    NT = S // 128
    NB = S // 512
    rt = ar.alloc("rtab", [PB, 4 * NT * 64], F32)
    b_rt = Buf("rtab")
    ar.mark()
    rb = ar.alloc("rbase", [PB, 4 * NT * 64], F32)
    b_rb = Buf()
    gn = ar.alloc("gain", [PB, 4 * 64], F32)
    b_gn = Buf()
    gr = ar.alloc("grow", [1, 2 * 64], F32)
    b_gr = Buf()
    for i in range(4):
        s.dma("sync", lambda e, i=i: e.dma_start(
            rb[:, i * NT * 64:(i + 1) * NT * 64].rearrange(
                "p (t d) -> p t d", d=64),
            cx.d_rot.ap()[i].rearrange("(t p) d -> p t d", p=128)),
            writes=[b_rb], tag="rbld")
    s.dma("sync", lambda e, l=l: e.dma_start(
        gr[0:1, 0:64], cx.w["L0_ffn_q_scale"].ap()[l:l + 1, :]),
        writes=[b_gr], tag="gq")
    s.dma("sync", lambda e, l=l: e.dma_start(
        gr[0:1, 64:128], cx.w["L0_ffn_k_scale"].ap()[l:l + 1, :]),
        writes=[b_gr], tag="gk")
    grb = ar.alloc("growb", [1, 2 * 64], BF16)
    b_grb = Buf()
    s.op("vector", lambda e: e.tensor_copy(grb[0:1, :], gr[0:1, :]),
         reads=[b_gr], writes=[b_grb])
    pb = bank(cx, 6)
    s.op("tensor", lambda e: e.matmul(pb[:, 0:128], cx.ones128[0:1, :],
                                      grb[0:1, :], start=True, stop=True),
         reads=[b_grb, cx.b_ones128], writes=[bbuf(cx, 6)], tag="gbc")
    s.op("vector", lambda e: e.tensor_copy(gn[:, 0:128], pb[:, 0:128]),
         reads=[bbuf(cx, 6)], writes=[b_gn])
    s.op("vector", lambda e: e.tensor_copy(
        gn[:, 128:256].rearrange("p (g d) -> p g d", g=2)[:, :, 0:32],
        gn[:, 0:128].rearrange("p (g d) -> p g d", g=2)[:, :, 32:64]),
        reads=[b_gn], writes=[b_gn], tag="gsh")
    s.op("vector", lambda e: e.tensor_copy(
        gn[:, 128:256].rearrange("p (g d) -> p g d", g=2)[:, :, 32:64],
        gn[:, 0:128].rearrange("p (g d) -> p g d", g=2)[:, :, 0:32]),
        reads=[b_gn], writes=[b_gn], tag="gsh2")
    for i, gcol in enumerate((0, 128, 64, 192)):
        s.op("vector", lambda e, i=i, gcol=gcol: e.tensor_tensor(
            rt[:, i * NT * 64:(i + 1) * NT * 64].rearrange(
                "p (t d) -> p t d", d=64),
            rb[:, i * NT * 64:(i + 1) * NT * 64].rearrange(
                "p (t d) -> p t d", d=64),
            gn[:, gcol:gcol + 64].unsqueeze(1).broadcast_to([PB, NT, 64]),
            op=ALU.mult),
            reads=[b_rb, b_gn], writes=[b_rt], tag="rtb")
    ar.release()
    wqkv, b_wqkv = load_w_fm(cx, cx.w["L0_ffn_wqkv"].ap()[l], D, 3 * INNER,
                             "wqkv")
    s.op("vector", lambda e, vp=cx.vp: e.memset(vp[:, :], 1.0),
         writes=cx.b_vp, tag="vpones")
    hfm = [ar.alloc(f"hfm{i}", [PB, 8 * 512], BF16) for i in range(2)]
    b_hfm = [Buf(), Buf()]
    qst = ar.alloc("qst", [PB, 2 * 512], BF16)
    b_qst = [Buf(), Buf()]
    kst = ar.alloc("kst", [PB, 2 * 512], BF16)
    b_kst = [Buf(), Buf()]
    for b in range(NB):
        d = b % 2
        for c in range(8):
            sh = 1 if c >= 4 else 0
            s.dma("scalar", lambda e, b=b, c=c, sh=sh, d=d: e.dma_start(
                hfm[d][:, c * 512:(c + 1) * 512],
                xbf_ap[ZR - sh + b * 512:ZR - sh + (b + 1) * 512,
                       c * 128:(c + 1) * 128], transpose=True),
                reads=tb_xbf.r(128 + b * 512 - sh, 128 + (b + 1) * 512),
                writes=[b_hfm[d]], tag="hfmT")
        for tt in range(4):
            t = b * 4 + tt
            bk = (tt % 2) * 3
            pq, pk, pv = bank(cx, bk), bank(cx, bk + 1), bank(cx, bk + 2)
            bq, bkk, bv = bbuf(cx, bk), bbuf(cx, bk + 1), bbuf(cx, bk + 2)
            def mm(e, tt=tt, d=d, pq=pq, pk=pk, pv=pv):
                ins = None
                for c in range(8):
                    for j, p in enumerate((pq, pk, pv)):
                        ins = e.matmul(
                            p, hfm[d][:, c * 512 + tt * 128:
                                      c * 512 + (tt + 1) * 128],
                            wqkv[:, c * 3 * INNER + j * 512:
                                 c * 3 * INNER + (j + 1) * 512],
                            start=(c == 0), stop=(c == 7))
                return ins
            s.op("tensor", mm, reads=[b_hfm[d], b_wqkv],
                 writes=[bq, bkk, bv], tag="qkvmm")
            s.op("scalar", lambda e, t=t, pv=pv, vp=cx.vp: e.activation(
                vp[:, t * 520:(t + 1) * 520].rearrange(
                    "p (h c) -> p h c", h=8)[:, :, 0:64],
                pv.rearrange("p (h d) -> p h d", h=8), AF.Copy),
                reads=[bv], writes=[cx.b_vp[t]], tag="vev")
            dd = t % 2
            emit_l2n_rot(cx, pq, bq, rt, b_rt, 0 * NT * 64, 1 * NT * 64, t,
                         qst[:, dd * 512:(dd + 1) * 512], b_qst[dd])
            s.dma("scalar", lambda e, t=t, dd=dd: e.dma_start(
                cx.qtm.ap()[_ts(t, 128), :], qst[:, dd * 512:(dd + 1) * 512]),
                reads=[b_qst[dd]], writes=cx.tb_qtm.row(t), tag="qspl")
            emit_l2n_rot(cx, pk, bkk, rt, b_rt, 2 * NT * 64, 3 * NT * 64, t,
                         kst[:, dd * 512:(dd + 1) * 512], b_kst[dd])
            s.dma("scalar", lambda e, t=t, dd=dd: e.dma_start(
                cx.ktm.ap()[_ts(t, 128), :], kst[:, dd * 512:(dd + 1) * 512]),
                reads=[b_kst[dd]], writes=cx.tb_ktm.row(t), tag="kspl")


def emit_ffn_attn(cx, l):
    s, ar, S = cx.s, cx.ar, cx.S
    NT = S // 128
    NW = S // 512
    qf = [ar.alloc(f"F2q{i}", [PB, S], BF16) for i in range(2)]
    kf = [ar.alloc(f"F2k{i}", [PB, S], BF16) for i in range(2)]
    b_qf = [Buf(), Buf()]
    b_kf = [Buf(), Buf()]
    qa = ar.alloc("F2qa", [PB, S], BF16)
    ka = ar.alloc("F2ka", [PB, S], BF16)
    b_qa = [Buf(f"qa{w}") for w in range(NW * 4)]
    b_qa0 = Buf("qa0")
    b_ka = Buf("ka")
    eT = ar.alloc("F2e", [PB, 8 * 512], BF16)
    b_eT = [Buf(f"F2e{sl}") for sl in range(8)]
    nm = ar.alloc("F2nm", [PB, 4], F32)
    b_nm = [Buf(f"F2nm{i}") for i in range(4)]
    rz = ar.alloc("F2rz", [PB, 2], F32)
    b_rz = Buf()
    s.op("vector", lambda e: e.memset(ka[64:128, :], 0.0), writes=[b_ka],
         tag="kaz")
    s.op("vector", lambda e: e.memset(ka[64:65, :], 1.0), writes=[b_ka],
         tag="kaones")
    s.op("vector", lambda e: e.memset(qa[64:128, :], 0.0), writes=[b_qa0],
         tag="qaz")
    for hp in range(4):
        d = hp % 2
        s.dma("sync", lambda e, hp=hp, d=d: e.dma_start(
            qf[d][:, :], cx.qtm.ap()[:, _ts(hp, 128)], transpose=True),
            reads=[cx.tb_qtm.b(t, hp) for t in range(NT)],
            writes=[b_qf[d]], tag="qfT")
        s.dma("sync", lambda e, hp=hp, d=d: e.dma_start(
            kf[d][:, :], cx.ktm.ap()[:, _ts(hp, 128)], transpose=True),
            reads=[cx.tb_ktm.b(t, hp) for t in range(NT)],
            writes=[b_kf[d]], tag="kfT")
        for h01 in range(2):
            h = 2 * hp + h01
            hs = slice(h01 * 64, h01 * 64 + 64)

            def emit_mrow(qg, sl4):
                pt = cx.PW[3]
                s.op("tensor", lambda e, sl4=sl4: e.transpose(
                    pt[0:1, 512:640], nm[:, sl4:sl4 + 1],
                    cx.c32[:, C_IDENT:C_IDENT + 128]),
                    reads=[b_nm[sl4], cx.b_c32], writes=[bbuf(cx, 7)],
                    tag="F2tr")
                s.op("vector", lambda e, qg=qg: e.tensor_copy(
                    qa[64:65, qg:qg + 128], pt[0:1, 512:640]),
                    reads=[bbuf(cx, 7)], writes=[b_qa[qg // 128]],
                    tag="F2mrow")
            lagq = []
            for w in range(NW):
                for qt in range(4):
                    qg = w * 512 + qt * 128
                    lo = qg - 512
                    loc = max(lo, 0)
                    width = qg + 128 - loc
                    pw = cx.PW[qt % 2]
                    bws = [bbuf(cx, 2 * (qt % 2)), bbuf(cx, 2 * (qt % 2) + 1)]
                    def madd(e, pw, c0, mt, sh, b0, last):
                        idb = cx.mkb[b0:b0 + 64, 512 + b0:512 + b0 + 64]
                        la = (cx.mkb[0:64, mt:mt + 128] if b0 == 0
                              else cx.mkb[64:128, sh:sh + 128])
                        lb = (cx.mkb[0:64, sh:sh + 128] if b0 == 0
                              else cx.mkb[64:128, mt:mt + 128])
                        e.matmul(pw[:, c0:c0 + 64], la, idb, start=False,
                                 stop=False)
                        return e.matmul(pw[:, c0 + 64:c0 + 128], lb, idb,
                                        start=False, stop=last)

                    def mmx(e, hs=hs, qg=qg, lo=lo, loc=loc, width=width,
                            pw=pw, d=d, b0=h01 * 64):
                        n1 = min(width, 512)
                        ina = (lo >= 0) or (width <= 512)
                        ins = e.matmul(pw[:, 0:n1], qf[d][hs, qg:qg + 128],
                                       kf[d][hs, loc:loc + n1], start=True,
                                       stop=(not ina))
                        if lo >= 0:
                            ins = madd(e, pw, 0, 256, 640, b0,
                                       width > 512)
                        if width <= 512:
                            ins = madd(e, pw, width - 128, 384, 768, b0,
                                       True)
                        if width > 512:
                            ins = e.matmul(pw[:, 512:width],
                                           qf[d][hs, qg:qg + 128],
                                           kf[d][hs, loc + 512:loc + width],
                                           start=True, stop=False)
                            ins = madd(e, pw, width - 128, 384, 768, b0,
                                       True)
                        return ins
                    s.op("tensor", mmx,
                         reads=[b_qf[d], b_kf[d], cx.b_mkb], writes=bws,
                         tag="F2mx")
                    sl4 = (w * 4 + qt) % 4
                    s.op("vector", lambda e, pw=pw, width=width, sl4=sl4:
                         e.reduce_max(
                             nm[:, sl4:sl4 + 1], pw[:, 0:width], axis=AX.X,
                             negate=True),
                         reads=bws, writes=[b_nm[sl4]], tag="F2rm")
                    lagq.append((qg, sl4))
                    if len(lagq) > 2:
                        emit_mrow(*lagq.pop(0))
            while lagq:
                emit_mrow(*lagq.pop(0))
            s.op("vector", lambda e, hs=hs, d=d: e.tensor_copy(
                qa[0:64, :], qf[d][hs, :]),
                reads=[b_qf[d]], writes=[b_qa0], tag="F2qa0")
            s.op("vector", lambda e, hs=hs, d=d: e.tensor_copy(
                ka[0:64, :], kf[d][hs, :]),
                reads=[b_kf[d]], writes=[b_ka], tag="F2ka0")
            for w in range(NW):
                kt0 = w * 4 - 4
                for ktg in range(max(0, kt0), w * 4 + 4):
                    slot = ktg - kt0
                    qs_lo = max(0, ktg - w * 4)
                    qs_hi = min(3, ktg - w * 4 + 4)
                    nqs = qs_hi - qs_lo + 1
                    qg0 = w * 512 + qs_lo * 128
                    pe = bank(cx, 4 + (ktg % 2))
                    bpe = bbuf(cx, 4 + (ktg % 2))
                    moffs = []
                    if 0 <= ktg - w * 4 <= 3:
                        moffs.append(((ktg - w * 4 - qs_lo) * 128, 128))
                    if 0 <= ktg - w * 4 + 4 <= 3:
                        moffs.append(((ktg - w * 4 + 4 - qs_lo) * 128, 0))

                    def mme(e, ktg=ktg, qg0=qg0, nqs=nqs, pe=pe,
                            moffs=tuple(moffs)):
                        ins = e.matmul(pe[:, 0:nqs * 128],
                                       ka[:, _ts(ktg, 128)],
                                       qa[:, qg0:qg0 + nqs * 128],
                                       start=True, stop=(not moffs))
                        for i, (off, m0) in enumerate(moffs):
                            ins = e.matmul(pe[:, off:off + 128],
                                           cx.mkb[:, m0:m0 + 128],
                                           cx.mkb[:, 512:640], start=False,
                                           stop=(i == len(moffs) - 1))
                        return ins
                    s.op("tensor", mme,
                         reads=[b_ka, b_qa0, cx.b_mkb] +
                         [b_qa[(qg0 // 128) + i] for i in range(nqs)],
                         writes=[bpe], tag="F2me")
                    s.op("scalar", lambda e, pe=pe, slot=slot, qs_lo=qs_lo,
                         nqs=nqs: e.activation(
                             eT[:, slot * 512 + qs_lo * 128:
                                slot * 512 + qs_lo * 128 + nqs * 128],
                             pe[:, 0:nqs * 128], AF.Exp),
                         reads=[bpe], writes=[b_eT[slot]], tag="F2ex")
                for qs in range(4):
                    qg = w * 512 + qs * 128
                    ktgs = [k for k in range(max(0, w * 4 + qs - 4),
                                             w * 4 + qs + 1)]
                    ab = bank(cx, 6)
                    bab = bbuf(cx, 6)
                    def av(e, qs=qs, w=w, h=h, ktgs=tuple(ktgs), ab=ab,
                           kt0=kt0, vp=cx.vp):
                        ins = None
                        for i, ktg in enumerate(ktgs):
                            slot = ktg - kt0
                            ins = e.matmul(
                                ab[:, 0:65],
                                eT[:, slot * 512 + qs * 128:
                                   slot * 512 + (qs + 1) * 128],
                                vp[:, ktg * 520 + h * 65:
                                   ktg * 520 + (h + 1) * 65],
                                start=(i == 0), stop=(i == len(ktgs) - 1))
                        return ins
                    s.op("tensor", av,
                         reads=[b_eT[k - kt0] for k in ktgs] +
                         [cx.b_vp[k] for k in ktgs],
                         writes=[bab], tag="F2av")
                    s.op("vector", lambda e, ab=ab: e.reciprocal(
                        rz[:, 0:1], ab[:, 64:65]),
                        reads=[bab], writes=[b_rz], tag="F2rz")
                    s.op("vector", lambda e, ab=ab, qg=qg, h=h, of=cx.of:
                         e.tensor_scalar_mul(
                             of[:, (qg // 128) * 512 + h * 64:
                                (qg // 128) * 512 + (h + 1) * 64],
                             ab[:, 0:64], rz[:, 0:1]),
                         reads=[bab, b_rz], writes=[cx.b_of[qg // 128][h]],
                         tag="F2div")


def emit_ffn_wo(cx, l, xf_in_ap, tb_xf_in, xf_mid_ap, tb_xf_mid):
    s, ar, S = cx.s, cx.ar, cx.S
    NT = S // 128
    NB = S // 512
    for t in range(NT):
        s.dma("scalar", lambda e, t=t, of=cx.of: e.dma_start(
            cx.osp.ap()[_ts(t, 128), 0:512], of[:, _ts(t, 512)]),
            reads=cx.b_of[t], writes=[cx.tb_osp.b(t, h) for h in range(8)],
            tag="F3osp")
    wo, b_wo = load_w_fm(cx, cx.w["L0_ffn_wo"].ap()[l], INNER, D, "F3wo")
    ofm = [ar.alloc(f"F3of{i}", [PB, 4 * 512], BF16) for i in range(2)]
    b_ofm = [Buf(), Buf()]
    xt = ar.alloc("F3xt", [PB, 2 * D], F32)
    b_xt = [Buf(), Buf()]
    x1 = ar.alloc("F3x1", [PB, 2 * D], F32)
    b_x1 = [Buf(), Buf()]
    for b in range(NB):
        d = b % 2
        for c in range(4):
            s.dma("sync", lambda e, b=b, c=c, d=d: e.dma_start(
                ofm[d][:, c * 512:(c + 1) * 512],
                cx.osp.ap()[_ts(b, 512), c * 128:(c + 1) * 128],
                transpose=True),
                reads=[cx.tb_osp.b(b * 4 + tt, h) for tt in range(4)
                       for h in range(8)],
                writes=[b_ofm[d]], tag="F3ofT")
        for tt in range(4):
            t = b * 4 + tt
            pr = t % 2
            cols = slice(pr * D, (pr + 1) * D)
            pw = cx.PW[pr]
            bws = [bbuf(cx, 2 * pr), bbuf(cx, 2 * pr + 1)]
            s.dma("sync", lambda e, t=t, cols=cols: e.dma_start(
                xt[:, cols], xf_in_ap[ZR + t * 128:ZR + (t + 1) * 128, :]),
                reads=sp_rs(tb_xf_in, t), writes=[b_xt[pr]], tag="F3ldx")
            def mm(e, tt=tt, d=d, pw=pw):
                ins = None
                for half in range(2):
                    for c in range(4):
                        ins = e.matmul(
                            pw[:, half * 512:(half + 1) * 512],
                            ofm[d][:, c * 512 + tt * 128:
                                   c * 512 + (tt + 1) * 128],
                            wo[:, c * D + half * 512:
                               c * D + (half + 1) * 512],
                            start=(c == 0), stop=(c == 3))
                return ins
            s.op("tensor", mm, reads=[b_ofm[d], b_wo], writes=bws,
                 tag="F3mm")
            s.op("vector", lambda e, cols=cols, pw=pw: e.tensor_add(
                x1[:, cols], pw[:, 0:1024], xt[:, cols]),
                reads=bws + [b_xt[pr]], writes=[b_x1[pr]], tag="F3res")
            s.dma("scalar", lambda e, t=t, cols=cols: e.dma_start(
                xf_mid_ap[ZR + t * 128:ZR + (t + 1) * 128, :], x1[:, cols]),
                reads=[b_x1[pr]], writes=sp_w(tb_xf_mid, t), tag="F3stx")


def emit_bcast_row(cx, src_row_ap, dst, b_dst, ncols, tag="bc"):
    s = cx.s
    src = bass.AP(src_row_ap.tensor, src_row_ap.offset,
                  [[0, PB]] + [list(x) for x in src_row_ap.ap[1:]])
    s.dma("sync", lambda e: e.dma_start(dst[:, 0:ncols], src),
          writes=[b_dst], tag=tag)


def emit_ffn_mlp(cx, l, xf_mid_ap, tb_xf_mid, tail):
    s, ar, S = cx.s, cx.ar, cx.S
    NT = S // 128
    BLK = min(1024, S)
    NBLK = S // BLK
    NCH = (HID + 127) // 128
    lngb = ar.alloc("lngb", [PB, 3 * D], F32)
    b_lngb = Buf()
    emit_bcast_row(cx, cx.w["L0_ffn_ln_g"].ap()[l:l + 1, :],
                   lngb[:, 0:D], b_lngb, D, "bcg")
    emit_bcast_row(cx, cx.w["L0_ffn_ln_b"].ap()[l:l + 1, :],
                   lngb[:, D:2 * D], b_lngb, D, "bcb")
    emit_bcast_row(cx, cx.w["L0_ffn_b2"].ap()[l:l + 1, :],
                   lngb[:, 2 * D:3 * D], b_lngb, D, "bc2")
    b1sb = ar.alloc("b1sb", [PB, 2 * NCH], F32)
    b_b1 = Buf()
    for part in range(2):
        off = part * HID
        s.dma("sync", lambda e, part=part, off=off: e.dma_start(
            b1sb[:, part * NCH:part * NCH + 21],
            cx.w["L0_ffn_b1"].ap()[l, off:off + 21 * 128]
            .rearrange("(c p) -> p c", p=128),
            allow_slow_non_contiguous=True),
            writes=[b_b1], tag="b1ld")
        s.dma("sync", lambda e, part=part, off=off: e.dma_start(
            b1sb[0:42, part * NCH + 21:part * NCH + 22],
            cx.w["L0_ffn_b1"].ap()[l, off + 21 * 128:off + HID]
            .rearrange("(c p) -> p c", p=42),
            allow_slow_non_contiguous=True),
            writes=[b_b1], tag="b1ldt")
    h2st = ar.alloc("h2st", [PB, 2 * D], F32)
    b_h2 = [Buf(), Buf()]
    h2bf = ar.alloc("h2bf", [PB, 2 * D], BF16)
    b_h2bf = [Buf(), Buf()]
    gl = ar.alloc("gl", [PB, 2 * 512], F32)
    b_gl = [Buf(), Buf()]
    x1 = ar.alloc("Mx1", [PB, 2 * D], F32)
    b_x1 = [Buf(), Buf()]
    xfm = ar.alloc("Mxf", [PB, 2 * D], F32)
    b_xfm = [Buf(), Buf()]
    h4 = ar.alloc("h4", [PB, NCH * BLK], BF16)
    b_h4 = [Buf(f"h4{c}") for c in range(NCH)]
    u0 = (ar.base + 63) // 64 * 64
    w2sb = ar.alloc("w2sb", [PB, NCH * D], BF16)
    h2nfm = ar.overlay("h2nfm", [PB, 8 * BLK], BF16, u0)
    w1a = ar.overlay("w1a", [PB, 2 * 8 * 128], BF16, u0 + 8 * BLK * 2)
    w1g = ar.overlay("w1g", [PB, 2 * 8 * 128], BF16,
                     u0 + 8 * BLK * 2 + 2 * 8 * 128 * 2)
    b_h2nfm = Buf()
    b_w1a = [Buf(), Buf()]
    b_w1g = [Buf(), Buf()]
    b_w2 = Buf()
    for B in range(NBLK):
        for tt in range(BLK // 128):
            t = B * (BLK // 128) + tt
            pr = tt % 2
            cols = slice(pr * D, (pr + 1) * D)
            s.dma("sync", lambda e, t=t, cols=cols: e.dma_start(
                h2st[:, cols][:, 0:512],
                xf_mid_ap[ZR + t * 128:ZR + (t + 1) * 128, 0:512]),
                reads=sp_rs(tb_xf_mid, t), writes=[b_h2[pr]], tag="Mh2a")
            s.dma("sync", lambda e, t=t, cols=cols: e.dma_start(
                h2st[:, cols][:, 512:1024],
                xf_mid_ap[ZR - 1 + t * 128:ZR - 1 + (t + 1) * 128, 512:1024]),
                reads=sp_rs(tb_xf_mid, t, 1), writes=[b_h2[pr]], tag="Mh2b")
            emit_ln(cx, h2st, b_h2[pr], cols, h2bf[:, cols], b_h2bf[pr],
                    gamma=lngb[:, 0:D], beta=lngb[:, D:2 * D],
                    b_gb=[b_lngb])
            s.dma("scalar", lambda e, t=t, cols=cols: e.dma_start(
                cx.osp.ap()[_ts(t, 128), :], h2bf[:, cols]),
                reads=[b_h2bf[pr]],
                writes=[cx.tb_osp.b(t, h) for h in range(8)], tag="Mh2sp")
        for c in range(8):
            s.dma("sync", lambda e, B=B, c=c: e.dma_start(
                h2nfm[:, c * BLK:(c + 1) * BLK],
                cx.osp.ap()[B * BLK:(B + 1) * BLK, c * 128:(c + 1) * 128],
                transpose=True),
                reads=[cx.tb_osp.b(B * (BLK // 128) + tt, h)
                       for tt in range(BLK // 128) for h in range(8)],
                writes=[b_h2nfm, b_w2], tag="Mh2T")
        for hc in range(NCH):
            hsz = 128 if hc < NCH - 1 else HID - 128 * (NCH - 1)
            d = hc % 2
            for part, wt, bw in (("a", w1a, b_w1a), ("g", w1g, b_w1g)):
                off = (0 if part == "a" else HID) + hc * 128
                s.dma("gpsimd", lambda e, wt=wt, d=d, off=off, hsz=hsz, l=l:
                      e.dma_start(
                          wt[:, d * 8 * 128:d * 8 * 128 + 8 * hsz]
                          .rearrange("p (c n) -> p c n", c=8),
                          cx.w["L0_ffn_w1"].ap()[l, :, off:off + hsz]
                          .rearrange("(c p) n -> p c n", p=128)),
                      writes=[bw[d], b_w2], tag="Mw1ld")
            for ts in range(BLK // 512):
                pa, pg = bank(cx, 2 * (ts % 2)), bank(cx, 2 * (ts % 2) + 1)
                ba, bg = bbuf(cx, 2 * (ts % 2)), bbuf(cx, 2 * (ts % 2) + 1)
                def mm(e, hsz=hsz, d=d, ts=ts, pa=pa, pg=pg):
                    ins = None
                    for c in range(8):
                        ins = e.matmul(
                            pa[0:hsz, 0:512],
                            w1a[:, d * 8 * 128 + c * hsz:
                                d * 8 * 128 + (c + 1) * hsz],
                            h2nfm[:, c * BLK + ts * 512:
                                  c * BLK + (ts + 1) * 512],
                            start=(c == 0), stop=(c == 7))
                        ins = e.matmul(
                            pg[0:hsz, 0:512],
                            w1g[:, d * 8 * 128 + c * hsz:
                                d * 8 * 128 + (c + 1) * hsz],
                            h2nfm[:, c * BLK + ts * 512:
                                  c * BLK + (ts + 1) * 512],
                            start=(c == 0), stop=(c == 7))
                    return ins
                s.op("tensor", mm, reads=[b_w1a[d], b_w1g[d], b_h2nfm],
                     writes=[ba, bg], tag="Mw1mm")
                gd = ts % 2
                s.op("scalar", lambda e, hsz=hsz, hc=hc, pg=pg, gd=gd:
                     e.activation(gl[0:hsz, gd * 512:(gd + 1) * 512],
                                  pg[0:hsz, 0:512], AF.Gelu,
                                  bias=b1sb[0:hsz, NCH + hc:NCH + hc + 1]),
                     reads=[bg, b_b1], writes=[b_gl[gd]], tag="Mgelu")
                s.op("vector", lambda e, hsz=hsz, hc=hc, ts=ts, pa=pa, gd=gd:
                     e.scalar_tensor_tensor(
                         h4[0:hsz, hc * BLK + ts * 512:
                            hc * BLK + (ts + 1) * 512],
                         pa[0:hsz, 0:512], b1sb[0:hsz, hc:hc + 1],
                         gl[0:hsz, gd * 512:(gd + 1) * 512],
                         op0=ALU.add, op1=ALU.mult),
                     reads=[ba, b_gl[gd], b_b1], writes=[b_h4[hc]],
                     tag="Mh4")
        s.dma("gpsimd", lambda e, l=l: e.dma_start(
            w2sb[:, 0:21 * D].rearrange("p (c n) -> p c n", c=21),
            cx.w["L0_ffn_w2"].ap()[l, 0:21 * 128, :]
            .rearrange("(c p) n -> p c n", p=128)),
            writes=[b_w2, b_h2nfm, b_w1a[0], b_w1a[1], b_w1g[0], b_w1g[1]],
            tag="Mw2ld")
        s.dma("gpsimd", lambda e, l=l: e.dma_start(
            w2sb[0:42, 21 * D:22 * D],
            cx.w["L0_ffn_w2"].ap()[l, 21 * 128:HID, :]),
            writes=[b_w2], tag="Mw2ldt")
        for tt in range(BLK // 128):
            t = B * (BLK // 128) + tt
            pr = tt % 2
            cols = slice(pr * D, (pr + 1) * D)
            pw = cx.PW[2 + pr]
            bws = [bbuf(cx, 4 + 2 * pr), bbuf(cx, 5 + 2 * pr)]
            def mm(e, tt=tt, pw=pw):
                ins = None
                for half in range(2):
                    for hc in range(NCH):
                        hsz = 128 if hc < NCH - 1 else HID - 128 * (NCH - 1)
                        ins = e.matmul(
                            pw[:, half * 512:(half + 1) * 512],
                            h4[0:hsz, hc * BLK + tt * 128:
                               hc * BLK + (tt + 1) * 128],
                            w2sb[0:hsz, hc * D + half * 512:
                                 hc * D + (half + 1) * 512],
                            start=(hc == 0), stop=(hc == NCH - 1))
                return ins
            s.op("tensor", mm, reads=b_h4 + [b_w2], writes=bws, tag="Mw2mm")
            s.dma("sync", lambda e, t=t, cols=cols: e.dma_start(
                xfm[:, cols], xf_mid_ap[ZR + t * 128:ZR + (t + 1) * 128, :]),
                reads=sp_rs(tb_xf_mid, t), writes=[b_xfm[pr]], tag="Mldxf")
            s.op("vector", lambda e, cols=cols, pw=pw: e.tensor_add(
                x1[:, cols], pw[:, 0:1024], lngb[:, 2 * D:3 * D]),
                reads=bws + [b_lngb], writes=[b_x1[pr]], tag="Mb2")
            s.op("vector", lambda e, cols=cols: e.tensor_add(
                x1[:, cols], x1[:, cols], xfm[:, cols]),
                reads=[b_x1[pr], b_xfm[pr]], writes=[b_x1[pr]], tag="Mres")
            tail(t, x1, cols, b_x1[pr])


def emit_ffn_layer(cx, l, xbf_in_ap, tb_xbf_in, xf_in_ap, tb_xf_in,
                   xf_mid_ap, tb_xf_mid, tail):
    ar = cx.ar
    S = cx.S
    NT = S // 128
    ar.mark()
    cx.vp = ar.alloc("vp", [PB, NT * 520], BF16)
    cx.b_vp = [Buf(f"vp{t}") for t in range(NT)]
    cx.of = ar.alloc("of", [PB, NT * 512], BF16)
    cx.b_of = [[Buf(f"of{t},{h}") for h in range(8)] for t in range(NT)]
    ar.mark()
    emit_ffn_qkv(cx, l, xbf_in_ap, tb_xbf_in)
    ar.release()
    ar.mark()
    emit_ffn_attn(cx, l)
    ar.release()
    ar.mark()
    emit_ffn_wo(cx, l, xf_in_ap, tb_xf_in, xf_mid_ap, tb_xf_mid)
    ar.release()
    ar.release()
    ar.mark()
    emit_ffn_mlp(cx, l, xf_mid_ap, tb_xf_mid, tail)
    ar.release()


def make_tail_stream(cx, xf_out_ap, tb_xf_out, xbf_out_ap, tb_xbf_out):
    s = cx.s
    ob = cx.t_ob
    b_ob = cx.bt_ob

    def tail(t, x1, cols, b_x1):
        pr = 0 if cols.start == 0 else 1
        s.dma("scalar", lambda e, t=t, cols=cols: e.dma_start(
            xf_out_ap[ZR + t * 128:ZR + (t + 1) * 128, :], x1[:, cols]),
            reads=[b_x1], writes=sp_w(tb_xf_out, t), tag="Tstf")
        s.op("scalar", lambda e, cols=cols, pr=pr: e.activation(
            ob[:, pr * D:(pr + 1) * D], x1[:, cols], AF.Copy),
            reads=[b_x1], writes=[b_ob[pr]], tag="Tcast")
        s.dma("scalar", lambda e, t=t, pr=pr: e.dma_start(
            xbf_out_ap[ZR + t * 128:ZR + (t + 1) * 128, :],
            ob[:, pr * D:(pr + 1) * D]),
            reads=[b_ob[pr]], writes=sp_w(tb_xbf_out, t), tag="Tstb")
    return tail


def make_tail_sublayer(cx, x_res_ap, tb_x_res, x_out_ap, tb_x_out,
                       xbf_out_ap, tb_xbf_out, xf_out_ap=None,
                       tb_xf_out=None):
    s = cx.s
    xr, b_xr = cx.t_xr, cx.bt_xr
    xo, b_xo = cx.t_xo, cx.bt_xo
    ob, b_ob = cx.t_ob, cx.bt_ob
    of, b_of = cx.t_of, cx.bt_of

    def tail(t, x1, cols, b_x1):
        pr = 0 if cols.start == 0 else 1
        s.dma("sync", lambda e, t=t, pr=pr: e.dma_start(
            xr[:, pr * D:(pr + 1) * D], x_res_ap[_ts(t, 128), :]),
            reads=tb_x_res.r(t * 128, (t + 1) * 128) if tb_x_res else [],
            writes=[b_xr[pr]], tag="Tldr")
        s.op("vector", lambda e, cols=cols, pr=pr: e.tensor_add(
            xo[:, pr * D:(pr + 1) * D], x1[:, cols],
            xr[:, pr * D:(pr + 1) * D]),
            reads=[b_x1, b_xr[pr]], writes=[b_xo[pr]], tag="Tadd")
        s.dma("scalar", lambda e, t=t, pr=pr: e.dma_start(
            x_out_ap[_ts(t, 128), :], xo[:, pr * D:(pr + 1) * D]),
            reads=[b_xo[pr]], writes=tb_x_out.w(t * 128, (t + 1) * 128),
            tag="Tstx")
        if xbf_out_ap is None:
            return
        emit_ln(cx, xo, b_xo[pr], slice(pr * D, (pr + 1) * D),
                ob[:, pr * D:(pr + 1) * D], b_ob[pr],
                (of[:, pr * D:(pr + 1) * D] if xf_out_ap is not None
                 else None),
                (b_of[pr] if xf_out_ap is not None else None))
        s.dma("scalar", lambda e, t=t, pr=pr: e.dma_start(
            xbf_out_ap[ZR + t * 128:ZR + (t + 1) * 128, :],
            ob[:, pr * D:(pr + 1) * D]),
            reads=[b_ob[pr]], writes=sp_w(tb_xbf_out, t), tag="Tstb2")
        if xf_out_ap is not None:
            s.dma("scalar", lambda e, t=t, pr=pr: e.dma_start(
                xf_out_ap[ZR + t * 128:ZR + (t + 1) * 128, :],
                of[:, pr * D:(pr + 1) * D]),
                reads=[b_of[pr]], writes=sp_w(tb_xf_out, t), tag="Tstf2")
    return tail



def emit_attn_b_proj(cx):
    s, ar, S = cx.s, cx.ar, cx.S
    NTW = S // 512
    ar.mark()
    xn, b_xn = load_fm_from_spill(cx, cx.xbf_b.ap(), S, 0, D, "xnB",
                                  reads=cx.tb_xbf_b.r())
    stg = ar.alloc("stgB", [PB, 4 * 512], BF16)
    b_stg = [Buf() for _ in range(4)]
    for slot, wname, scale in ((0, "L0_attn_b_Wq", 0.125),
                               (2, "L0_attn_b_Wk", 1.0)):
        ar.mark()
        wt, b_wt = load_w_fm(cx, cx.w[wname].ap(), D, D, "wB" + wname[-2:])
        acc, b_acc = (cx.cs_q, cx.b_cs_q) if slot == 0 else \
            (cx.cs_k, cx.b_cs_k)
        for n in range(8):
            for t in range(NTW):
                pk = bank(cx, (n + t) % 4)
                bpk = bbuf(cx, (n + t) % 4)
                def mm(e, n=n, t=t, pk=pk, wt=wt):
                    ins = None
                    for c in range(8):
                        ins = e.matmul(
                            pk, wt[:, c * D + n * 128:c * D + (n + 1) * 128],
                            xn[:, c * S + t * 512:c * S + (t + 1) * 512],
                            start=(c == 0), stop=(c == 7))
                    return ins
                s.op("tensor", mm, reads=[b_wt] + b_xn, writes=[bpk],
                     tag="Bmm")
                i = t % 4
                s.op("scalar", lambda e, pk=pk, i=i, scale=scale:
                     e.activation(stg[:, i * 512:(i + 1) * 512], pk, AF.Copy,
                                  scale=scale),
                     reads=[bpk], writes=[b_stg[i]], tag="Bev")
                s.dma("scalar", lambda e, n=n, t=t, i=i, slot=slot:
                      e.dma_start(cx.qsp.ap()[slot, _ts(n, 128), _ts(t, 512)],
                                  stg[:, i * 512:(i + 1) * 512]),
                      reads=[b_stg[i]], writes=[cx.tb_qsp[slot].b(n, t)],
                      tag="Bst")
                emit_cs_stat(cx, pk, bpk, acc, b_acc, n == 0 and t == 0)
        ar.release()
    ar.mark()
    wv, b_wv = load_w_fm(cx, cx.w["L0_attn_b_Wv"].ap(), D, D, "wBv")
    vst = ar.alloc("vstB", [PB, 2 * 1040], BF16)
    b_vst = [Buf(), Buf()]
    s.op("vector", lambda e: e.memset(vst[:, :], 1.0),
         writes=[b_vst[0], b_vst[1]])
    for t in range(S // 128):
        pw = cx.PW[2 + (t % 2)]
        bws = [bbuf(cx, 4 + 2 * (t % 2)), bbuf(cx, 5 + 2 * (t % 2))]
        def mm(e, t=t, pw=pw):
            ins = None
            for half in range(2):
                for c in range(8):
                    ins = e.matmul(
                        pw[:, half * 512:(half + 1) * 512],
                        xn[:, c * S + t * 128:c * S + (t + 1) * 128],
                        wv[:, c * D + half * 512:c * D + (half + 1) * 512],
                        start=(c == 0), stop=(c == 7))
            return ins
        s.op("tensor", mm, reads=[b_wv] + b_xn, writes=bws, tag="BvRmm")
        p = t % 2
        s.op("scalar", lambda e, pw=pw, p=p: e.activation(
            vst[:, p * 1040:(p + 1) * 1040].rearrange(
                "q (h c) -> q h c", h=16)[:, :, 0:64],
            pw[:, 0:1024].rearrange("q (h d) -> q h d", h=16),
            AF.Copy),
            reads=bws, writes=[b_vst[p]], tag="Bvev")
        s.dma("scalar", lambda e, t=t, p=p: e.dma_start(
            cx.vsp.ap().rearrange("g r c -> r g c")[_ts(t, 128)],
            vst[:, p * 1040:(p + 1) * 1040].rearrange("q (g c) -> q g c",
                                                      g=8)),
            reads=[b_vst[p]], writes=cx.tb_vsp.row(t), tag="Bvst")
    emit_negb(cx, cx.cs_q, cx.b_cs_q, cx.cs_k, cx.b_cs_k, 0.1250001,
              cx.negb, cx.b_negb)
    ar.release()
    ar.release()


def emit_attn_b_attn(cx):
    s, ar, S = cx.s, cx.ar, cx.S
    NKT = S // 128
    NQC = S // 512
    ar.mark()
    qq = [ar.alloc(f"B3q{i}", [PB, S], BF16) for i in range(2)]
    kk = ar.alloc("B3k", [PB, S], BF16)
    b_qq = [Buf(), Buf()]
    b_kk = Buf()
    vv = [ar.alloc(f"B3v{i}", [PB, NKT * 130], BF16) for i in range(2)]
    b_vv = [Buf(), Buf()]
    eT = ar.alloc("B3e", [PB, NKT * 1024], BF16)
    b_eT = [Buf(f"B3e{k}") for k in range(NKT)]
    ofst = [ar.alloc(f"B3o{i}", [PB, 512], BF16) for i in range(2)]
    b_ofst = [Buf(), Buf()]
    rzrow = [ar.alloc(f"B3rz{i}", [PB, 512], F32) for i in range(2)]
    b_rzrow = [Buf() for _ in range(2)]
    rzbc = [ar.alloc(f"B3rb{i}", [64, 512], F32) for i in range(2)]
    b_rzbc = [Buf(), Buf()]

    def emit_kt(qc, kt, d, pend, npend):
        par = kt % 3
        pw = cx.PW[(0, 1, 3)[par]]
        bk0 = (0, 2, 6)[par]
        pwb = [bbuf(cx, bk0), bbuf(cx, bk0 + 1)]
        r = kt - 4 * qc
        if r < 0:
            def mmp(e, kt=kt, qc=qc, d=d, pw=pw):
                ins = None
                for h in range(2):
                    hs = slice(h * 64, h * 64 + 64)
                    ins = e.matmul(pw[:, h * 512:(h + 1) * 512],
                                   kk[hs, _ts(kt, 128)],
                                   qq[d][hs, _ts(qc, 512)],
                                   start=True, stop=True)
                return ins
            s.op("tensor", mmp, reads=[b_kk, b_qq[d]], writes=pwb, tag="B3mm")
        else:
            def mmd(e, kt=kt, qc=qc, d=d, pw=pw, r=r):
                ins = None
                for qs in range(r, 4):
                    for h in range(2):
                        hs = slice(h * 64, h * 64 + 64)
                        ins = e.matmul(
                            pw[:, h * 512 + qs * 128:h * 512 + (qs + 1) * 128],
                            kk[hs, _ts(kt, 128)],
                            qq[d][hs, qc * 512 + qs * 128:
                                  qc * 512 + (qs + 1) * 128],
                            start=True, stop=True)
                return ins
            s.op("tensor", mmd, reads=[b_kk, b_qq[d]], writes=pwb, tag="B3md")
            for h in range(2):
                s.op("vector", lambda e, h=h, r=r, pw=pw: e.tensor_add(
                    pw[:, h * 512 + r * 128:h * 512 + (r + 1) * 128],
                    pw[:, h * 512 + r * 128:h * 512 + (r + 1) * 128],
                    cx.c32[:, C_MTFUT:C_MTFUT + 128]),
                    reads=pwb + [cx.b_c32], writes=[pwb[h]], tag="B3msk")
        if pend is not None and kt < npend:
            emit_av(pend, kt, d)
        if r < 0:
            s.op("scalar", lambda e, kt=kt, pw=pw: e.activation(
                eT[:, _ts(kt, 1024)], pw[:, 0:1024], AF.Exp,
                bias=cx.negb[:, 0:1]),
                reads=pwb + [cx.b_negb], writes=[b_eT[kt]], tag="B3exp")
        else:
            for h in range(2):
                s.op("scalar", lambda e, h=h, kt=kt, r=r, pw=pw: e.activation(
                    eT[:, kt * 1024 + h * 512 + r * 128:
                       kt * 1024 + (h + 1) * 512],
                    pw[:, h * 512 + r * 128:h * 512 + 512],
                    AF.Exp, bias=cx.negb[:, 0:1]),
                    reads=pwb + [cx.b_negb], writes=[b_eT[kt]], tag="B3exd")

    def emit_av(qcp, kt, d):
        n = 4 * qcp + 4
        r = kt - 4 * qcp
        cc = max(r, 0) * 128
        for h in range(2):
            ab = cx.PW[2][:, h * 512:(h + 1) * 512]
            s.op("tensor", lambda e, h=h, d=d, kt=kt, ab=ab, cc=cc, n=n:
                 e.matmul(
                     ab[0:65, cc:512],
                     vv[d][:, kt * 130 + h * 65:kt * 130 + (h + 1) * 65],
                     eT[:, kt * 1024 + h * 512 + cc:
                        kt * 1024 + (h + 1) * 512],
                     start=(kt == 0), stop=(kt == n - 1)),
                 reads=[b_eT[kt], b_vv[d]], writes=[bbuf(cx, 4 + h)],
                 tag="B3av")

    def emit_div(p, qc):
        for h in range(2):
            ab = cx.PW[2][:, h * 512:(h + 1) * 512]
            dpar = (qc * 2 + h) % 4
            emit_div_evac(cx, p, qc, h, ab, bbuf(cx, 4 + h), ofst[qc % 2],
                          b_ofst[qc % 2], rzrow[dpar % 2], b_rzrow[dpar % 2],
                          rzbc[dpar % 2], b_rzbc[dpar % 2], "B3")
        s.dma("scalar", lambda e, p=p, qc=qc: e.dma_start(
            cx.ospf.ap()[_ts(p, 128), _ts(qc, 512)], ofst[qc % 2][:, :]),
            reads=[b_ofst[qc % 2]], writes=[cx.tb_ospf.b(p, qc)], tag="B3ost")

    for p in range(8):
        d = p % 2
        s.dma("sync", lambda e, p=p, d=d: e.dma_start(
            qq[d][:, :], cx.qsp.ap()[0, _ts(p, 128), :]),
            reads=[cx.tb_qsp[0].row(p)], writes=[b_qq[d]], tag="B3ldq")
        s.dma("sync", lambda e, p=p: e.dma_start(
            kk[:, :], cx.qsp.ap()[2, _ts(p, 128), :]),
            reads=[cx.tb_qsp[2].row(p)], writes=[b_kk], tag="B3ldk")
        s.dma("gpsimd", lambda e, p=p, d=d: e.dma_start(
            vv[d][:, :].rearrange("q (t c) -> q t c", c=130),
            cx.vsp.ap()[p].rearrange("(t q) c -> q t c", q=128)),
            reads=[b.b(t, p) for t in range(NKT) for b in [cx.tb_vsp]],
            writes=[b_vv[d]], tag="B3ldv")
        pend = None
        for qc in range(NQC):
            npend = 4 * pend + 4 if pend is not None else 0
            for kt in range(4 * qc + 4):
                emit_kt(qc, kt, d, pend, npend)
            if pend is not None:
                emit_div(p, pend)
            pend = qc
        for kt in range(4 * pend + 4):
            emit_av(pend, kt, d)
        emit_div(p, pend)
    ar.release()


_PROGRAM_CACHE = {}


def kernel(**inputs):
    S = 4096
    B = int(inputs["L0_attn_a_x"].shape[0])
    if S not in _PROGRAM_CACHE:
        _PROGRAM_CACHE[S] = (build_program(S), build_consts(S))
    cx, (cbf, c32, rot) = _PROGRAM_CACHE[S]
    x = np.ascontiguousarray(np.asarray(inputs["L0_attn_a_x"], np.float32))
    base = {n: np.ascontiguousarray(np.asarray(inputs[n], np.float32))
            for n in W_NAMES if n != "L0_attn_a_x"}
    base["CBF"] = cbf
    base["C32"] = c32
    base["ROT"] = rot
    in_maps = []
    for b in range(B):
        m = dict(base)
        m["L0_attn_a_x"] = x[b]
        in_maps.append(m)
    res = run_bass_kernel_spmd(cx.nc, in_maps, core_ids=list(range(B)))
    return np.stack([np.asarray(r["out"], np.float32)
                     for r in res.results], axis=0)
